# version 15
# baseline (speedup 1.0000x reference)
import math
import numpy as np
import concourse.bass as bass
import concourse.mybir as mybir
from concourse.bass import IndirectOffsetOnAxis
from concourse.bass_utils import run_bass_kernel_spmd

F32 = mybir.dt.float32
BF16 = mybir.dt.bfloat16
U32 = mybir.dt.uint32
AF = mybir.ActivationFunctionType
ALU = mybir.AluOpType
AX = mybir.AxisListType

NCORE = 8
SEQ = 2
S = 2048
HD = 2048
NH = 16
DN = 128
DR = 64
DV = 128
QL = 1536
KL = 512
EXP = 32
INTER = 768
CAPL = 1280
NCH = CAPL // 128
NSLOT = EXP * CAPL
TLOC = SEQ * S
NTT = TLOC // 128
NQ = NH * (DN + DR)
NKV = NH * (DN + DV)
FU = QL + KL + DR
SCALE = 1.0 / math.sqrt(DN + DR)
NEG = -3.0e38

BUILD = ("A1", "A2", "A3B", "B2", "C", "D")
TAPS = ()
DEV_SEQ = None
DEV_T = None
DEV_QB = None
DEV_G = None
DEV_E = None
DEV_D = None


def host_consts():
    c = {}
    c["ident"] = np.eye(128, dtype=np.float32)
    c["lt"] = np.triu(np.ones((128, 128), np.float32), 1)
    c["negtri"] = np.triu(np.full((128, 128), NEG, np.float32), 1)
    c["iota32"] = np.broadcast_to(np.arange(32, dtype=np.float32), (128, 32)).copy()
    c["iota_f"] = np.arange(128, dtype=np.float32).reshape(128, 1)
    c["ones_col"] = np.ones((128, 1), np.float32)
    c["ones_row"] = np.ones((1, 128), np.float32)
    inv_freq = 1.0 / (10000.0 ** (np.arange(0, DR, 2, dtype=np.float64) / DR))
    ang = np.arange(S, dtype=np.float64)[:, None] * inv_freq[None, :]
    c["cos_t"] = np.concatenate([np.cos(ang)] * 2, axis=1).astype(np.float32)
    c["sin_t"] = np.concatenate([np.sin(ang)] * 2, axis=1).astype(np.float32)
    c["cos16"] = np.ascontiguousarray(np.broadcast_to(c["cos_t"][:, None, :], (S, NH, DR)))
    c["sin16"] = np.ascontiguousarray(np.broadcast_to(c["sin_t"][:, None, :], (S, NH, DR)))
    return c


class Sem:
    __slots__ = ("h", "n")

    def __init__(self, h):
        self.h = h
        self.n = 0


class _Scope:
    def __init__(self, b):
        self.b = b
        self.handles = []

    _uid = [0]

    def alloc(self, name, shape, dtype):
        _Scope._uid[0] += 1
        g = self.b.nc.sbuf_tensor(f"{name}_{_Scope._uid[0]}", shape, dtype)
        h = g.__enter__()
        self.handles.append(g)
        return h

    def close(self):
        for g in reversed(self.handles):
            g.__exit__(None, None, None)
        self.handles = []


class Builder:
    ENGS = ("sync", "gpsimd", "scalar", "vector", "tensor")

    def __init__(self):
        nc = bass.Bass("TRN2", target_bir_lowering=False)
        self.nc = nc
        self.st = {k: [] for k in self.ENGS}
        self._semcache = {}

        self.c_pe = self.sem("c_pe")
        self.c_gp = self.sem("c_gp")
        self.c_dv = self.sem("c_dv")
        self.c_ac = self.sem("c_ac")

        self.psA = [nc.alloc_psum_tensor(f"psA{i}", [128, 512], F32) for i in range(2)]
        self.psB = [nc.alloc_psum_tensor(f"psB{i}", [128, 512], F32) for i in range(2)]
        self.psT = [nc.alloc_psum_tensor(f"psT{i}", [128, 1024], BF16) for i in range(2)]
        self.psR = nc.alloc_psum_tensor("psR", [128, 512], F32)
        self.psF = nc.alloc_psum_tensor("psF", [128, 512], F32)
        self.bankfree = {}
        self.ring = {}

    def sem(self, name):
        if name not in self._semcache:
            self._semcache[name] = Sem(self.nc.alloc_semaphore(name=name))
        return self._semcache[name]

    def all_dma_sems(self):
        return [s for nm, s in self._semcache.items() if nm.startswith("d_")]

    def wait(self, eng, sem, val):
        if sem is None or val is None or val <= 0:
            return
        self.st[eng].append(lambda E, h=sem.h, v=val: E.wait_ge(h, v))

    def waitt(self, eng, tick):
        if tick is not None:
            self.wait(eng, tick[0], tick[1])

    def waitr(self, eng, key):
        self.waitt(eng, self.ring.get(key))

    def setr(self, key, tick):
        self.ring[key] = tick

    def dma(self, eng, sem, out, in_, **kw):
        sem.n += 16
        v = sem.n
        self.st[eng].append(
            lambda E, o=out, i=in_, h=sem.h, k=dict(kw): E.dma_start(out=o, in_=i, **k).then_inc(h, 16))
        return (sem, v)

    @staticmethod
    def _breg(E, bound):
        cache = getattr(E, "_bnd_regs", None)
        if cache is None:
            cache = {}
            E._bnd_regs = cache
        if bound not in cache:
            cache[bound] = E.to_reg(bound)
        return cache[bound]

    def gather(self, sem, out, src, idx, bound, accum=False):
        sem.n += 16
        v = sem.n
        if accum:
            kw = dict(compute_op=ALU.add, oob_is_err=True)
        else:
            kw = dict(oob_is_err=False)
        self.st["gpsimd"].append(
            lambda E, o=out, s2=src, i2=idx, h=sem.h, bb=bound, k=dict(kw):
            E.indirect_dma_start(out=o, out_offset=None, in_=s2,
                                 in_offset=IndirectOffsetOnAxis(i2, 0),
                                 bounds_check=Builder._breg(E, bb), **k).then_inc(h, 16))
        return (sem, v)

    def scatter(self, sem, dst, src, idx, bound):
        sem.n += 16
        v = sem.n
        self.st["gpsimd"].append(
            lambda E, d=dst, s2=src, i2=idx, h=sem.h, bb=bound:
            E.indirect_dma_start(out=d, out_offset=IndirectOffsetOnAxis(i2, 0),
                                 in_=s2, in_offset=None,
                                 bounds_check=Builder._breg(E, bb), oob_is_err=False).then_inc(h, 16))
        return (sem, v)

    def op(self, eng, fn, tick=False):
        if eng in ("vector", "scalar", "gpsimd"):
            sem = {"vector": self.c_dv, "scalar": self.c_ac, "gpsimd": self.c_gp}[eng]
            self.wait(eng, sem, sem.n)
            sem.n += 1
            v = sem.n
            self.st[eng].append(lambda E, f=fn, h=sem.h: f(E).then_inc(h, 1))
            return (sem, v)
        if not tick:
            self.st[eng].append(fn)
            return None
        sem = self.c_pe
        sem.n += 1
        v = sem.n
        self.st[eng].append(lambda E, f=fn, h=sem.h: f(E).then_inc(h, 1))
        return (sem, v)

    def pet(self):
        return (self.c_pe, self.c_pe.n)

    def dvt(self):
        return (self.c_dv, self.c_dv.n)

    def act(self):
        return (self.c_ac, self.c_ac.n)

    def alloc(self, name, shape, dtype):
        return self.nc.alloc_sbuf_tensor(name, shape, dtype)

    def scope(self):
        return _Scope(self)

    def bank_wait(self, bank):
        self.waitt("tensor", self.bankfree.get(bank.name))

    def bank_set(self, bank, tick):
        self.bankfree[bank.name] = tick

    def fence(self, exclude=()):
        marks = [(self.c_pe, self.c_pe.n), (self.c_dv, self.c_dv.n), (self.c_ac, self.c_ac.n),
                 (self.c_gp, self.c_gp.n)]
        ex = {id(s) for s in exclude}
        marks += [(s, s.n) for s in self.all_dma_sems() if s.n > 0 and id(s) not in ex]
        for eng in self.ENGS:
            for s, v in marks:
                self.wait(eng, s, v)

    def transposes(self, items, bank, ident, dst, pre_evac_wait=None, eng="vector"):
        self.bank_wait(bank)
        n = len(items)
        for i, (src, np_, off) in enumerate(items):
            self.op("tensor",
                    lambda E, s=src, o=off, pp=bank, idn=ident, q=np_, st=(i == 0), sp=(i == n - 1):
                    E.matmul(pp[0:q, o:o + 128], s, idn, start=st, stop=sp, is_transpose=True),
                    tick=(i == n - 1))
        tpe = self.pet()
        self.waitt(eng, tpe)
        if pre_evac_wait is not None:
            self.waitt(eng, pre_evac_wait)
        q = items[0][1]
        w = items[-1][2] + 128
        if eng == "vector":
            tev = self.op(eng,
                          lambda E, d=dst, pp=bank, q2=q, w2=w:
                          E.tensor_copy(d, pp[0:q2, 0:w2].rearrange("p (a x) -> p a x", x=128)),
                          tick=True)
        else:
            tev = self.op(eng,
                          lambda E, d=dst, pp=bank, q2=q, w2=w:
                          E.activation(out=d, in_=pp[0:q2, 0:w2].rearrange("p (a x) -> p a x", x=128),
                                       func=AF.Copy),
                          tick=True)
        self.bank_set(bank, tev)
        return tev

    def mm_chain(self, bank, cols, pieces, out_off=0):
        n = len(pieces)
        for i, (lh, rh) in enumerate(pieces):
            self.op("tensor",
                    lambda E, b2=bank, l=lh, r=rh, s=(i == 0), t=(i == n - 1), c=cols, oo=out_off:
                    E.matmul(b2[:, oo:oo + c], l, r, start=s, stop=t),
                    tick=(i == n - 1))
        return self.pet()


def build_program():
    b = Builder()
    nc = b.nc
    p = b

    d_cst = b.sem("d_cst")
    d_cstg = b.sem("d_cstg")
    d_cs2 = b.sem("d_cs2")
    d_al = [b.sem(f"d_al{i}") for i in range(8)]
    d_w = b.sem("d_w")
    d_st = b.sem("d_st")
    d_z = b.sem("d_z")
    d_sA = [b.sem(f"d_sA{i}") for i in range(2)]
    d_sB = [b.sem(f"d_sB{i}") for i in range(2)]
    d_sQ = [b.sem(f"d_sQ{i}") for i in range(2)]
    d_sV = [b.sem(f"d_sV{i}") for i in range(2)]
    d_c16 = [b.sem(f"d_c16{i}") for i in range(2)]
    d_sAt = b.sem("d_sAt")
    d_sx1 = [b.sem(f"d_sx1{i}") for i in range(2)]
    d_x = [b.sem(f"d_x{i}") for i in range(2)]
    d_q = [b.sem(f"d_q{i}") for i in range(2)]
    d_v = [b.sem(f"d_v{i}") for i in range(2)]
    d_sc = [b.sem(f"d_sc{i}") for i in range(2)]
    d_g = [b.sem(f"d_g{i}") for i in range(4)]
    d_y = [b.sem(f"d_y{i}") for i in range(2)]
    d_ga = [b.sem(f"d_ga{i}") for i in range(8)]
    d_ob = [b.sem(f"d_ob{i}") for i in range(2)]
    d_wd = b.sem("d_wd")
    d_wc = [b.sem(f"d_wc{i}") for i in range(2)]

    xin = nc.dram_tensor("xin", [SEQ, S, HD], F32, kind="ExternalInput").ap()
    w_kv_a = nc.dram_tensor("w_kv_a", [HD, FU], F32, kind="ExternalInput").ap()
    w_qb = nc.dram_tensor("w_qb", [QL, NQ], F32, kind="ExternalInput").ap()
    w_kvb = nc.dram_tensor("w_kvb", [KL, NKV], F32, kind="ExternalInput").ap()
    w_o = nc.dram_tensor("w_o", [NH * DV, HD], F32, kind="ExternalInput").ap()
    gate_w = nc.dram_tensor("gate_w", [HD, EXP], F32, kind="ExternalInput").ap()
    w_gate = nc.dram_tensor("w_gate", [EXP, HD, INTER], F32, kind="ExternalInput").ap()
    w_up = nc.dram_tensor("w_up", [EXP, HD, INTER], F32, kind="ExternalInput").ap()
    w_down = nc.dram_tensor("w_down", [EXP, INTER, HD], F32, kind="ExternalInput").ap()
    gamma_qb_i = nc.dram_tensor("gamma_qb", [128, QL], F32, kind="ExternalInput").ap()
    gamma_kvb_i = nc.dram_tensor("gamma_kvb", [128, KL], F32, kind="ExternalInput").ap()
    HC = host_consts()
    cs = {k: nc.dram_tensor("c_" + k, list(v.shape), F32, kind="ExternalInput").ap()
          for k, v in HC.items()}

    OUT = nc.dram_tensor("out", [TLOC, HD], F32, kind="ExternalOutput").ap()

    qcn_hbm = nc.dram_tensor("qcn_hbm", [NTT, 128, QL], BF16, kind="Internal").ap()
    kvn_hbm = nc.dram_tensor("kvn_hbm", [NTT, 128, KL], BF16, kind="Internal").ap()
    kpe_hbm = nc.dram_tensor("kpe_hbm", [NTT, 128, DR], BF16, kind="Internal").ap()
    Q_hbm = nc.dram_tensor("q_hbm", [SEQ, S, NQ], BF16, kind="Internal").ap()
    V_hbm = nc.dram_tensor("v_hbm", [SEQ, S, NH * DV], BF16, kind="Internal").ap()
    attn_hbm = nc.dram_tensor("attn_hbm", [SEQ, S, HD], BF16, kind="Internal").ap()
    x1f_hbm = nc.dram_tensor("x1f_hbm", [NTT, 128, HD], F32, kind="Internal").ap()
    x1b_hbm = nc.dram_tensor("x1b_hbm", [TLOC, HD], BF16, kind="Internal").ap()
    dtw_hbm = nc.dram_tensor("dtw_hbm", [NSLOT, 2], F32, kind="Internal").ap()
    y_hbm = nc.dram_tensor("y_hbm", [NSLOT, HD], BF16, kind="Internal").ap()

    taps = {}

    def tapt(name, shape, dtype):
        if name in TAPS:
            taps[name] = nc.dram_tensor("tap_" + name, shape, dtype, kind="ExternalOutput").ap()
            return taps[name]
        return None

    ident_bf = b.alloc("ident_bf", [128, 128], BF16)
    ident_f = b.alloc("ident_f", [128, 128], F32)
    lt_f = b.alloc("lt_f", [128, 128], F32)
    negtri = b.alloc("negtri", [128, 128], F32)
    iota32 = b.alloc("iota32", [128, 32], F32)
    iota_f = b.alloc("iota_f", [128, 1], F32)
    ones_col = b.alloc("ones_col", [128, 1], F32)
    ones_row = b.alloc("ones_row", [1, 128], F32)

    w8_all = b.alloc("w8_all", [128, NTT, 8], F32)
    slot_all = b.alloc("slot_all", [128, NTT, 8], U32)
    cnt_base = b.alloc("cnt_base", [1, 32], F32)
    poisv = b.alloc("poisv", [1, 4], F32)
    scG = b.scope()
    cos_sb = scG.alloc("cos_sb", [128, 16, DR], F32)
    sin_sb = scG.alloc("sin_sb", [128, 16, DR], F32)
    gq = scG.alloc("gq", [128, QL], F32)
    gkv = scG.alloc("gkv", [128, KL], F32)

    p.dma("sync", d_cst, ident_f[:], cs["ident"])
    p.dma("gpsimd", d_cstg, ident_bf[:], cs["ident"])
    p.dma("sync", d_cst, lt_f[:], cs["lt"])
    p.dma("sync", d_cst, negtri[:], cs["negtri"])
    p.dma("sync", d_cst, iota32[:], cs["iota32"])
    p.dma("sync", d_cst, iota_f[:], cs["iota_f"])
    p.dma("sync", d_cst, ones_col[:], cs["ones_col"])
    p.dma("sync", d_cst, ones_row[:], cs["ones_row"])
    p.dma("sync", d_cst, cos_sb[:], cs["cos_t"].rearrange("(t p) d -> p t d", p=128))
    p.dma("sync", d_cst, sin_sb[:], cs["sin_t"].rearrange("(t p) d -> p t d", p=128))
    p.dma("sync", d_cst, gq[:], gamma_qb_i)
    p.dma("sync", d_cst, gkv[:], gamma_kvb_i)
    CST = (d_cst, d_cst.n)

    p.op("vector", lambda E: E.memset(cnt_base[:], 0.0))
    sc0 = b.scope()
    zer = sc0.alloc("zer", [128, 640], F32)
    tz = p.op("vector", lambda E: E.memset(zer[:], 0.0), tick=True)
    p.waitt("sync", tz)
    p.dma("sync", d_z, dtw_hbm.rearrange("(p j) c -> p (j c)", p=128), zer[:])

    if "A1" in BUILD:
        sc = b.scope()
        wkva = sc.alloc("wkva", [128, 16, FU], BF16)
        xt = sc.alloc("xt", [128, 2, HD], F32)
        xbf = sc.alloc("xbf", [128, 2, HD], BF16)
        xfm = sc.alloc("xfm", [128, 2, 16, 128], BF16)
        fus = sc.alloc("fus", [128, 2, FU], F32)
        sqs = sc.alloc("sqs", [128, QL], F32)
        qcn = sc.alloc("qcn", [128, 2, QL], BF16)
        qcnf = sc.alloc("qcnf", [128, 2, 12, 128], BF16)
        kvn = sc.alloc("kvn", [128, 2, KL], BF16)
        kpo = sc.alloc("kpo", [128, 2, DR], BF16)
        rsc = sc.alloc("rsc", [128, 12], F32)
        rps = sc.alloc("rps", [128, 2 * DR], F32)

        tw = p.dma("gpsimd", d_w, wkva[:], w_kv_a.rearrange("(c p) n -> p c n", p=128))
        p.waitt("tensor", tw)
        p.wait("tensor", d_cstg, 16)
        p.waitt("tensor", CST)
        p.waitt("vector", CST)
        p.waitt("scalar", CST)

        NCHUNKS = [(0, 512), (512, 1024), (1024, 1536), (1536, 2048), (2048, FU)]
        for g in range(NTT if DEV_G is None else DEV_G):
            s, t = divmod(g, 16)
            a = g % 2
            p.waitr("sync", f"xt{a}")
            tx = p.dma("sync", d_x[a], xt[:, a, :], xin[s, t * 128:(t + 1) * 128, :])
            p.waitt("vector", tx)
            p.waitr("vector", f"xbf{a}")
            t_cast = p.op("vector", lambda E, aa=a: E.tensor_copy(xbf[:, aa, :], xt[:, aa, :]), tick=True)
            b.setr(f"xt{a}", t_cast)
            p.waitt("tensor", t_cast)
            for grp in range(2):
                items = [(xbf[:, a, (grp * 8 + j) * 128:(grp * 8 + j + 1) * 128], 128, j * 128)
                         for j in range(8)]
                tev = b.transposes(items, b.psT[grp], ident_bf[:], xfm[:, a, grp * 8:grp * 8 + 8, :],
                                   pre_evac_wait=b.ring.get(f"xfm{a}"))
            b.setr(f"xbf{a}", b.pet())
            t_xfm = tev
            p.waitt("tensor", t_xfm)
            for ci, (n0, n1) in enumerate(NCHUNKS):
                bank = b.psA[ci % 2]
                b.bank_wait(bank)
                tmm = b.mm_chain(bank, n1 - n0,
                                 [(xfm[:, a, k, :], wkva[:, k, n0:n1]) for k in range(16)])
                eng = "vector" if ci % 2 == 0 else "scalar"
                p.waitt(eng, tmm)
                if eng == "vector":
                    tev = p.op(eng, lambda E, aa=a, m0=n0, m1=n1, bk=bank:
                               E.tensor_copy(fus[:, aa, m0:m1], bk[:, 0:m1 - m0]), tick=True)
                else:
                    tev = p.op(eng, lambda E, aa=a, m0=n0, m1=n1, bk=bank:
                               E.activation(out=fus[:, aa, m0:m1], in_=bk[:, 0:m1 - m0], func=AF.Copy), tick=True)
                b.bank_set(bank, tev)
            b.setr(f"xfm{a}", b.pet())
            t_fusv, t_fusa = b.dvt(), b.act()
            p.waitt("scalar", t_fusv)
            p.op("scalar", lambda E, aa=a: E.activation(out=sqs[:, 0:QL], in_=fus[:, aa, 0:QL],
                                                        func=AF.Square, accum_out=rsc[:, 0:1]))
            t_s2 = p.op("scalar", lambda E, aa=a: E.activation(out=sqs[:, 0:KL], in_=fus[:, aa, QL:QL + KL],
                                                               func=AF.Square, accum_out=rsc[:, 1:2]), tick=True)
            p.waitt("vector", t_s2)
            p.op("vector", lambda E: E.tensor_scalar(out=rsc[:, 2:3], in0=rsc[:, 0:1], scalar1=1.0 / QL,
                                                     scalar2=1e-6, op0=ALU.mult, op1=ALU.add))
            t_pre = p.op("vector", lambda E: E.tensor_scalar(out=rsc[:, 3:4], in0=rsc[:, 1:2], scalar1=1.0 / KL,
                                                             scalar2=1e-6, op0=ALU.mult, op1=ALU.add), tick=True)
            p.waitt("scalar", t_pre)
            t_sq = p.op("scalar", lambda E: E.activation(out=rsc[:, 4:6], in_=rsc[:, 2:4], func=AF.Sqrt), tick=True)
            p.waitt("vector", t_sq)
            p.op("vector", lambda E: E.reciprocal(rsc[:, 6:7], rsc[:, 4:5]))
            p.op("vector", lambda E: E.reciprocal(rsc[:, 7:8], rsc[:, 5:6]))
            p.waitr("vector", f"qcn{a}")
            p.waitr("vector", f"kvn{a}")
            p.op("vector", lambda E, aa=a: E.scalar_tensor_tensor(
                out=qcn[:, aa, :], in0=fus[:, aa, 0:QL], scalar=rsc[:, 6:7], in1=gq[:],
                op0=ALU.mult, op1=ALU.mult))
            t_kvn = p.op("vector", lambda E, aa=a: E.scalar_tensor_tensor(
                out=kvn[:, aa, :], in0=fus[:, aa, QL:QL + KL], scalar=rsc[:, 7:8], in1=gkv[:],
                op0=ALU.mult, op1=ALU.mult), tick=True)
            kp = fus[:, a, QL + KL:FU]
            cv = cos_sb[:, t, :]
            sv = sin_sb[:, t, :]
            p.op("vector", lambda E, x=kp, c2=cv: E.tensor_mul(rps[:, 0:DR], x, c2))
            p.op("vector", lambda E, x=kp, s2=sv: E.tensor_mul(rps[:, DR:DR + 32], x[:, 32:64], s2[:, 0:32]))
            p.op("vector", lambda E, x=kp, s2=sv: E.tensor_mul(rps[:, DR + 32:DR + 64], x[:, 0:32], s2[:, 32:64]))
            p.op("vector", lambda E, aa=a: E.tensor_sub(kpo[:, aa, 0:32], rps[:, 0:32], rps[:, DR:DR + 32]))
            t_kpo = p.op("vector", lambda E, aa=a: E.tensor_add(kpo[:, aa, 32:64], rps[:, 32:64],
                                                                rps[:, DR + 32:DR + 64]), tick=True)
            p.waitt("tensor", t_kvn)
            for grp, ntr in ((0, 8), (1, 4)):
                items = [(qcn[:, a, (grp * 8 + j) * 128:(grp * 8 + j + 1) * 128], 128, j * 128)
                         for j in range(ntr)]
                tev = b.transposes(items, b.psT[grp], ident_bf[:],
                                   qcnf[:, a, grp * 8:grp * 8 + ntr, :],
                                   pre_evac_wait=b.ring.get(f"qcnf{a}"))
            b.setr(f"qcn{a}", b.pet())
            t_qf = tev
            p.waitt("sync", t_qf)
            p.waitt("sync", t_kpo)
            tst = p.dma("sync", d_sA[a], qcn_hbm[g], qcnf[:, a, :, :])
            b.setr(f"qcnf{a}", tst)
            p.dma("sync", d_sB[a], kvn_hbm[g], kvn[:, a, :])
            tst = p.dma("sync", d_sB[a], kpe_hbm[g], kpo[:, a, :])
            b.setr(f"kvn{a}", tst)

        tp = tapt("fused31", [128, FU], F32)
        if tp is not None:
            p.dma("sync", d_st, tp, fus[:, 1, :])
        tp = tapt("qcn31", [128, QL], BF16)
        if tp is not None:
            p.dma("sync", d_st, tp, qcn[:, 1, :])
        b.fence()
        sc.close()
        b.ring.clear()

    if "A2" in BUILD:
        sc = b.scope()
        wqb = sc.alloc("wqb", [128, 12, NQ], BF16)
        qti = sc.alloc("qti", [128, 2, QL], BF16)
        qf32 = sc.alloc("qf32", [128, 2, NQ], F32)
        qbf = sc.alloc("qbf", [128, 2, NQ], BF16)
        rp2 = sc.alloc("rp2", [128, NH, 2 * DR], F32)
        c16 = sc.alloc("c16", [128, 2, 2, NH, DR], F32)

        tw = p.dma("gpsimd", d_w, wqb[:], w_qb.rearrange("(c p) n -> p c n", p=128))
        p.waitt("tensor", tw)

        for g in range(NTT if DEV_G is None else DEV_G):
            s, t = divmod(g, 16)
            a = g % 2
            p.waitr("sync", f"qti{a}")
            tq = p.dma("sync", d_q[a], qti[:, a, :], qcn_hbm[g])
            p.waitr("scalar", f"c16{a}")
            p.dma("sync", d_c16[a], c16[:, a, 0], cs["cos16"].rearrange("(t p) h d -> t p h d", p=128)[t])
            tc16 = p.dma("sync", d_c16[a], c16[:, a, 1], cs["sin16"].rearrange("(t p) h d -> t p h d", p=128)[t])
            p.waitt("tensor", tq)
            p.waitr("tensor", f"qf{a}")
            for ci in range(6):
                n0 = ci * 512
                bank = b.psA[ci % 2]
                b.bank_wait(bank)
                tmm = b.mm_chain(bank, 512,
                                 [(qti[:, a, k * 128:(k + 1) * 128], wqb[:, k, n0:n0 + 512])
                                  for k in range(12)])
                eng = "vector" if ci % 2 == 0 else "scalar"
                p.waitt(eng, tmm)
                if eng == "vector":
                    tev = p.op(eng, lambda E, aa=a, m0=n0, bk=bank:
                               E.tensor_copy(qf32[:, aa, m0:m0 + 512], bk[:]), tick=True)
                else:
                    tev = p.op(eng, lambda E, aa=a, m0=n0, bk=bank:
                               E.activation(out=qf32[:, aa, m0:m0 + 512], in_=bk[:], func=AF.Copy), tick=True)
                b.bank_set(bank, tev)
            b.setr(f"qti{a}", b.pet())
            p.waitt("vector", b.act())
            p.waitt("vector", tc16)
            xpe = qf32[:, a, :].rearrange("p (h d) -> p h d", d=DN + DR)[:, :, DN:DN + DR]
            cv16 = c16[:, a, 0]
            sv16 = c16[:, a, 1]
            p.op("vector", lambda E, x=xpe, c2=cv16: E.tensor_tensor(rp2[:, :, 0:DR], x, c2, ALU.mult))
            p.op("vector", lambda E, x=xpe, s2=sv16:
                 E.tensor_tensor(rp2[:, :, DR:DR + 32], x[:, :, 32:64], s2[:, :, 0:32], ALU.mult))
            p.op("vector", lambda E, x=xpe, s2=sv16:
                 E.tensor_tensor(rp2[:, :, DR + 32:DR + 64], x[:, :, 0:32], s2[:, :, 32:64], ALU.mult))
            p.op("vector", lambda E, x=xpe: E.tensor_tensor(x[:, :, 0:32], rp2[:, :, 0:DR][:, :, 0:32],
                                                            rp2[:, :, DR:DR + 32], ALU.subtract))
            tr5 = p.op("vector", lambda E, x=xpe: E.tensor_tensor(x[:, :, 32:64], rp2[:, :, 32:64],
                                                                  rp2[:, :, DR + 32:DR + 64], ALU.add), tick=True)
            b.setr(f"c16{a}", tr5)
            p.waitr("vector", f"qbf{a}")
            t_rope = p.op("vector", lambda E, aa=a: E.tensor_scalar(
                out=qbf[:, aa, :], in0=qf32[:, aa, :], scalar1=SCALE, scalar2=None, op0=ALU.mult), tick=True)
            b.setr(f"qf{a}", t_rope)
            p.waitt("sync", t_rope)
            tst = p.dma("sync", d_sQ[a], Q_hbm[s, t * 128:(t + 1) * 128, :], qbf[:, a, :])
            b.setr(f"qbf{a}", tst)

        tp = tapt("q31", [128, NQ], BF16)
        if tp is not None:
            p.dma("sync", d_st, tp, qbf[:, 1, :])
        b.fence()
        sc.close()
        b.ring.clear()

    sc0.close()
    scG.close()

    if "A3B" in BUILD or "A3" in BUILD or "B" in BUILD:
        DO_B = ("A3B" in BUILD) or ("B" in BUILD)
        scK = b.scope()
        KN = scK.alloc("KN", [128, NH, S], BF16)
        KP = scK.alloc("KP", [128, S], BF16)
        VS = scK.alloc("VS", [128, S // 128, NH * DV], BF16)

        def _seq_body(s):
            sc = b.scope()
            wkvb = sc.alloc("wkvb", [128, 4, NKV], BF16)
            kvi = sc.alloc("kvi", [128, 2, KL], BF16)
            kpi = sc.alloc("kpi", [128, 2, DR], BF16)
            kvf = sc.alloc("kvf", [128, 2, 4, 128], BF16)
            knb = sc.alloc("knb", [128, 2, NH, DN], BF16)

            t_wkvb = p.dma("gpsimd", d_w, wkvb[:], w_kvb.rearrange("(c p) n -> p c n", p=128))
            p.waitt("tensor", t_wkvb)
            for t in range(16 if DEV_T is None else DEV_T):
                g = s * 16 + t
                a = t % 2
                p.waitr("sync", f"kvi{a}")
                p.dma("sync", d_q[a], kvi[:, a, :], kvn_hbm[g])
                tk2 = p.dma("sync", d_q[a], kpi[:, a, :], kpe_hbm[g])
                p.waitt("tensor", tk2)
                items = [(kvi[:, a, j * 128:(j + 1) * 128], 128, j * 128) for j in range(4)]
                tev = b.transposes(items, b.psT[0], ident_bf[:], kvf[:, a, :, :],
                                   pre_evac_wait=b.ring.get(f"kvf{a}"))
                p.waitt("tensor", tev)
                p.waitr("tensor", f"knb{a}")
                for ci in range(8):
                    n0 = ci * 512
                    bank = b.psA[ci % 2]
                    b.bank_wait(bank)
                    tmm = b.mm_chain(bank, 512,
                                     [(kvf[:, a, k, :], wkvb[:, k, n0:n0 + 512]) for k in range(4)])
                    eng = "vector" if ci % 2 == 0 else "scalar"
                    p.waitt(eng, tmm)
                    pvw = bank[:].rearrange("p (h c) -> p h c", c=256)
                    vdst = VS[:, t, :].rearrange("p (h x) -> p h x", x=128)[:, 2 * ci:2 * ci + 2, :]
                    if eng == "vector":
                        p.op(eng, lambda E, aa=a, h0=2 * ci, q=pvw:
                             E.tensor_copy(knb[:, aa, h0:h0 + 2, :], q[:, :, 0:128]))
                        tev = p.op(eng, lambda E, q=pvw, vd=vdst:
                                   E.tensor_copy(vd, q[:, :, 128:256]), tick=True)
                    else:
                        p.op(eng, lambda E, aa=a, h0=2 * ci, q=pvw:
                             E.activation(out=knb[:, aa, h0:h0 + 2, :], in_=q[:, :, 0:128], func=AF.Copy))
                        tev = p.op(eng, lambda E, q=pvw, vd=vdst:
                                   E.activation(out=vd, in_=q[:, :, 128:256], func=AF.Copy),
                                   tick=True)
                    b.bank_set(bank, tev)
                b.setr(f"kvf{a}", b.pet())
                t_ev_v, t_ev_a = b.dvt(), b.act()
                p.waitt("tensor", t_ev_v)
                p.waitt("tensor", t_ev_a)
                for grp in range(2):
                    items = [(knb[:, a, grp * 8 + j, :], 128, j * 128) for j in range(8)]
                    dst = KN[:, grp * 8:grp * 8 + 8, t * 128:(t + 1) * 128]
                    b.transposes(items, b.psT[grp], ident_bf[:], dst)
                b.setr(f"knb{a}", b.pet())
                b.bank_wait(b.psT[0])
                p.op("tensor", lambda E, aa=a: E.transpose(b.psT[0][0:64, 0:128], kpi[:, aa, :], ident_bf[:]),
                     tick=True)
                tpe = b.pet()
                b.setr(f"kvi{a}", tpe)
                p.waitt("vector", tpe)
                tev = p.op("vector", lambda E, tt=t: E.tensor_copy(KP[0:64, tt * 128:(tt + 1) * 128],
                                                                   b.psT[0][0:64, 0:128]), tick=True)
                b.bank_set(b.psT[0], tev)
            b.fence()
            sc.close()

            if not DO_B:
                return
            sc = b.scope()
            q_in = sc.alloc("q_in", [128, 2, NQ], BF16)
            Qn = sc.alloc("Qn", [128, 32, 128], BF16)
            Qp = sc.alloc("Qp", [128, 32, 128], BF16)
            sc_f = sc.alloc("sc_f", [128, 2, 2, S], BF16)
            P_sb = sc.alloc("P_sb", [128, 4, 512], BF16)
            PT_sb = sc.alloc("PT", [128, 2, 4, 128], BF16)
            mtl = sc.alloc("mtl", [128, 2, 8], F32)
            nmw = sc.alloc("nmw", [128, 2, 2], F32)
            rs8 = sc.alloc("rs8", [128, 2, 8], F32)
            lsr = sc.alloc("lsr", [128, 2, 4], F32)
            atb = sc.alloc("atb", [128, 2, HD], BF16)

            nm_t = {}
            NHW = 2
            NW = 16 // NHW * 2

            def pass1(w, qb):
                u = w % 2
                qt = w // (NW // 2)
                hp = w % (NW // 2)
                q0 = qb * 256 + qt * 128
                nkb = (q0 + 128 + 511) // 512
                sbanks = [b.psR, b.psF, b.psB[0], b.psB[1]]
                for hi in range(NHW):
                    h = hp * NHW + hi
                    hq = h * 2 + qt
                    for kb in range(nkb):
                        ncols = min(512, q0 + 128 - kb * 512)
                        full = (kb * 512 + 512 <= q0 + 1)
                        bank = sbanks[(hi * nkb + kb) % 4]
                        b.bank_wait(bank)
                        p.op("tensor", lambda E, bk=bank, q2=hq, c=ncols, k0=kb * 512:
                             E.matmul(bk[:, 0:c], Qn[:, q2, :], KN[:, q2 // 2, k0:k0 + c],
                                      start=True, stop=False))
                        tsc2 = p.op("tensor", lambda E, bk=bank, q2=hq, c=ncols, k0=kb * 512:
                                    E.matmul(bk[:, 0:c], Qp[0:64, q2, :], KP[0:64, k0:k0 + c],
                                             start=False, stop=True), tick=True)
                        p.waitt("vector", tsc2)
                        if hi == 0 and kb == 0:
                            p.waitr("vector", f"scf{u}")
                        if not full:
                            dlt = ncols - 128
                            p.op("vector", lambda E, bk=bank, d2=dlt:
                                 E.tensor_add(bk[:, d2:d2 + 128], bk[:, d2:d2 + 128], negtri[:]))
                        tev = p.op("vector", lambda E, bk=bank, uu=u, c=ncols, k0=kb * 512, h2=hi, kk=kb:
                                   E.tensor_scalar(out=sc_f[:, uu, h2, k0:k0 + c], in0=bk[:, 0:c],
                                                   scalar1=1.0, scalar2=NEG,
                                                   op0=ALU.mult, op1=ALU.max,
                                                   accum_out=mtl[:, uu, h2 * 4 + kk:h2 * 4 + kk + 1]),
                                   tick=True)
                        b.bank_set(bank, tev)
                p.op("vector", lambda E, uu=u, nk=nkb:
                     E.reduce_max(nmw[:, uu, :], mtl[:, uu, :].rearrange("p (h k) -> p h k", k=4)[:, :, 0:nk],
                                  axis=AX.X, negate=True))
                nm_t[w] = b.dvt()

            def pass2(w, qb):
                u = w % 2
                qt = w // (NW // 2)
                hp = w % (NW // 2)
                q0 = qb * 256 + qt * 128
                nkb = (q0 + 128 + 511) // 512
                p.waitt("scalar", nm_t[w])
                vbanks = [b.psA[0], b.psA[1]]

                def pv(hi, kb, u2):
                    h = hp * NHW + hi
                    bank2 = vbanks[hi % 2]
                    ncols = min(512, q0 + 128 - kb * 512)
                    nsub = ncols // 128
                    if kb == 0:
                        b.bank_wait(bank2)
                    p.waitt("tensor", pt_t[u2])
                    for sub in range(nsub):
                        st2 = (kb == 0 and sub == 0)
                        sp2 = (kb == nkb - 1 and sub == nsub - 1)
                        p.op("tensor", lambda E, bk=bank2, u3=u2, s3=sub, h2=h, kt=kb * 4 + sub,
                             st3=st2, sp3=sp2:
                             E.matmul(bk[:, 0:128], PT_sb[:, u3, s3, :],
                                      VS[:, kt, h2 * DV:(h2 + 1) * DV], start=st3, stop=sp3),
                             tick=sp2)
                    b.setr(f"PT{u2}", b.pet())

                units = [(hi, kb) for hi in range(NHW) for kb in range(nkb)]
                pt_t = {}
                t_exp = {}

                def emit_exp(idx):
                    hi, kb = units[idx]
                    ncols = min(512, q0 + 128 - kb * 512)
                    u4 = idx % 4
                    p.waitr("scalar", f"P{u4}")
                    t_exp[idx] = p.op(
                        "scalar", lambda E, uu=u, u3=u4, c=ncols, k0=kb * 512, h2=hi, kk=kb:
                        E.activation(out=P_sb[:, u3, 0:c], in_=sc_f[:, uu, h2, k0:k0 + c],
                                     func=AF.Exp, bias=nmw[:, uu, h2:h2 + 1],
                                     accum_out=rs8[:, uu, h2 * 4 + kk:h2 * 4 + kk + 1]), tick=True)

                for i2 in range(min(2, len(units))):
                    emit_exp(i2)
                prev = None
                for idx, (hi, kb) in enumerate(units):
                    ncols = min(512, q0 + 128 - kb * 512)
                    nsub = ncols // 128
                    u2 = idx % 2
                    u4 = idx % 4
                    items = [(P_sb[:, u4, j * 128:(j + 1) * 128], 128, j * 128) for j in range(nsub)]
                    p.waitt("tensor", t_exp[idx])
                    pt_t[u2] = b.transposes(items, b.psT[u2], ident_bf[:], PT_sb[:, u2, 0:nsub, :],
                                            pre_evac_wait=b.ring.get(f"PT{u2}"),
                                            eng="vector" if u2 == 0 else "scalar")
                    b.setr(f"P{u4}", b.pet())
                    if idx + 2 < len(units):
                        emit_exp(idx + 2)
                    if prev is not None:
                        pv(*prev)
                    prev = (hi, kb, u2)
                pv(*prev)
                tpv = b.pet()
                p.waitt("vector", tpv)
                p.op("vector", lambda E, uu=u, nk=nkb:
                     E.reduce_sum(lsr[:, uu, 0:NHW], rs8[:, uu, :].rearrange("p (h k) -> p h k", k=4)[:, :, 0:nk],
                                  axis=AX.X))
                p.op("vector", lambda E, uu=u: E.reciprocal(lsr[:, uu, NHW:2 * NHW], lsr[:, uu, 0:NHW]))
                p.waitr("vector", "atb")
                for hi in range(NHW):
                    h = hp * NHW + hi
                    tat = p.op("vector", lambda E, uu=u, qt2=qt, h2=h, h3=hi, bk=vbanks[hi % 2]:
                               E.tensor_scalar(out=atb[:, qt2, h2 * DV:(h2 + 1) * DV], in0=bk[:, 0:128],
                                               scalar1=lsr[:, uu, NHW + h3:NHW + h3 + 1], scalar2=None,
                                               op0=ALU.mult), tick=True)
                    b.bank_set(vbanks[hi % 2], tat)
                b.setr(f"scf{u}", b.act())
                return tat

            for qb in range(8 if DEV_QB is None else DEV_QB):
                q0b = qb * 256
                p.waitr("sync", "q_in")
                tq = p.dma("sync", d_q[qb % 2], q_in[:],
                           Q_hbm[s, q0b:q0b + 256, :].rearrange("(a p) n -> p a n", p=128))
                p.waitt("tensor", tq)
                for grp in range(4):
                    items = []
                    for i in range(8):
                        hq = grp * 8 + i
                        h, qt = divmod(hq, 2)
                        items.append((q_in[:, qt, h * (DN + DR):h * (DN + DR) + DN], 128, i * 128))
                    b.transposes(items, b.psT[grp % 2], ident_bf[:], Qn[:, grp * 8:grp * 8 + 8, :],
                                 pre_evac_wait=b.ring.get("Qfm"))
                for grp in range(4):
                    items = []
                    for i in range(8):
                        hq = grp * 8 + i
                        h, qt = divmod(hq, 2)
                        items.append((q_in[:, qt, h * (DN + DR) + DN:h * (DN + DR) + DN + DR], 64, i * 128))
                    b.transposes(items, b.psT[grp % 2], ident_bf[:], Qp[0:64, grp * 8:grp * 8 + 8, :],
                                 pre_evac_wait=b.ring.get("Qfm"))
                b.setr("q_in", b.pet())
                for w in range(NW):
                    pass1(w, qb)
                    if w >= 1:
                        pass2(w - 1, qb)
                t_last = pass2(NW - 1, qb)
                p.waitt("sync", b.dvt())
                p.waitr("sync", "atb")
                tst = p.dma("sync", d_sAt,
                            attn_hbm[s, q0b:q0b + 256, :].rearrange("(a p) n -> p a n", p=128), atb[:])
                b.setr("atb", tst)
                b.setr("Qfm", b.pet())
            tp = tapt(f"atb_s{s}", [128, 2, HD], BF16)
            if tp is not None:
                p.dma("sync", d_st, tp, atb[:])
            b.fence()
            sc.close()
            b.ring.clear()
        for s in range(SEQ if DEV_SEQ is None else DEV_SEQ):
            _seq_body(s)
        scK.close()

    if "B2" in BUILD:
        b.fence()
        sc = b.scope()
        wo = sc.alloc("wo", [128, 16, HD], BF16)
        ati = sc.alloc("ati", [128, 2, HD], BF16)
        afm = sc.alloc("afm", [128, 2, 16, 128], BF16)
        xr = sc.alloc("xr", [128, 2, HD], F32)
        x1 = sc.alloc("x1", [128, 2, HD], F32)
        x1b = sc.alloc("x1b", [128, 2, HD], BF16)
        x1fm = sc.alloc("x1fm", [128, 2, 16, 128], F32)
        sq2 = sc.alloc("sq2", [128, HD], F32)
        gwsb = sc.alloc("gwsb", [128, 16, EXP], F32)
        lg = sc.alloc("lg", [128, 3, EXP], F32)
        lgr = sc.alloc("lgr", [128, EXP], F32)
        I8 = sc.alloc("I8", [128, 3, EXP], F32)
        Rm = sc.alloc("Rm", [128, EXP], F32)
        smap = sc.alloc("smap", [128, EXP], F32)
        pmap = sc.alloc("pmap", [128, EXP], F32)
        dmap = sc.alloc("dmap", [128, 2, EXP], F32)
        tv8 = sc.alloc("tv8", [128, 3, 8], F32)
        ti8 = sc.alloc("ti8", [128, 8], U32)
        ex8 = sc.alloc("ex8", [128, 8], F32)
        oh = sc.alloc("oh", [128, EXP], F32)
        csc = sc.alloc("csc", [128, EXP], F32)
        ohf = sc.alloc("ohf", [128, EXP], F32)
        slotf = sc.alloc("slotf", [128, 8], F32)
        posf = sc.alloc("posf", [128, 8], F32)
        posu = sc.alloc("posu", [128, 2, 8], U32)
        pv = sc.alloc("pv", [128, 2, 8, 2], F32)
        tokf = sc.alloc("tokf", [128, 1], F32)
        sm8 = sc.alloc("sm8", [128, 4], F32)
        lnst = sc.alloc("lnst", [128, 12], F32)

        p.wait("gpsimd", d_z, d_z.n)
        tw = p.dma("gpsimd", d_w, wo[:], w_o.rearrange("(c p) n -> p c n", p=128))
        tgw = p.dma("sync", d_cs2, gwsb[:], gate_w.rearrange("(c p) n -> p c n", p=128))
        p.waitt("tensor", tw)
        p.waitt("tensor", tgw)

        t_i8 = {}

        def b2_stage1(g):
            s, t = divmod(g, 16)
            a = g % 2
            u3 = g % 3
            p.waitr("sync", f"ati{a}")
            ta = p.dma("sync", d_q[a], ati[:, a, :], attn_hbm[s, t * 128:(t + 1) * 128, :])
            p.waitr("sync", f"xr{a}")
            txr = p.dma("sync", d_x[a], xr[:, a, :], xin[s, t * 128:(t + 1) * 128, :])
            p.waitt("tensor", ta)
            for grp in range(2):
                items = [(ati[:, a, (grp * 8 + j) * 128:(grp * 8 + j + 1) * 128], 128, j * 128)
                         for j in range(8)]
                b.transposes(items, b.psT[grp], ident_bf[:], afm[:, a, grp * 8:grp * 8 + 8, :],
                             pre_evac_wait=b.ring.get(f"afm{a}"))
            b.setr(f"ati{a}", b.pet())
            t_afm = b.dvt()
            p.waitt("tensor", t_afm)
            p.waitt("vector", txr)
            p.waitr("vector", f"x1{a}")
            p.waitr("vector", f"x1b{a}")
            for ci in range(4):
                n0 = ci * 512
                bank = b.psA[ci % 2]
                b.bank_wait(bank)
                tmm = b.mm_chain(bank, 512,
                                 [(afm[:, a, k, :], wo[:, k, n0:n0 + 512]) for k in range(16)])
                p.waitt("vector", tmm)
                tev = p.op("vector", lambda E, aa=a, m0=n0, bk=bank, c2=ci:
                           E.scalar_tensor_tensor(out=x1[:, aa, m0:m0 + 512], in0=bk[:],
                                                  scalar=1.0, in1=xr[:, aa, m0:m0 + 512],
                                                  op0=ALU.mult, op1=ALU.add,
                                                  accum_out=lnst[:, c2:c2 + 1]), tick=True)
                b.bank_set(bank, tev)
            b.setr(f"afm{a}", b.pet())
            b.setr(f"xr{a}", b.dvt())
            p.op("vector", lambda E: E.tensor_add(lnst[:, 4:5], lnst[:, 0:1], lnst[:, 1:2]))
            p.op("vector", lambda E: E.tensor_add(lnst[:, 5:6], lnst[:, 2:3], lnst[:, 3:4]))
            t_resid = p.op("vector", lambda E: E.tensor_add(lnst[:, 6:7], lnst[:, 4:5], lnst[:, 5:6]), tick=True)
            p.waitt("scalar", t_resid)
            t_s2 = p.op("scalar", lambda E, aa=a: E.activation(out=sq2[:], in_=x1[:, aa, :],
                                                               func=AF.Square, accum_out=lnst[:, 7:8]), tick=True)
            p.op("vector", lambda E: E.tensor_scalar(out=lnst[:, 8:9], in0=lnst[:, 6:7],
                                                     scalar1=1.0 / HD, scalar2=None, op0=ALU.mult))
            p.waitt("vector", t_s2)
            p.op("vector", lambda E: E.tensor_scalar(out=lnst[:, 9:10], in0=lnst[:, 7:8],
                                                     scalar1=1.0 / HD, scalar2=None, op0=ALU.mult))
            p.op("vector", lambda E: E.tensor_mul(lnst[:, 10:11], lnst[:, 8:9], lnst[:, 8:9]))
            p.op("vector", lambda E: E.scalar_tensor_tensor(out=lnst[:, 10:11], in0=lnst[:, 10:11],
                                                            scalar=-1.0, in1=lnst[:, 9:10],
                                                            op0=ALU.mult, op1=ALU.add))
            t_ve = p.op("vector", lambda E: E.tensor_scalar(out=lnst[:, 11:12], in0=lnst[:, 10:11],
                                                            scalar1=1.0, scalar2=1e-5,
                                                            op0=ALU.mult, op1=ALU.add), tick=True)
            p.waitt("scalar", t_ve)
            t_sqr = p.op("scalar", lambda E: E.activation(out=lnst[:, 10:11], in_=lnst[:, 11:12],
                                                          func=AF.Sqrt), tick=True)
            p.waitt("vector", t_sqr)
            p.op("vector", lambda E: E.reciprocal(lnst[:, 11:12], lnst[:, 10:11]))
            p.op("vector", lambda E, aa=a: E.tensor_scalar(out=x1[:, aa, :], in0=x1[:, aa, :],
                                                           scalar1=lnst[:, 8:9], scalar2=lnst[:, 11:12],
                                                           op0=ALU.subtract, op1=ALU.mult))
            t_x1 = p.op("vector", lambda E, aa=a: E.tensor_copy(x1b[:, aa, :], x1[:, aa, :]), tick=True)
            p.waitt("sync", t_x1)
            p.dma("sync", d_sx1[a], x1f_hbm[g], x1[:, a, :])
            tst = p.dma("sync", d_sx1[a], x1b_hbm[g * 128:(g + 1) * 128, :], x1b[:, a, :])
            b.setr(f"x1b{a}", tst)
            p.waitt("tensor", t_x1)
            for grp in range(4):
                bank = b.psF if grp % 2 == 0 else b.psR
                b.bank_wait(bank)
                for i in range(4):
                    c2 = grp * 4 + i
                    p.op("tensor", lambda E, aa=a, cc=c2, o=i * 128, bk=bank, st=(i == 0), sp=(i == 3):
                         E.matmul(bk[:, o:o + 128], x1[:, aa, cc * 128:(cc + 1) * 128], ident_f[:],
                                  start=st, stop=sp, is_transpose=True),
                         tick=(i == 3))
                tpe = b.pet()
                p.waitt("vector", tpe)
                if grp < 2:
                    p.waitr("vector", f"x1fm{a}")
                tev = p.op("vector", lambda E, aa=a, g2=grp, bk=bank:
                           E.tensor_copy(x1fm[:, aa, g2 * 4:g2 * 4 + 4, :],
                                         bk[:].rearrange("p (c x) -> p c x", c=4)), tick=True)
                b.bank_set(bank, tev)
            t_fm = b.dvt()
            b.setr(f"x1{a}", b.pet())
            p.waitt("tensor", t_fm)
            bank = b.psR
            b.bank_wait(bank)
            tlg = b.mm_chain(bank, EXP, [(x1fm[:, a, k, :], gwsb[:, k, :]) for k in range(16)])
            b.setr(f"x1fm{a}", tlg)
            p.waitt("vector", tlg)
            tev = p.op("vector", lambda E, u4=u3, bk=bank: E.tensor_copy(lg[:, u4, :], bk[:, 0:EXP]), tick=True)
            b.bank_set(bank, tev)
            p.op("vector", lambda E, u4=u3: E.max(tv8[:, u4, :], lg[:, u4, :]))
            p.op("vector", lambda E, u4=u3: E.max_index(ti8[:], tv8[:, u4, :], lg[:, u4, :]))
            p.op("vector", lambda E, u4=u3: E.match_replace(lgr[:], tv8[:, u4, :], lg[:, u4, :], NEG))
            p.op("vector", lambda E, u4=u3: E.tensor_tensor(I8[:, u4, :], lg[:, u4, :], lgr[:], ALU.not_equal))
            t_nm8 = p.op("vector", lambda E, u4=u3: E.tensor_scalar(out=sm8[:, 0:1], in0=tv8[:, u4, 0:1],
                                                                    scalar1=-1.0, scalar2=None, op0=ALU.mult),
                         tick=True)
            p.waitt("scalar", t_nm8)
            t_e8 = p.op("scalar", lambda E, u4=u3: E.activation(out=ex8[:], in_=tv8[:, u4, :], func=AF.Exp,
                                                                bias=sm8[:, 0:1], accum_out=sm8[:, 1:2]), tick=True)
            p.waitt("vector", t_e8)
            p.op("vector", lambda E: E.reciprocal(sm8[:, 2:3], sm8[:, 1:2]))
            t_w8 = p.op("vector", lambda E, g2=g: E.tensor_scalar(out=w8_all[:, g2, :], in0=ex8[:],
                                                                  scalar1=sm8[:, 2:3], scalar2=None,
                                                                  op0=ALU.mult), tick=True)
            t_i8[g] = t_w8

        def b2_stage2(g):
            a = g % 2
            u3 = g % 3
            p.waitt("tensor", t_i8[g])
            bank = b.psR
            b.bank_wait(bank)
            p.op("tensor", lambda E, u4=u3, bk=bank: E.matmul(bk[:, 0:EXP], lt_f[:], I8[:, u4, :],
                                                              start=True, stop=False))
            tR = p.op("tensor", lambda E, bk=bank: E.matmul(bk[:, 0:EXP], ones_row[:], cnt_base[:],
                                                            start=False, stop=True), tick=True)
            bank2 = b.psF
            b.bank_wait(bank2)
            tTot = p.op("tensor", lambda E, u4=u3, bk2=bank2: E.matmul(bk2[0:1, 0:EXP], ones_col[:],
                                                                       I8[:, u4, :], start=True, stop=True),
                        tick=True)
            p.waitt("vector", tR)
            tev = p.op("vector", lambda E, bk=bank: E.tensor_copy(Rm[:], bk[:, 0:EXP]), tick=True)
            b.bank_set(bank, tev)
            p.waitt("vector", tTot)
            t_cnt = p.op("vector", lambda E, bk2=bank2: E.tensor_add(cnt_base[:], cnt_base[:],
                                                                     bk2[0:1, 0:EXP]), tick=True)
            b.bank_set(bank2, t_cnt)
            p.waitt("tensor", t_cnt)
            p.op("vector", lambda E: E.scalar_tensor_tensor(out=smap[:], in0=iota32[:], scalar=float(CAPL),
                                                            in1=Rm[:], op0=ALU.mult, op1=ALU.add))
            p.op("vector", lambda E: E.memset(dmap[:, 0, :], 0.0))
            for j in range(1, NCH):
                src2 = dmap[:, (j - 1) % 2, :]
                dst2 = dmap[:, j % 2, :]
                p.op("vector", lambda E, s2=src2, d2=dst2, jj=j:
                     E.scalar_tensor_tensor(out=d2, in0=Rm[:], scalar=float(jj * 128), in1=s2,
                                            op0=ALU.is_ge, op1=ALU.add))
            dfin = dmap[:, (NCH - 1) % 2, :]
            p.op("vector", lambda E, d2=dfin: E.scalar_tensor_tensor(out=pmap[:], in0=d2, scalar=-128.0,
                                                                     in1=Rm[:], op0=ALU.mult, op1=ALU.add))
            p.op("vector", lambda E, d2=dfin: E.scalar_tensor_tensor(out=oh[:], in0=iota32[:], scalar=float(NCH),
                                                                     in1=d2, op0=ALU.mult, op1=ALU.add))
            p.op("vector", lambda E: E.scalar_tensor_tensor(out=pmap[:], in0=pmap[:], scalar=float(EXP * NCH),
                                                            in1=oh[:], op0=ALU.mult, op1=ALU.add))
            p.op("vector", lambda E, g2=g: E.tensor_scalar(out=tokf[:], in0=iota_f[:], scalar1=float(g2 * 128),
                                                           scalar2=None, op0=ALU.add))
            for k in range(8):
                p.op("vector", lambda E, u4=u3, kk=k: E.tensor_scalar(out=oh[:], in0=lg[:, u4, :],
                                                                      scalar1=tv8[:, u4, kk:kk + 1],
                                                                      scalar2=None, op0=ALU.is_equal))
                p.op("vector", lambda E: E.tensor_tensor_scan(csc[:], oh[:], oh[:], 0.0, ALU.add, ALU.bypass))
                p.op("vector", lambda E: E.scalar_tensor_tensor(out=ohf[:], in0=csc[:], scalar=1.0,
                                                                in1=oh[:], op0=ALU.is_equal, op1=ALU.mult))
                p.op("vector", lambda E: E.tensor_mul(csc[:], ohf[:], smap[:]))
                p.op("vector", lambda E, kk=k: E.reduce_sum(slotf[:, kk:kk + 1], csc[:], axis=AX.X))
                p.op("vector", lambda E: E.tensor_mul(csc[:], ohf[:], pmap[:]))
                p.op("vector", lambda E, kk=k: E.reduce_sum(posf[:, kk:kk + 1], csc[:], axis=AX.X))
            p.op("vector", lambda E, g2=g: E.tensor_copy(slot_all[:, g2, :], slotf[:]))
            p.waitr("vector", f"pv{a}")
            p.op("vector", lambda E, aa=a: E.tensor_copy(posu[:, aa, :], posf[:]))
            for k in range(8):
                p.op("vector", lambda E, aa=a, kk=k: E.tensor_copy(pv[:, aa, kk, 0:1], tokf[:]))
            t_pv = p.op("vector", lambda E, aa=a, g2=g: E.tensor_copy(
                pv[:, aa, :, 1:2].rearrange("p k c -> p (k c)"), w8_all[:, g2, :]), tick=True)
            p.waitt("gpsimd", t_pv)
            for k in range(8):
                b.scatter(d_sc[a], dtw_hbm, pv[:, a, k, :], posu[:, a, k:k + 1], NSLOT - 1)
            b.setr(f"pv{a}", (d_sc[a], d_sc[a].n))

        NG2 = NTT if DEV_G is None else DEV_G
        for g in range(NG2):
            b2_stage1(g)
            if g >= 2:
                b2_stage2(g - 2)
        if NG2 >= 2:
            b2_stage2(NG2 - 2)
        b2_stage2(NG2 - 1)

        p.op("vector", lambda E: E.reduce_max(poisv[0:1, 0:1], cnt_base[:], axis=AX.X))
        p.op("vector", lambda E: E.tensor_scalar(out=poisv[0:1, 1:2], in0=poisv[0:1, 0:1],
                                                 scalar1=float(CAPL), scalar2=3.0e38,
                                                 op0=ALU.is_gt, op1=ALU.mult))
        p.op("vector", lambda E: E.tensor_add(poisv[0:1, 2:3], poisv[0:1, 1:2], poisv[0:1, 1:2]))
        tp = tapt("x1_t31", [128, HD], F32)
        if tp is not None:
            p.dma("sync", d_st, tp, x1[:, 1, :])
        tp = tapt("cnt", [1, 32], F32)
        if tp is not None:
            p.dma("sync", d_st, tp, cnt_base[:])
        tp = tapt("slotf31", [128, 8], F32)
        if tp is not None:
            p.dma("sync", d_st, tp, slotf[:])
        b.fence()
        sc.close()
        b.ring.clear()

    if "C" in BUILD:
        b.fence()
        sc = b.scope()
        wg = sc.alloc("wg", [128, 2, 16, INTER], BF16)
        wu = sc.alloc("wu", [128, 2, 16, INTER], BF16)
        wd = sc.alloc("wd", [128, 6, HD], BF16)
        xg = sc.alloc("xg", [128, 4, HD], BF16)
        xfc = sc.alloc("xfc", [128, 2, 16, 512], BF16)
        gsb = sc.alloc("gsb", [128, 2, 512], BF16)
        actc = sc.alloc("actc", [128, 2, 6, 512], BF16)
        ysb = sc.alloc("ysb", [128, 2, HD], BF16)
        dtw = sc.alloc("dtw", [128, NCH * EXP, 2], F32)
        didx = sc.alloc("didx", [128, NCH * EXP], U32)

        tld = p.dma("sync", d_cs2, dtw[:], dtw_hbm.rearrange("(p j) c -> p j c", p=128))
        p.waitt("vector", tld)
        t_didx = p.op("vector", lambda E: E.tensor_copy(didx[:], dtw[:, :, 0]), tick=True)
        p.waitt("gpsimd", t_didx)
        p.waitt("scalar", tld)

        stgc = sc.alloc("stgc", [128, 2, INTER], F32)
        wgv = w_gate.rearrange("e (c p) i -> e p c i", p=128)
        wuv = w_up.rearrange("e (c p) i -> e p c i", p=128)
        wdv = w_down.rearrange("e (c p) i -> e p c i", p=128)

        t_wg = {}
        t_wu = {}
        CJ = {"jobs": [], "i": 0}

        def cast_add(e):
            we2 = e % 2
            for j in range(16):
                CJ["jobs"].append((wgv[e, :, j, :], wg[:, we2, j, :], ("wg", e, j)))
                CJ["jobs"].append((wuv[e, :, j, :], wu[:, we2, j, :], ("wu", e, j)))

        def cast_pump(n):
            while n > 0 and CJ["i"] < len(CJ["jobs"]):
                src, dst, mark = CJ["jobs"][CJ["i"]]
                sl = CJ["i"] % 2
                CJ["i"] += 1
                n -= 1
                p.waitr("sync", f"cstg{sl}")
                tld2 = p.dma("sync", d_wc[sl], stgc[:, sl, 0:src.shape[-1]], src)
                if mark[2] == 0:
                    p.waitr("scalar", f"{mark[0]}{mark[1] % 2}")
                p.waitt("scalar", tld2)
                tc = p.op("scalar", lambda E, d=dst, s2=stgc[:, sl, 0:dst.shape[-1]]:
                          E.activation(out=d, in_=s2, func=AF.Copy), tick=True)
                b.setr(f"cstg{sl}", tc)
                (t_wg if mark[0] == "wg" else t_wu)[mark[1]] = tc

        cast_add(0)
        cast_pump(32)

        for e in range(EXP if DEV_E is None else DEV_E):
            we = e % 2
            p.waitr("gpsimd", "wd")
            t_wd = p.dma("gpsimd", d_wd, wd[:], wdv[e])
            if e + 1 < EXP:
                cast_add(e + 1)
            for c3 in range(3):
                cast_pump(11)
                csz = 512 if c3 < 2 else 256
                nsub = csz // 128
                ring = c3 % 2
                for sub in range(nsub):
                    j = e * NCH + c3 * 4 + sub
                    gslot = (c3 * 4 + sub) % 4
                    p.waitr("gpsimd", f"xg{gslot}")
                    tgt = b.gather(d_g[gslot], xg[:, gslot, :], x1b_hbm, didx[:, j:j + 1], TLOC - 1)
                    p.waitt("tensor", tgt)
                    if sub == 0:
                        p.waitr("tensor", f"xfc{ring}")
                    for grp in range(2):
                        items = [(xg[:, gslot, (grp * 8 + i) * 128:(grp * 8 + i + 1) * 128], 128, i * 128)
                                 for i in range(8)]
                        b.transposes(items, b.psT[grp], ident_bf[:],
                                     xfc[:, ring, grp * 8:grp * 8 + 8, sub * 128:(sub + 1) * 128])
                    b.setr(f"xg{gslot}", b.pet())
                t_xf = b.dvt()
                p.waitt("tensor", t_xf)
                p.waitt("tensor", t_wg[e])
                p.waitt("tensor", t_wu[e])
                for m in range(6):
                    bg = b.psA[m % 2]
                    bu = b.psB[m % 2]
                    gslot2 = m % 2
                    b.bank_wait(bg)
                    tg2 = b.mm_chain(bg, csz, [(wg[:, we, k, m * 128:(m + 1) * 128], xfc[:, ring, k, 0:csz])
                                               for k in range(16)])
                    b.bank_wait(bu)
                    tu2 = b.mm_chain(bu, csz, [(wu[:, we, k, m * 128:(m + 1) * 128], xfc[:, ring, k, 0:csz])
                                               for k in range(16)])
                    p.waitt("scalar", tg2)
                    p.waitr("scalar", f"gsb{gslot2}")
                    tsi = p.op("scalar", lambda E, bk=bg, c=csz, gs=gslot2:
                               E.activation(out=gsb[:, gs, 0:c], in_=bk[:, 0:c], func=AF.Silu), tick=True)
                    b.bank_set(bg, tsi)
                    p.waitt("vector", tsi)
                    p.waitt("vector", tu2)
                    p.waitr("vector", f"act{ring}_{m}")
                    tmu = p.op("vector", lambda E, bk=bu, c=csz, mm=m, rr=ring, gs=gslot2:
                               E.tensor_tensor(actc[:, rr, mm, 0:c], gsb[:, gs, 0:c], bk[:, 0:c], ALU.mult),
                               tick=True)
                    b.bank_set(bu, tmu)
                    b.setr(f"gsb{gslot2}", tmu)
                t_act = b.dvt()
                p.waitt("tensor", t_act)
                p.waitt("tensor", t_wd)
                for sub in range(nsub):
                    yring = sub % 2
                    j = e * NCH + c3 * 4 + sub
                    for ci in range(4):
                        bank = b.psF if ci % 2 == 0 else b.psR
                        b.bank_wait(bank)
                        tdn = b.mm_chain(bank, 512,
                                         [(actc[:, ring, m, sub * 128:(sub + 1) * 128],
                                           wd[:, m, ci * 512:(ci + 1) * 512]) for m in range(6)])
                        p.waitt("scalar", tdn)
                        if ci == 0:
                            p.waitr("scalar", f"ysb{yring}")
                        tev = p.op("scalar", lambda E, bk=bank, yr=yring, c2=ci, jj=j:
                                   E.activation(out=ysb[:, yr, c2 * 512:(c2 + 1) * 512], in_=bk[:],
                                                func=AF.Copy, scale=dtw[:, jj, 1:2]), tick=True)
                        b.bank_set(bank, tev)
                    t_ysb = b.act()
                    p.waitt("sync", t_ysb)
                    r0 = e * CAPL + (c3 * 4 + sub) * 128
                    tst = p.dma("sync", d_y[yring], y_hbm[r0:r0 + 128, :], ysb[:, yring, :])
                    b.setr(f"ysb{yring}", tst)
                b.setr(f"xfc{ring}", t_act)
                for mm in range(6):
                    b.setr(f"act{ring}_{mm}", b.pet())
            b.setr(f"wg{we}", b.pet())
            b.setr(f"wu{we}", b.pet())
            b.setr("wd", b.pet())
        b.fence()
        sc.close()
        b.ring.clear()

    if "D" in BUILD:
        b.fence()
        sc = b.scope()
        y8 = sc.alloc("y8", [128, 2, 8, HD], BF16)
        accX = sc.alloc("accX", [128, 2, HD], F32)
        x2f = sc.alloc("x2f", [128, 2, HD], F32)
        outb = sc.alloc("outb", [128, 2, HD], F32)
        sq3 = sc.alloc("sq3", [128, HD], F32)
        ln3 = sc.alloc("ln3", [128, 2, 12], F32)

        banks4 = [b.psA[0], b.psA[1], b.psB[0], b.psB[1]]
        tg_all = {}

        def d_issue(g):
            a = g % 2
            p.waitr("gpsimd", f"y8{a}")
            tg_all[g] = [b.gather(d_ga[k], y8[:, a, k, :], y_hbm, slot_all[:, g, k:k + 1], NSLOT - 1)
                         for k in range(8)]
            p.waitr("sync", f"accX{a}")
            tg_all[(g, "x")] = p.dma("sync", d_al[a], accX[:, a, :], x1f_hbm[g])

        def d_proc(g):
            a = g % 2
            for k in range(8):
                p.waitt("tensor", tg_all[g][k])
            for c in range(4):
                bank = banks4[c]
                b.bank_wait(bank)
                for k in range(8):
                    p.op("tensor", lambda E, bk=bank, aa=a, kk=k, cc=c, st=(k == 0), sp=(k == 7):
                         E.matmul(bk[:, 0:512], ident_bf[:], y8[:, aa, kk, cc * 512:(cc + 1) * 512],
                                  start=st, stop=sp), tick=(k == 7))
                tmm = b.pet()
                p.waitt("vector", tmm)
                if c == 0:
                    p.waitt("vector", tg_all[(g, "x")])
                    p.waitr("vector", f"x2f{a}")
                tev = p.op("vector", lambda E, bk=bank, aa=a, cc=c:
                           E.scalar_tensor_tensor(out=x2f[:, aa, cc * 512:(cc + 1) * 512], in0=bk[:],
                                                  scalar=1.0, in1=accX[:, aa, cc * 512:(cc + 1) * 512],
                                                  op0=ALU.mult, op1=ALU.add,
                                                  accum_out=ln3[:, aa, cc:cc + 1]), tick=True)
                b.bank_set(bank, tev)
            b.setr(f"y8{a}", b.pet())
            b.setr(f"accX{a}", b.dvt())
            p.op("vector", lambda E, aa=a: E.tensor_add(ln3[:, aa, 4:5], ln3[:, aa, 0:1], ln3[:, aa, 1:2]))
            p.op("vector", lambda E, aa=a: E.tensor_add(ln3[:, aa, 5:6], ln3[:, aa, 2:3], ln3[:, aa, 3:4]))
            t_sum = p.op("vector", lambda E, aa=a: E.tensor_add(ln3[:, aa, 6:7], ln3[:, aa, 4:5],
                                                                ln3[:, aa, 5:6]), tick=True)
            p.waitt("scalar", t_sum)
            t_s2 = p.op("scalar", lambda E, aa=a: E.activation(out=sq3[:], in_=x2f[:, aa, :],
                                                               func=AF.Square, accum_out=ln3[:, aa, 7:8]),
                        tick=True)
            p.op("vector", lambda E, aa=a: E.tensor_scalar(out=ln3[:, aa, 8:9], in0=ln3[:, aa, 6:7],
                                                           scalar1=1.0 / HD, scalar2=None, op0=ALU.mult))
            p.waitt("vector", t_s2)
            p.op("vector", lambda E, aa=a: E.tensor_scalar(out=ln3[:, aa, 9:10], in0=ln3[:, aa, 7:8],
                                                           scalar1=1.0 / HD, scalar2=None, op0=ALU.mult))
            p.op("vector", lambda E, aa=a: E.tensor_mul(ln3[:, aa, 10:11], ln3[:, aa, 8:9], ln3[:, aa, 8:9]))
            p.op("vector", lambda E, aa=a: E.scalar_tensor_tensor(out=ln3[:, aa, 10:11], in0=ln3[:, aa, 10:11],
                                                                  scalar=-1.0, in1=ln3[:, aa, 9:10],
                                                                  op0=ALU.mult, op1=ALU.add))
            t_ve = p.op("vector", lambda E, aa=a: E.tensor_scalar(out=ln3[:, aa, 11:12], in0=ln3[:, aa, 10:11],
                                                                  scalar1=1.0, scalar2=1e-5,
                                                                  op0=ALU.mult, op1=ALU.add), tick=True)
            p.waitt("scalar", t_ve)
            t_sq = p.op("scalar", lambda E, aa=a: E.activation(out=ln3[:, aa, 10:11], in_=ln3[:, aa, 11:12],
                                                               func=AF.Sqrt), tick=True)
            p.waitt("vector", t_sq)
            p.op("vector", lambda E, aa=a: E.reciprocal(ln3[:, aa, 11:12], ln3[:, aa, 10:11]))
            p.waitr("vector", f"outb{a}")
            t_o = p.op("vector", lambda E, aa=a: E.tensor_scalar(
                out=outb[:, aa, :], in0=x2f[:, aa, :], scalar1=ln3[:, aa, 8:9], scalar2=ln3[:, aa, 11:12],
                op0=ALU.subtract, op1=ALU.mult), tick=True)
            if g == NTT - 1:
                t_o = p.op("vector", lambda E, aa=a: E.tensor_scalar(
                    out=outb[0:1, aa, :], in0=outb[0:1, aa, :], scalar1=poisv[0:1, 2:3],
                    scalar2=None, op0=ALU.add), tick=True)
            b.setr(f"x2f{a}", b.dvt())
            p.waitt("sync", t_o)
            tst = p.dma("sync", d_ob[a], OUT[g * 128:(g + 1) * 128, :], outb[:, a, :])
            b.setr(f"outb{a}", tst)

        ND = NTT if DEV_D is None else DEV_D * 8
        d_issue(0)
        for g in range(1, ND):
            d_issue(g)
            d_proc(g - 1)
        d_proc(ND - 1)
        p.wait("sync", d_ob[0], d_ob[0].n)
        p.wait("sync", d_ob[1], d_ob[1].n)
        sc.close()

    b.taps_map = taps
    return b


def finalize(b):
    nc = b.nc
    with nc.Block() as block:
        @block.sync
        def _(E):
            for fn in b.st["sync"]:
                fn(E)

        @block.gpsimd
        def _(E):
            for fn in b.st["gpsimd"]:
                fn(E)

        @block.scalar
        def _(E):
            for fn in b.st["scalar"]:
                fn(E)

        @block.vector
        def _(E):
            for fn in b.st["vector"]:
                fn(E)

        @block.tensor
        def _(E):
            for fn in b.st["tensor"]:
                fn(E)
    return nc


_CACHE = {}


def _get_program():
    key = (tuple(BUILD), tuple(TAPS))
    if key not in _CACHE:
        bb = build_program()
        _CACHE[key] = (finalize(bb), bb)
    return _CACHE[key]


def make_in_maps(x, wts):
    hc = host_consts()
    base = dict(wts)
    for k, v in hc.items():
        base["c_" + k] = v
    in_maps = []
    for c in range(NCORE):
        m = dict(base)
        m["xin"] = np.ascontiguousarray(x[c * SEQ:(c + 1) * SEQ])
        in_maps.append(m)
    return in_maps


def kernel(L0_attn_hidden, L0_attn_w_kv_a, L0_attn_q_a_gamma, L0_attn_w_qb,
           L0_attn_kv_a_gamma, L0_attn_w_kvb, L0_attn_w_o,
           L0_ffn_gate_w, L0_ffn_w_gate, L0_ffn_w_up, L0_ffn_w_down):
    nc, bb = _get_program()
    wts = {
        "w_kv_a": np.ascontiguousarray(L0_attn_w_kv_a, np.float32),
        "w_qb": np.ascontiguousarray(L0_attn_w_qb, np.float32),
        "w_kvb": np.ascontiguousarray(L0_attn_w_kvb, np.float32),
        "w_o": np.ascontiguousarray(L0_attn_w_o, np.float32),
        "gate_w": np.ascontiguousarray(L0_ffn_gate_w, np.float32),
        "w_gate": np.ascontiguousarray(L0_ffn_w_gate, np.float32),
        "w_up": np.ascontiguousarray(L0_ffn_w_up, np.float32),
        "w_down": np.ascontiguousarray(L0_ffn_w_down, np.float32),
        "gamma_qb": np.ascontiguousarray(np.broadcast_to(
            np.asarray(L0_attn_q_a_gamma, np.float32), (128, QL))),
        "gamma_kvb": np.ascontiguousarray(np.broadcast_to(
            np.asarray(L0_attn_kv_a_gamma, np.float32), (128, KL))),
    }
    x = np.ascontiguousarray(L0_attn_hidden, np.float32)
    in_maps = make_in_maps(x, wts)
    res = run_bass_kernel_spmd(nc, in_maps, core_ids=list(range(NCORE)))
    out = np.concatenate([r["out"].reshape(SEQ, S, HD) for r in res.results], axis=0)
    return out


# version 16
# speedup vs baseline: 1.0938x; 1.0938x over previous
import math
import numpy as np
import concourse.bass as bass
import concourse.mybir as mybir
from concourse.bass import IndirectOffsetOnAxis
from concourse.bass_utils import run_bass_kernel_spmd

F32 = mybir.dt.float32
BF16 = mybir.dt.bfloat16
U32 = mybir.dt.uint32
AF = mybir.ActivationFunctionType
ALU = mybir.AluOpType
AX = mybir.AxisListType

NCORE = 8
SEQ = 2
S = 2048
HD = 2048
NH = 16
DN = 128
DR = 64
DV = 128
QL = 1536
KL = 512
EXP = 32
INTER = 768
CAPL = 1280
NCH = CAPL // 128
NSLOT = EXP * CAPL
TLOC = SEQ * S
NTT = TLOC // 128
NQ = NH * (DN + DR)
NKV = NH * (DN + DV)
FU = QL + KL + DR
SCALE = 1.0 / math.sqrt(DN + DR)
NEG = -3.0e38

BUILD = ("A1", "A2", "A3B", "B2", "C", "D")
TAPS = ()
DEV_SEQ = None
DEV_T = None
DEV_QB = None
DEV_G = None
DEV_E = None
DEV_D = None


def host_consts():
    c = {}
    c["ident"] = np.eye(128, dtype=np.float32)
    c["lt"] = np.triu(np.ones((128, 128), np.float32), 1)
    c["negtri"] = np.triu(np.full((128, 128), NEG, np.float32), 1)
    c["iota32"] = np.broadcast_to(np.arange(32, dtype=np.float32), (128, 32)).copy()
    c["iota_f"] = np.arange(128, dtype=np.float32).reshape(128, 1)
    c["ones_col"] = np.ones((128, 1), np.float32)
    c["ones_row"] = np.ones((1, 128), np.float32)
    inv_freq = 1.0 / (10000.0 ** (np.arange(0, DR, 2, dtype=np.float64) / DR))
    ang = np.arange(S, dtype=np.float64)[:, None] * inv_freq[None, :]
    c["cos_t"] = np.concatenate([np.cos(ang)] * 2, axis=1).astype(np.float32)
    c["sin_t"] = np.concatenate([np.sin(ang)] * 2, axis=1).astype(np.float32)
    c["cos16"] = np.ascontiguousarray(np.broadcast_to(c["cos_t"][:, None, :], (S, NH, DR)))
    c["sin16"] = np.ascontiguousarray(np.broadcast_to(c["sin_t"][:, None, :], (S, NH, DR)))
    return c


class Sem:
    __slots__ = ("h", "n")

    def __init__(self, h):
        self.h = h
        self.n = 0


class _Scope:
    def __init__(self, b):
        self.b = b
        self.handles = []

    _uid = [0]

    def alloc(self, name, shape, dtype):
        _Scope._uid[0] += 1
        g = self.b.nc.sbuf_tensor(f"{name}_{_Scope._uid[0]}", shape, dtype)
        h = g.__enter__()
        self.handles.append(g)
        return h

    def close(self):
        for g in reversed(self.handles):
            g.__exit__(None, None, None)
        self.handles = []


class Builder:
    ENGS = ("sync", "gpsimd", "scalar", "vector", "tensor")

    def __init__(self):
        nc = bass.Bass("TRN2", target_bir_lowering=False)
        self.nc = nc
        self.st = {k: [] for k in self.ENGS}
        self._semcache = {}

        self.c_pe = self.sem("c_pe")
        self.c_gp = self.sem("c_gp")
        self.c_dv = self.sem("c_dv")
        self.c_ac = self.sem("c_ac")

        self.psA = [nc.alloc_psum_tensor(f"psA{i}", [128, 512], F32) for i in range(2)]
        self.psB = [nc.alloc_psum_tensor(f"psB{i}", [128, 512], F32) for i in range(2)]
        self.psT = [nc.alloc_psum_tensor(f"psT{i}", [128, 1024], BF16) for i in range(2)]
        self.psR = nc.alloc_psum_tensor("psR", [128, 512], F32)
        self.psF = nc.alloc_psum_tensor("psF", [128, 512], F32)
        self.bankfree = {}
        self.ring = {}

    def sem(self, name):
        if name not in self._semcache:
            self._semcache[name] = Sem(self.nc.alloc_semaphore(name=name))
        return self._semcache[name]

    def all_dma_sems(self):
        return [s for nm, s in self._semcache.items() if nm.startswith("d_")]

    def wait(self, eng, sem, val):
        if sem is None or val is None or val <= 0:
            return
        self.st[eng].append(lambda E, h=sem.h, v=val: E.wait_ge(h, v))

    def waitt(self, eng, tick):
        if tick is not None:
            self.wait(eng, tick[0], tick[1])

    def waitr(self, eng, key):
        self.waitt(eng, self.ring.get(key))

    def setr(self, key, tick):
        self.ring[key] = tick

    def dma(self, eng, sem, out, in_, **kw):
        sem.n += 16
        v = sem.n
        self.st[eng].append(
            lambda E, o=out, i=in_, h=sem.h, k=dict(kw): E.dma_start(out=o, in_=i, **k).then_inc(h, 16))
        return (sem, v)

    @staticmethod
    def _breg(E, bound):
        cache = getattr(E, "_bnd_regs", None)
        if cache is None:
            cache = {}
            E._bnd_regs = cache
        if bound not in cache:
            cache[bound] = E.to_reg(bound)
        return cache[bound]

    def gather(self, sem, out, src, idx, bound, accum=False):
        sem.n += 16
        v = sem.n
        if accum:
            kw = dict(compute_op=ALU.add, oob_is_err=True)
        else:
            kw = dict(oob_is_err=False)
        self.st["gpsimd"].append(
            lambda E, o=out, s2=src, i2=idx, h=sem.h, bb=bound, k=dict(kw):
            E.indirect_dma_start(out=o, out_offset=None, in_=s2,
                                 in_offset=IndirectOffsetOnAxis(i2, 0),
                                 bounds_check=Builder._breg(E, bb), **k).then_inc(h, 16))
        return (sem, v)

    def scatter(self, sem, dst, src, idx, bound):
        sem.n += 16
        v = sem.n
        self.st["gpsimd"].append(
            lambda E, d=dst, s2=src, i2=idx, h=sem.h, bb=bound:
            E.indirect_dma_start(out=d, out_offset=IndirectOffsetOnAxis(i2, 0),
                                 in_=s2, in_offset=None,
                                 bounds_check=Builder._breg(E, bb), oob_is_err=False).then_inc(h, 16))
        return (sem, v)

    def op(self, eng, fn, tick=False):
        if eng in ("vector", "scalar", "gpsimd"):
            sem = {"vector": self.c_dv, "scalar": self.c_ac, "gpsimd": self.c_gp}[eng]
            self.wait(eng, sem, sem.n)
            sem.n += 1
            v = sem.n
            self.st[eng].append(lambda E, f=fn, h=sem.h: f(E).then_inc(h, 1))
            return (sem, v)
        if not tick:
            self.st[eng].append(fn)
            return None
        sem = self.c_pe
        sem.n += 1
        v = sem.n
        self.st[eng].append(lambda E, f=fn, h=sem.h: f(E).then_inc(h, 1))
        return (sem, v)

    def pet(self):
        return (self.c_pe, self.c_pe.n)

    def dvt(self):
        return (self.c_dv, self.c_dv.n)

    def act(self):
        return (self.c_ac, self.c_ac.n)

    def alloc(self, name, shape, dtype):
        return self.nc.alloc_sbuf_tensor(name, shape, dtype)

    def scope(self):
        return _Scope(self)

    def bank_wait(self, bank):
        self.waitt("tensor", self.bankfree.get(bank.name))

    def bank_set(self, bank, tick):
        self.bankfree[bank.name] = tick

    def fence(self, exclude=()):
        marks = [(self.c_pe, self.c_pe.n), (self.c_dv, self.c_dv.n), (self.c_ac, self.c_ac.n),
                 (self.c_gp, self.c_gp.n)]
        ex = {id(s) for s in exclude}
        marks += [(s, s.n) for s in self.all_dma_sems() if s.n > 0 and id(s) not in ex]
        for eng in self.ENGS:
            for s, v in marks:
                self.wait(eng, s, v)

    def transposes(self, items, bank, ident, dst, pre_evac_wait=None, eng="vector"):
        self.bank_wait(bank)
        n = len(items)
        for i, (src, np_, off) in enumerate(items):
            self.op("tensor",
                    lambda E, s=src, o=off, pp=bank, idn=ident, q=np_, st=(i == 0), sp=(i == n - 1):
                    E.matmul(pp[0:q, o:o + 128], s, idn, start=st, stop=sp, is_transpose=True),
                    tick=(i == n - 1))
        tpe = self.pet()
        self.waitt(eng, tpe)
        if pre_evac_wait is not None:
            self.waitt(eng, pre_evac_wait)
        q = items[0][1]
        w = items[-1][2] + 128
        if eng == "vector":
            tev = self.op(eng,
                          lambda E, d=dst, pp=bank, q2=q, w2=w:
                          E.tensor_copy(d, pp[0:q2, 0:w2].rearrange("p (a x) -> p a x", x=128)),
                          tick=True)
        else:
            tev = self.op(eng,
                          lambda E, d=dst, pp=bank, q2=q, w2=w:
                          E.activation(out=d, in_=pp[0:q2, 0:w2].rearrange("p (a x) -> p a x", x=128),
                                       func=AF.Copy),
                          tick=True)
        self.bank_set(bank, tev)
        return tev

    def mm_chain(self, bank, cols, pieces, out_off=0):
        n = len(pieces)
        for i, (lh, rh) in enumerate(pieces):
            self.op("tensor",
                    lambda E, b2=bank, l=lh, r=rh, s=(i == 0), t=(i == n - 1), c=cols, oo=out_off:
                    E.matmul(b2[:, oo:oo + c], l, r, start=s, stop=t),
                    tick=(i == n - 1))
        return self.pet()


def build_program():
    b = Builder()
    nc = b.nc
    p = b

    d_cst = b.sem("d_cst")
    d_cstg = b.sem("d_cstg")
    d_cs2 = b.sem("d_cs2")
    d_al = [b.sem(f"d_al{i}") for i in range(8)]
    d_w = b.sem("d_w")
    d_st = b.sem("d_st")
    d_z = b.sem("d_z")
    d_sA = [b.sem(f"d_sA{i}") for i in range(2)]
    d_sB = [b.sem(f"d_sB{i}") for i in range(2)]
    d_sQ = [b.sem(f"d_sQ{i}") for i in range(2)]
    d_sV = [b.sem(f"d_sV{i}") for i in range(2)]
    d_c16 = [b.sem(f"d_c16{i}") for i in range(2)]
    d_sAt = b.sem("d_sAt")
    d_sx1 = [b.sem(f"d_sx1{i}") for i in range(2)]
    d_x = [b.sem(f"d_x{i}") for i in range(2)]
    d_q = [b.sem(f"d_q{i}") for i in range(2)]
    d_v = [b.sem(f"d_v{i}") for i in range(2)]
    d_sc = [b.sem(f"d_sc{i}") for i in range(2)]
    d_g = [b.sem(f"d_g{i}") for i in range(4)]
    d_y = [b.sem(f"d_y{i}") for i in range(2)]
    d_ga = [b.sem(f"d_ga{i}") for i in range(8)]
    d_ob = [b.sem(f"d_ob{i}") for i in range(2)]
    d_wd = b.sem("d_wd")
    d_wc = [b.sem(f"d_wc{i}") for i in range(2)]

    xin = nc.dram_tensor("xin", [SEQ, S, HD], F32, kind="ExternalInput").ap()
    w_kv_a = nc.dram_tensor("w_kv_a", [HD, FU], F32, kind="ExternalInput").ap()
    w_qb = nc.dram_tensor("w_qb", [QL, NQ], F32, kind="ExternalInput").ap()
    w_kvb = nc.dram_tensor("w_kvb", [KL, NKV], F32, kind="ExternalInput").ap()
    w_o = nc.dram_tensor("w_o", [NH * DV, HD], F32, kind="ExternalInput").ap()
    gate_w = nc.dram_tensor("gate_w", [HD, EXP], F32, kind="ExternalInput").ap()
    w_gate = nc.dram_tensor("w_gate", [EXP, HD, INTER], F32, kind="ExternalInput").ap()
    w_up = nc.dram_tensor("w_up", [EXP, HD, INTER], F32, kind="ExternalInput").ap()
    w_down = nc.dram_tensor("w_down", [EXP, INTER, HD], F32, kind="ExternalInput").ap()
    gamma_qb_i = nc.dram_tensor("gamma_qb", [128, QL], F32, kind="ExternalInput").ap()
    gamma_kvb_i = nc.dram_tensor("gamma_kvb", [128, KL], F32, kind="ExternalInput").ap()
    HC = host_consts()
    cs = {k: nc.dram_tensor("c_" + k, list(v.shape), F32, kind="ExternalInput").ap()
          for k, v in HC.items()}

    OUT = nc.dram_tensor("out", [TLOC, HD], F32, kind="ExternalOutput").ap()

    qcn_hbm = nc.dram_tensor("qcn_hbm", [NTT, 128, QL], BF16, kind="Internal").ap()
    kvn_hbm = nc.dram_tensor("kvn_hbm", [NTT, 128, KL], BF16, kind="Internal").ap()
    kpe_hbm = nc.dram_tensor("kpe_hbm", [NTT, 128, DR], BF16, kind="Internal").ap()
    Q_hbm = nc.dram_tensor("q_hbm", [SEQ, S, NQ], BF16, kind="Internal").ap()
    V_hbm = nc.dram_tensor("v_hbm", [SEQ, S, NH * DV], BF16, kind="Internal").ap()
    attn_hbm = nc.dram_tensor("attn_hbm", [SEQ, S, HD], BF16, kind="Internal").ap()
    x1f_hbm = nc.dram_tensor("x1f_hbm", [NTT, 128, HD], F32, kind="Internal").ap()
    x1b_hbm = nc.dram_tensor("x1b_hbm", [TLOC, HD], BF16, kind="Internal").ap()
    dtw_hbm = nc.dram_tensor("dtw_hbm", [NSLOT, 2], F32, kind="Internal").ap()
    y_hbm = nc.dram_tensor("y_hbm", [NSLOT, HD], BF16, kind="Internal").ap()

    taps = {}

    def tapt(name, shape, dtype):
        if name in TAPS:
            taps[name] = nc.dram_tensor("tap_" + name, shape, dtype, kind="ExternalOutput").ap()
            return taps[name]
        return None

    ident_bf = b.alloc("ident_bf", [128, 128], BF16)
    ident_f = b.alloc("ident_f", [128, 128], F32)
    lt_f = b.alloc("lt_f", [128, 128], F32)
    negtri = b.alloc("negtri", [128, 128], F32)
    iota32 = b.alloc("iota32", [128, 32], F32)
    iota_f = b.alloc("iota_f", [128, 1], F32)
    ones_col = b.alloc("ones_col", [128, 1], F32)
    ones_row = b.alloc("ones_row", [1, 128], F32)

    w8_all = b.alloc("w8_all", [128, NTT, 8], F32)
    slot_all = b.alloc("slot_all", [128, NTT, 8], U32)
    cnt_base = b.alloc("cnt_base", [1, 32], F32)
    poisv = b.alloc("poisv", [1, 4], F32)
    scG = b.scope()
    cos_sb = scG.alloc("cos_sb", [128, 16, DR], F32)
    sin_sb = scG.alloc("sin_sb", [128, 16, DR], F32)
    gq = scG.alloc("gq", [128, QL], F32)
    gkv = scG.alloc("gkv", [128, KL], F32)

    p.dma("sync", d_cst, ident_f[:], cs["ident"])
    p.dma("gpsimd", d_cstg, ident_bf[:], cs["ident"])
    p.dma("sync", d_cst, lt_f[:], cs["lt"])
    p.dma("sync", d_cst, negtri[:], cs["negtri"])
    p.dma("sync", d_cst, iota32[:], cs["iota32"])
    p.dma("sync", d_cst, iota_f[:], cs["iota_f"])
    p.dma("sync", d_cst, ones_col[:], cs["ones_col"])
    p.dma("sync", d_cst, ones_row[:], cs["ones_row"])
    p.dma("sync", d_cst, cos_sb[:], cs["cos_t"].rearrange("(t p) d -> p t d", p=128))
    p.dma("sync", d_cst, sin_sb[:], cs["sin_t"].rearrange("(t p) d -> p t d", p=128))
    p.dma("sync", d_cst, gq[:], gamma_qb_i)
    p.dma("sync", d_cst, gkv[:], gamma_kvb_i)
    CST = (d_cst, d_cst.n)

    p.op("vector", lambda E: E.memset(cnt_base[:], 0.0))
    sc0 = b.scope()
    zer = sc0.alloc("zer", [128, 640], F32)
    tz = p.op("vector", lambda E: E.memset(zer[:], 0.0), tick=True)
    p.waitt("sync", tz)
    p.dma("sync", d_z, dtw_hbm.rearrange("(p j) c -> p (j c)", p=128), zer[:])

    if "A1" in BUILD:
        sc = b.scope()
        wkva = sc.alloc("wkva", [128, 16, FU], BF16)
        xt = sc.alloc("xt", [128, 2, HD], F32)
        xbf = sc.alloc("xbf", [128, 2, HD], BF16)
        xfm = sc.alloc("xfm", [128, 2, 16, 128], BF16)
        fus = sc.alloc("fus", [128, 2, FU], F32)
        sqs = sc.alloc("sqs", [128, QL], F32)
        qcn = sc.alloc("qcn", [128, 2, QL], BF16)
        qcnf = sc.alloc("qcnf", [128, 2, 12, 128], BF16)
        kvn = sc.alloc("kvn", [128, 2, KL], BF16)
        kpo = sc.alloc("kpo", [128, 2, DR], BF16)
        rsc = sc.alloc("rsc", [128, 12], F32)
        rps = sc.alloc("rps", [128, 2 * DR], F32)

        tw = p.dma("gpsimd", d_w, wkva[:], w_kv_a.rearrange("(c p) n -> p c n", p=128))
        p.waitt("tensor", tw)
        p.wait("tensor", d_cstg, 16)
        p.waitt("tensor", CST)
        p.waitt("vector", CST)
        p.waitt("scalar", CST)

        NCHUNKS = [(0, 512), (512, 1024), (1024, 1536), (1536, 2048), (2048, FU)]
        for g in range(NTT if DEV_G is None else DEV_G):
            s, t = divmod(g, 16)
            a = g % 2
            p.waitr("sync", f"xt{a}")
            tx = p.dma("sync", d_x[a], xt[:, a, :], xin[s, t * 128:(t + 1) * 128, :])
            p.waitt("vector", tx)
            p.waitr("vector", f"xbf{a}")
            t_cast = p.op("vector", lambda E, aa=a: E.tensor_copy(xbf[:, aa, :], xt[:, aa, :]), tick=True)
            b.setr(f"xt{a}", t_cast)
            p.waitt("tensor", t_cast)
            for grp in range(2):
                items = [(xbf[:, a, (grp * 8 + j) * 128:(grp * 8 + j + 1) * 128], 128, j * 128)
                         for j in range(8)]
                tev = b.transposes(items, b.psT[grp], ident_bf[:], xfm[:, a, grp * 8:grp * 8 + 8, :],
                                   pre_evac_wait=b.ring.get(f"xfm{a}"))
            b.setr(f"xbf{a}", b.pet())
            t_xfm = tev
            p.waitt("tensor", t_xfm)
            for ci, (n0, n1) in enumerate(NCHUNKS):
                bank = b.psA[ci % 2]
                b.bank_wait(bank)
                tmm = b.mm_chain(bank, n1 - n0,
                                 [(xfm[:, a, k, :], wkva[:, k, n0:n1]) for k in range(16)])
                eng = "vector" if ci % 2 == 0 else "scalar"
                p.waitt(eng, tmm)
                if eng == "vector":
                    tev = p.op(eng, lambda E, aa=a, m0=n0, m1=n1, bk=bank:
                               E.tensor_copy(fus[:, aa, m0:m1], bk[:, 0:m1 - m0]), tick=True)
                else:
                    tev = p.op(eng, lambda E, aa=a, m0=n0, m1=n1, bk=bank:
                               E.activation(out=fus[:, aa, m0:m1], in_=bk[:, 0:m1 - m0], func=AF.Copy), tick=True)
                b.bank_set(bank, tev)
            b.setr(f"xfm{a}", b.pet())
            t_fusv, t_fusa = b.dvt(), b.act()
            p.waitt("scalar", t_fusv)
            p.op("scalar", lambda E, aa=a: E.activation(out=sqs[:, 0:QL], in_=fus[:, aa, 0:QL],
                                                        func=AF.Square, accum_out=rsc[:, 0:1]))
            t_s2 = p.op("scalar", lambda E, aa=a: E.activation(out=sqs[:, 0:KL], in_=fus[:, aa, QL:QL + KL],
                                                               func=AF.Square, accum_out=rsc[:, 1:2]), tick=True)
            p.waitt("vector", t_s2)
            p.op("vector", lambda E: E.tensor_scalar(out=rsc[:, 2:3], in0=rsc[:, 0:1], scalar1=1.0 / QL,
                                                     scalar2=1e-6, op0=ALU.mult, op1=ALU.add))
            t_pre = p.op("vector", lambda E: E.tensor_scalar(out=rsc[:, 3:4], in0=rsc[:, 1:2], scalar1=1.0 / KL,
                                                             scalar2=1e-6, op0=ALU.mult, op1=ALU.add), tick=True)
            p.waitt("scalar", t_pre)
            t_sq = p.op("scalar", lambda E: E.activation(out=rsc[:, 4:6], in_=rsc[:, 2:4], func=AF.Sqrt), tick=True)
            p.waitt("vector", t_sq)
            p.op("vector", lambda E: E.reciprocal(rsc[:, 6:7], rsc[:, 4:5]))
            p.op("vector", lambda E: E.reciprocal(rsc[:, 7:8], rsc[:, 5:6]))
            p.waitr("vector", f"qcn{a}")
            p.waitr("vector", f"kvn{a}")
            p.op("vector", lambda E, aa=a: E.scalar_tensor_tensor(
                out=qcn[:, aa, :], in0=fus[:, aa, 0:QL], scalar=rsc[:, 6:7], in1=gq[:],
                op0=ALU.mult, op1=ALU.mult))
            t_kvn = p.op("vector", lambda E, aa=a: E.scalar_tensor_tensor(
                out=kvn[:, aa, :], in0=fus[:, aa, QL:QL + KL], scalar=rsc[:, 7:8], in1=gkv[:],
                op0=ALU.mult, op1=ALU.mult), tick=True)
            kp = fus[:, a, QL + KL:FU]
            cv = cos_sb[:, t, :]
            sv = sin_sb[:, t, :]
            p.op("vector", lambda E, x=kp, c2=cv: E.tensor_mul(rps[:, 0:DR], x, c2))
            p.op("vector", lambda E, x=kp, s2=sv: E.tensor_mul(rps[:, DR:DR + 32], x[:, 32:64], s2[:, 0:32]))
            p.op("vector", lambda E, x=kp, s2=sv: E.tensor_mul(rps[:, DR + 32:DR + 64], x[:, 0:32], s2[:, 32:64]))
            p.op("vector", lambda E, aa=a: E.tensor_sub(kpo[:, aa, 0:32], rps[:, 0:32], rps[:, DR:DR + 32]))
            t_kpo = p.op("vector", lambda E, aa=a: E.tensor_add(kpo[:, aa, 32:64], rps[:, 32:64],
                                                                rps[:, DR + 32:DR + 64]), tick=True)
            p.waitt("tensor", t_kvn)
            for grp, ntr in ((0, 8), (1, 4)):
                items = [(qcn[:, a, (grp * 8 + j) * 128:(grp * 8 + j + 1) * 128], 128, j * 128)
                         for j in range(ntr)]
                tev = b.transposes(items, b.psT[grp], ident_bf[:],
                                   qcnf[:, a, grp * 8:grp * 8 + ntr, :],
                                   pre_evac_wait=b.ring.get(f"qcnf{a}"))
            b.setr(f"qcn{a}", b.pet())
            t_qf = tev
            p.waitt("sync", t_qf)
            p.waitt("sync", t_kpo)
            tst = p.dma("sync", d_sA[a], qcn_hbm[g], qcnf[:, a, :, :])
            b.setr(f"qcnf{a}", tst)
            p.dma("sync", d_sB[a], kvn_hbm[g], kvn[:, a, :])
            tst = p.dma("sync", d_sB[a], kpe_hbm[g], kpo[:, a, :])
            b.setr(f"kvn{a}", tst)

        tp = tapt("fused31", [128, FU], F32)
        if tp is not None:
            p.dma("sync", d_st, tp, fus[:, 1, :])
        tp = tapt("qcn31", [128, QL], BF16)
        if tp is not None:
            p.dma("sync", d_st, tp, qcn[:, 1, :])
        b.fence()
        sc.close()
        b.ring.clear()

    if "A2" in BUILD:
        sc = b.scope()
        wqb = sc.alloc("wqb", [128, 12, NQ], BF16)
        qti = sc.alloc("qti", [128, 2, QL], BF16)
        qf32 = sc.alloc("qf32", [128, 2, NQ], F32)
        qbf = sc.alloc("qbf", [128, 2, NQ], BF16)
        rp2 = sc.alloc("rp2", [128, NH, 2 * DR], F32)
        c16 = sc.alloc("c16", [128, 2, 2, NH, DR], F32)

        tw = p.dma("gpsimd", d_w, wqb[:], w_qb.rearrange("(c p) n -> p c n", p=128))
        p.waitt("tensor", tw)

        for g in range(NTT if DEV_G is None else DEV_G):
            s, t = divmod(g, 16)
            a = g % 2
            p.waitr("sync", f"qti{a}")
            tq = p.dma("sync", d_q[a], qti[:, a, :], qcn_hbm[g])
            p.waitr("scalar", f"c16{a}")
            p.dma("sync", d_c16[a], c16[:, a, 0], cs["cos16"].rearrange("(t p) h d -> t p h d", p=128)[t])
            tc16 = p.dma("sync", d_c16[a], c16[:, a, 1], cs["sin16"].rearrange("(t p) h d -> t p h d", p=128)[t])
            p.waitt("tensor", tq)
            p.waitr("tensor", f"qf{a}")
            for ci in range(6):
                n0 = ci * 512
                bank = b.psA[ci % 2]
                b.bank_wait(bank)
                tmm = b.mm_chain(bank, 512,
                                 [(qti[:, a, k * 128:(k + 1) * 128], wqb[:, k, n0:n0 + 512])
                                  for k in range(12)])
                eng = "vector" if ci % 2 == 0 else "scalar"
                p.waitt(eng, tmm)
                if eng == "vector":
                    tev = p.op(eng, lambda E, aa=a, m0=n0, bk=bank:
                               E.tensor_copy(qf32[:, aa, m0:m0 + 512], bk[:]), tick=True)
                else:
                    tev = p.op(eng, lambda E, aa=a, m0=n0, bk=bank:
                               E.activation(out=qf32[:, aa, m0:m0 + 512], in_=bk[:], func=AF.Copy), tick=True)
                b.bank_set(bank, tev)
            b.setr(f"qti{a}", b.pet())
            p.waitt("vector", b.act())
            p.waitt("vector", tc16)
            xpe = qf32[:, a, :].rearrange("p (h d) -> p h d", d=DN + DR)[:, :, DN:DN + DR]
            cv16 = c16[:, a, 0]
            sv16 = c16[:, a, 1]
            p.op("vector", lambda E, x=xpe, c2=cv16: E.tensor_tensor(rp2[:, :, 0:DR], x, c2, ALU.mult))
            p.op("vector", lambda E, x=xpe, s2=sv16:
                 E.tensor_tensor(rp2[:, :, DR:DR + 32], x[:, :, 32:64], s2[:, :, 0:32], ALU.mult))
            p.op("vector", lambda E, x=xpe, s2=sv16:
                 E.tensor_tensor(rp2[:, :, DR + 32:DR + 64], x[:, :, 0:32], s2[:, :, 32:64], ALU.mult))
            p.op("vector", lambda E, x=xpe: E.tensor_tensor(x[:, :, 0:32], rp2[:, :, 0:DR][:, :, 0:32],
                                                            rp2[:, :, DR:DR + 32], ALU.subtract))
            tr5 = p.op("vector", lambda E, x=xpe: E.tensor_tensor(x[:, :, 32:64], rp2[:, :, 32:64],
                                                                  rp2[:, :, DR + 32:DR + 64], ALU.add), tick=True)
            b.setr(f"c16{a}", tr5)
            p.waitr("vector", f"qbf{a}")
            t_rope = p.op("vector", lambda E, aa=a: E.tensor_scalar(
                out=qbf[:, aa, :], in0=qf32[:, aa, :], scalar1=SCALE, scalar2=None, op0=ALU.mult), tick=True)
            b.setr(f"qf{a}", t_rope)
            p.waitt("sync", t_rope)
            tst = p.dma("sync", d_sQ[a], Q_hbm[s, t * 128:(t + 1) * 128, :], qbf[:, a, :])
            b.setr(f"qbf{a}", tst)

        tp = tapt("q31", [128, NQ], BF16)
        if tp is not None:
            p.dma("sync", d_st, tp, qbf[:, 1, :])
        b.fence()
        sc.close()
        b.ring.clear()

    sc0.close()
    scG.close()

    if "A3B" in BUILD or "A3" in BUILD or "B" in BUILD:
        DO_B = ("A3B" in BUILD) or ("B" in BUILD)
        scK = b.scope()
        KN = scK.alloc("KN", [128, NH, S], BF16)
        KP = scK.alloc("KP", [128, S], BF16)
        VS = scK.alloc("VS", [128, S // 128, NH * DV], BF16)

        def _seq_body(s):
            sc = b.scope()
            wkvb = sc.alloc("wkvb", [128, 4, NKV], BF16)
            kvi = sc.alloc("kvi", [128, 2, KL], BF16)
            kpi = sc.alloc("kpi", [128, 2, DR], BF16)
            kvf = sc.alloc("kvf", [128, 2, 4, 128], BF16)
            knb = sc.alloc("knb", [128, 2, NH, DN], BF16)

            t_wkvb = p.dma("gpsimd", d_w, wkvb[:], w_kvb.rearrange("(c p) n -> p c n", p=128))
            p.waitt("tensor", t_wkvb)
            for t in range(16 if DEV_T is None else DEV_T):
                g = s * 16 + t
                a = t % 2
                p.waitr("sync", f"kvi{a}")
                p.dma("sync", d_q[a], kvi[:, a, :], kvn_hbm[g])
                tk2 = p.dma("sync", d_q[a], kpi[:, a, :], kpe_hbm[g])
                p.waitt("tensor", tk2)
                items = [(kvi[:, a, j * 128:(j + 1) * 128], 128, j * 128) for j in range(4)]
                tev = b.transposes(items, b.psT[0], ident_bf[:], kvf[:, a, :, :],
                                   pre_evac_wait=b.ring.get(f"kvf{a}"))
                p.waitt("tensor", tev)
                p.waitr("tensor", f"knb{a}")
                for ci in range(8):
                    n0 = ci * 512
                    bank = b.psA[ci % 2]
                    b.bank_wait(bank)
                    tmm = b.mm_chain(bank, 512,
                                     [(kvf[:, a, k, :], wkvb[:, k, n0:n0 + 512]) for k in range(4)])
                    eng = "vector" if ci % 2 == 0 else "scalar"
                    p.waitt(eng, tmm)
                    pvw = bank[:].rearrange("p (h c) -> p h c", c=256)
                    vdst = VS[:, t, :].rearrange("p (h x) -> p h x", x=128)[:, 2 * ci:2 * ci + 2, :]
                    if eng == "vector":
                        p.op(eng, lambda E, aa=a, h0=2 * ci, q=pvw:
                             E.tensor_copy(knb[:, aa, h0:h0 + 2, :], q[:, :, 0:128]))
                        tev = p.op(eng, lambda E, q=pvw, vd=vdst:
                                   E.tensor_copy(vd, q[:, :, 128:256]), tick=True)
                    else:
                        p.op(eng, lambda E, aa=a, h0=2 * ci, q=pvw:
                             E.activation(out=knb[:, aa, h0:h0 + 2, :], in_=q[:, :, 0:128], func=AF.Copy))
                        tev = p.op(eng, lambda E, q=pvw, vd=vdst:
                                   E.activation(out=vd, in_=q[:, :, 128:256], func=AF.Copy),
                                   tick=True)
                    b.bank_set(bank, tev)
                b.setr(f"kvf{a}", b.pet())
                t_ev_v, t_ev_a = b.dvt(), b.act()
                p.waitt("tensor", t_ev_v)
                p.waitt("tensor", t_ev_a)
                for grp in range(2):
                    items = [(knb[:, a, grp * 8 + j, :], 128, j * 128) for j in range(8)]
                    dst = KN[:, grp * 8:grp * 8 + 8, t * 128:(t + 1) * 128]
                    b.transposes(items, b.psT[grp], ident_bf[:], dst)
                b.setr(f"knb{a}", b.pet())
                b.bank_wait(b.psT[0])
                p.op("tensor", lambda E, aa=a: E.transpose(b.psT[0][0:64, 0:128], kpi[:, aa, :], ident_bf[:]),
                     tick=True)
                tpe = b.pet()
                b.setr(f"kvi{a}", tpe)
                p.waitt("vector", tpe)
                tev = p.op("vector", lambda E, tt=t: E.tensor_copy(KP[0:64, tt * 128:(tt + 1) * 128],
                                                                   b.psT[0][0:64, 0:128]), tick=True)
                b.bank_set(b.psT[0], tev)
            b.fence()
            sc.close()

            if not DO_B:
                return
            sc = b.scope()
            q_in = sc.alloc("q_in", [128, 2, NQ], BF16)
            Qn = sc.alloc("Qn", [128, 32, 128], BF16)
            Qp = sc.alloc("Qp", [128, 32, 128], BF16)
            sc_f = sc.alloc("sc_f", [128, 2, 2, S], BF16)
            P_sb = sc.alloc("P_sb", [128, 4, 512], BF16)
            PT_sb = sc.alloc("PT", [128, 2, 4, 128], BF16)
            mtl = sc.alloc("mtl", [128, 2, 8], F32)
            nmw = sc.alloc("nmw", [128, 2, 2], F32)
            rs8 = sc.alloc("rs8", [128, 2, 8], F32)
            lsr = sc.alloc("lsr", [128, 2, 4], F32)
            atb = sc.alloc("atb", [128, 2, HD], BF16)

            nm_t = {}
            NHW = 2
            NW = 16 // NHW * 2

            def pass1(w, qb):
                u = w % 2
                qt = w // (NW // 2)
                hp = w % (NW // 2)
                q0 = qb * 256 + qt * 128
                nkb = (q0 + 128 + 511) // 512
                sbanks = [b.psR, b.psF, b.psB[0], b.psB[1]]
                for hi in range(NHW):
                    h = hp * NHW + hi
                    hq = h * 2 + qt
                    for kb in range(nkb):
                        ncols = min(512, q0 + 128 - kb * 512)
                        full = (kb * 512 + 512 <= q0 + 1)
                        bank = sbanks[(hi * nkb + kb) % 4]
                        b.bank_wait(bank)
                        p.op("tensor", lambda E, bk=bank, q2=hq, c=ncols, k0=kb * 512:
                             E.matmul(bk[:, 0:c], Qn[:, q2, :], KN[:, q2 // 2, k0:k0 + c],
                                      start=True, stop=False))
                        tsc2 = p.op("tensor", lambda E, bk=bank, q2=hq, c=ncols, k0=kb * 512:
                                    E.matmul(bk[:, 0:c], Qp[0:64, q2, :], KP[0:64, k0:k0 + c],
                                             start=False, stop=True), tick=True)
                        p.waitt("vector", tsc2)
                        if hi == 0 and kb == 0:
                            p.waitr("vector", f"scf{u}")
                        if not full:
                            dlt = ncols - 128
                            p.op("vector", lambda E, bk=bank, d2=dlt:
                                 E.tensor_add(bk[:, d2:d2 + 128], bk[:, d2:d2 + 128], negtri[:]))
                        tev = p.op("vector", lambda E, bk=bank, uu=u, c=ncols, k0=kb * 512, h2=hi, kk=kb:
                                   E.tensor_scalar(out=sc_f[:, uu, h2, k0:k0 + c], in0=bk[:, 0:c],
                                                   scalar1=1.0, scalar2=NEG,
                                                   op0=ALU.mult, op1=ALU.max,
                                                   accum_out=mtl[:, uu, h2 * 4 + kk:h2 * 4 + kk + 1]),
                                   tick=True)
                        b.bank_set(bank, tev)
                p.op("vector", lambda E, uu=u, nk=nkb:
                     E.reduce_max(nmw[:, uu, :], mtl[:, uu, :].rearrange("p (h k) -> p h k", k=4)[:, :, 0:nk],
                                  axis=AX.X, negate=True))
                nm_t[w] = b.dvt()

            def pass2(w, qb):
                u = w % 2
                qt = w // (NW // 2)
                hp = w % (NW // 2)
                q0 = qb * 256 + qt * 128
                nkb = (q0 + 128 + 511) // 512
                p.waitt("scalar", nm_t[w])
                vbanks = [b.psA[0], b.psA[1]]

                def pv(hi, kb, u2):
                    h = hp * NHW + hi
                    bank2 = vbanks[hi % 2]
                    ncols = min(512, q0 + 128 - kb * 512)
                    nsub = ncols // 128
                    if kb == 0:
                        b.bank_wait(bank2)
                    p.waitt("tensor", pt_t[u2])
                    for sub in range(nsub):
                        st2 = (kb == 0 and sub == 0)
                        sp2 = (kb == nkb - 1 and sub == nsub - 1)
                        p.op("tensor", lambda E, bk=bank2, u3=u2, s3=sub, h2=h, kt=kb * 4 + sub,
                             st3=st2, sp3=sp2:
                             E.matmul(bk[:, 0:128], PT_sb[:, u3, s3, :],
                                      VS[:, kt, h2 * DV:(h2 + 1) * DV], start=st3, stop=sp3),
                             tick=sp2)
                    b.setr(f"PT{u2}", b.pet())

                units = [(hi, kb) for hi in range(NHW) for kb in range(nkb)]
                pt_t = {}
                t_exp = {}

                def emit_exp(idx):
                    hi, kb = units[idx]
                    ncols = min(512, q0 + 128 - kb * 512)
                    u4 = idx % 4
                    p.waitr("scalar", f"P{u4}")
                    t_exp[idx] = p.op(
                        "scalar", lambda E, uu=u, u3=u4, c=ncols, k0=kb * 512, h2=hi, kk=kb:
                        E.activation(out=P_sb[:, u3, 0:c], in_=sc_f[:, uu, h2, k0:k0 + c],
                                     func=AF.Exp, bias=nmw[:, uu, h2:h2 + 1],
                                     accum_out=rs8[:, uu, h2 * 4 + kk:h2 * 4 + kk + 1]), tick=True)

                for i2 in range(min(2, len(units))):
                    emit_exp(i2)
                prev = None
                for idx, (hi, kb) in enumerate(units):
                    ncols = min(512, q0 + 128 - kb * 512)
                    nsub = ncols // 128
                    u2 = idx % 2
                    u4 = idx % 4
                    items = [(P_sb[:, u4, j * 128:(j + 1) * 128], 128, j * 128) for j in range(nsub)]
                    p.waitt("tensor", t_exp[idx])
                    pt_t[u2] = b.transposes(items, b.psT[u2], ident_bf[:], PT_sb[:, u2, 0:nsub, :],
                                            pre_evac_wait=b.ring.get(f"PT{u2}"),
                                            eng="vector" if u2 == 0 else "scalar")
                    b.setr(f"P{u4}", b.pet())
                    if idx + 2 < len(units):
                        emit_exp(idx + 2)
                    if prev is not None:
                        pv(*prev)
                    prev = (hi, kb, u2)
                pv(*prev)
                tpv = b.pet()
                p.waitt("vector", tpv)
                p.op("vector", lambda E, uu=u, nk=nkb:
                     E.reduce_sum(lsr[:, uu, 0:NHW], rs8[:, uu, :].rearrange("p (h k) -> p h k", k=4)[:, :, 0:nk],
                                  axis=AX.X))
                p.op("vector", lambda E, uu=u: E.reciprocal(lsr[:, uu, NHW:2 * NHW], lsr[:, uu, 0:NHW]))
                p.waitr("vector", "atb")
                for hi in range(NHW):
                    h = hp * NHW + hi
                    tat = p.op("vector", lambda E, uu=u, qt2=qt, h2=h, h3=hi, bk=vbanks[hi % 2]:
                               E.tensor_scalar(out=atb[:, qt2, h2 * DV:(h2 + 1) * DV], in0=bk[:, 0:128],
                                               scalar1=lsr[:, uu, NHW + h3:NHW + h3 + 1], scalar2=None,
                                               op0=ALU.mult), tick=True)
                    b.bank_set(vbanks[hi % 2], tat)
                b.setr(f"scf{u}", b.act())
                return tat

            for qb in range(8 if DEV_QB is None else DEV_QB):
                q0b = qb * 256
                p.waitr("sync", "q_in")
                tq = p.dma("sync", d_q[qb % 2], q_in[:],
                           Q_hbm[s, q0b:q0b + 256, :].rearrange("(a p) n -> p a n", p=128))
                p.waitt("tensor", tq)
                for grp in range(4):
                    items = []
                    for i in range(8):
                        hq = grp * 8 + i
                        h, qt = divmod(hq, 2)
                        items.append((q_in[:, qt, h * (DN + DR):h * (DN + DR) + DN], 128, i * 128))
                    b.transposes(items, b.psT[grp % 2], ident_bf[:], Qn[:, grp * 8:grp * 8 + 8, :],
                                 pre_evac_wait=b.ring.get("Qfm"))
                for grp in range(4):
                    items = []
                    for i in range(8):
                        hq = grp * 8 + i
                        h, qt = divmod(hq, 2)
                        items.append((q_in[:, qt, h * (DN + DR) + DN:h * (DN + DR) + DN + DR], 64, i * 128))
                    b.transposes(items, b.psT[grp % 2], ident_bf[:], Qp[0:64, grp * 8:grp * 8 + 8, :],
                                 pre_evac_wait=b.ring.get("Qfm"))
                b.setr("q_in", b.pet())
                for w in range(NW):
                    pass1(w, qb)
                    if w >= 1:
                        pass2(w - 1, qb)
                t_last = pass2(NW - 1, qb)
                p.waitt("sync", b.dvt())
                p.waitr("sync", "atb")
                tst = p.dma("sync", d_sAt,
                            attn_hbm[s, q0b:q0b + 256, :].rearrange("(a p) n -> p a n", p=128), atb[:])
                b.setr("atb", tst)
                b.setr("Qfm", b.pet())
            tp = tapt(f"atb_s{s}", [128, 2, HD], BF16)
            if tp is not None:
                p.dma("sync", d_st, tp, atb[:])
            b.fence()
            sc.close()
            b.ring.clear()
        for s in range(SEQ if DEV_SEQ is None else DEV_SEQ):
            _seq_body(s)
        scK.close()

    if "B2" in BUILD:
        b.fence()
        sc = b.scope()
        wo = sc.alloc("wo", [128, 16, HD], BF16)
        ati = sc.alloc("ati", [128, 2, HD], BF16)
        afm = sc.alloc("afm", [128, 2, 16, 128], BF16)
        xr = sc.alloc("xr", [128, 2, HD], F32)
        x1 = sc.alloc("x1", [128, 2, HD], F32)
        x1b = sc.alloc("x1b", [128, 2, HD], BF16)
        x1fm = sc.alloc("x1fm", [128, 2, 16, 128], F32)
        sq2 = sc.alloc("sq2", [128, HD], F32)
        gwsb = sc.alloc("gwsb", [128, 16, EXP], F32)
        lg = sc.alloc("lg", [128, 3, EXP], F32)
        lgr = sc.alloc("lgr", [128, EXP], F32)
        I8 = sc.alloc("I8", [128, 3, EXP], F32)
        Rm = sc.alloc("Rm", [128, EXP], F32)
        smap = sc.alloc("smap", [128, EXP], F32)
        pmap = sc.alloc("pmap", [128, EXP], F32)
        dmap = sc.alloc("dmap", [128, 2, EXP], F32)
        tv8 = sc.alloc("tv8", [128, 3, 8], F32)
        ti8 = sc.alloc("ti8", [128, 8], U32)
        ex8 = sc.alloc("ex8", [128, 8], F32)
        oh = sc.alloc("oh", [128, EXP], F32)
        csc = sc.alloc("csc", [128, EXP], F32)
        ohf = sc.alloc("ohf", [128, EXP], F32)
        slotf = sc.alloc("slotf", [128, 8], F32)
        posf = sc.alloc("posf", [128, 8], F32)
        posu = sc.alloc("posu", [128, 2, 8], U32)
        pv = sc.alloc("pv", [128, 2, 8, 2], F32)
        tokf = sc.alloc("tokf", [128, 1], F32)
        sm8 = sc.alloc("sm8", [128, 4], F32)
        lnst = sc.alloc("lnst", [128, 12], F32)

        p.wait("gpsimd", d_z, d_z.n)
        tw = p.dma("gpsimd", d_w, wo[:], w_o.rearrange("(c p) n -> p c n", p=128))
        tgw = p.dma("sync", d_cs2, gwsb[:], gate_w.rearrange("(c p) n -> p c n", p=128))
        p.waitt("tensor", tw)
        p.waitt("tensor", tgw)

        t_i8 = {}

        def b2_stage1(g):
            s, t = divmod(g, 16)
            a = g % 2
            u3 = g % 3
            p.waitr("sync", f"ati{a}")
            ta = p.dma("sync", d_q[a], ati[:, a, :], attn_hbm[s, t * 128:(t + 1) * 128, :])
            p.waitr("sync", f"xr{a}")
            txr = p.dma("sync", d_x[a], xr[:, a, :], xin[s, t * 128:(t + 1) * 128, :])
            p.waitt("tensor", ta)
            for grp in range(2):
                items = [(ati[:, a, (grp * 8 + j) * 128:(grp * 8 + j + 1) * 128], 128, j * 128)
                         for j in range(8)]
                b.transposes(items, b.psT[grp], ident_bf[:], afm[:, a, grp * 8:grp * 8 + 8, :],
                             pre_evac_wait=b.ring.get(f"afm{a}"))
            b.setr(f"ati{a}", b.pet())
            t_afm = b.dvt()
            p.waitt("tensor", t_afm)
            p.waitt("vector", txr)
            p.waitr("vector", f"x1{a}")
            p.waitr("vector", f"x1b{a}")
            for ci in range(4):
                n0 = ci * 512
                bank = b.psA[ci % 2]
                b.bank_wait(bank)
                tmm = b.mm_chain(bank, 512,
                                 [(afm[:, a, k, :], wo[:, k, n0:n0 + 512]) for k in range(16)])
                p.waitt("vector", tmm)
                tev = p.op("vector", lambda E, aa=a, m0=n0, bk=bank, c2=ci:
                           E.scalar_tensor_tensor(out=x1[:, aa, m0:m0 + 512], in0=bk[:],
                                                  scalar=1.0, in1=xr[:, aa, m0:m0 + 512],
                                                  op0=ALU.mult, op1=ALU.add,
                                                  accum_out=lnst[:, c2:c2 + 1]), tick=True)
                b.bank_set(bank, tev)
            b.setr(f"afm{a}", b.pet())
            b.setr(f"xr{a}", b.dvt())
            p.op("vector", lambda E: E.tensor_add(lnst[:, 4:5], lnst[:, 0:1], lnst[:, 1:2]))
            p.op("vector", lambda E: E.tensor_add(lnst[:, 5:6], lnst[:, 2:3], lnst[:, 3:4]))
            t_resid = p.op("vector", lambda E: E.tensor_add(lnst[:, 6:7], lnst[:, 4:5], lnst[:, 5:6]), tick=True)
            p.waitt("scalar", t_resid)
            t_s2 = p.op("scalar", lambda E, aa=a: E.activation(out=sq2[:], in_=x1[:, aa, :],
                                                               func=AF.Square, accum_out=lnst[:, 7:8]), tick=True)
            p.op("vector", lambda E: E.tensor_scalar(out=lnst[:, 8:9], in0=lnst[:, 6:7],
                                                     scalar1=1.0 / HD, scalar2=None, op0=ALU.mult))
            p.waitt("vector", t_s2)
            p.op("vector", lambda E: E.tensor_scalar(out=lnst[:, 9:10], in0=lnst[:, 7:8],
                                                     scalar1=1.0 / HD, scalar2=None, op0=ALU.mult))
            p.op("vector", lambda E: E.tensor_mul(lnst[:, 10:11], lnst[:, 8:9], lnst[:, 8:9]))
            p.op("vector", lambda E: E.scalar_tensor_tensor(out=lnst[:, 10:11], in0=lnst[:, 10:11],
                                                            scalar=-1.0, in1=lnst[:, 9:10],
                                                            op0=ALU.mult, op1=ALU.add))
            t_ve = p.op("vector", lambda E: E.tensor_scalar(out=lnst[:, 11:12], in0=lnst[:, 10:11],
                                                            scalar1=1.0, scalar2=1e-5,
                                                            op0=ALU.mult, op1=ALU.add), tick=True)
            p.waitt("scalar", t_ve)
            t_sqr = p.op("scalar", lambda E: E.activation(out=lnst[:, 10:11], in_=lnst[:, 11:12],
                                                          func=AF.Sqrt), tick=True)
            p.waitt("vector", t_sqr)
            p.op("vector", lambda E: E.reciprocal(lnst[:, 11:12], lnst[:, 10:11]))
            p.op("vector", lambda E, aa=a: E.tensor_scalar(out=x1[:, aa, :], in0=x1[:, aa, :],
                                                           scalar1=lnst[:, 8:9], scalar2=lnst[:, 11:12],
                                                           op0=ALU.subtract, op1=ALU.mult))
            t_x1 = p.op("vector", lambda E, aa=a: E.tensor_copy(x1b[:, aa, :], x1[:, aa, :]), tick=True)
            p.waitt("sync", t_x1)
            p.dma("sync", d_sx1[a], x1f_hbm[g], x1[:, a, :])
            tst = p.dma("sync", d_sx1[a], x1b_hbm[g * 128:(g + 1) * 128, :], x1b[:, a, :])
            b.setr(f"x1b{a}", tst)
            p.waitt("tensor", t_x1)
            for grp in range(4):
                bank = b.psF if grp % 2 == 0 else b.psR
                b.bank_wait(bank)
                for i in range(4):
                    c2 = grp * 4 + i
                    p.op("tensor", lambda E, aa=a, cc=c2, o=i * 128, bk=bank, st=(i == 0), sp=(i == 3):
                         E.matmul(bk[:, o:o + 128], x1[:, aa, cc * 128:(cc + 1) * 128], ident_f[:],
                                  start=st, stop=sp, is_transpose=True),
                         tick=(i == 3))
                tpe = b.pet()
                p.waitt("vector", tpe)
                if grp < 2:
                    p.waitr("vector", f"x1fm{a}")
                tev = p.op("vector", lambda E, aa=a, g2=grp, bk=bank:
                           E.tensor_copy(x1fm[:, aa, g2 * 4:g2 * 4 + 4, :],
                                         bk[:].rearrange("p (c x) -> p c x", c=4)), tick=True)
                b.bank_set(bank, tev)
            t_fm = b.dvt()
            b.setr(f"x1{a}", b.pet())
            p.waitt("tensor", t_fm)
            bank = b.psR
            b.bank_wait(bank)
            tlg = b.mm_chain(bank, EXP, [(x1fm[:, a, k, :], gwsb[:, k, :]) for k in range(16)])
            b.setr(f"x1fm{a}", tlg)
            p.waitt("vector", tlg)
            tev = p.op("vector", lambda E, u4=u3, bk=bank: E.tensor_copy(lg[:, u4, :], bk[:, 0:EXP]), tick=True)
            b.bank_set(bank, tev)
            p.op("vector", lambda E, u4=u3: E.max(tv8[:, u4, :], lg[:, u4, :]))
            p.op("vector", lambda E, u4=u3: E.max_index(ti8[:], tv8[:, u4, :], lg[:, u4, :]))
            p.op("vector", lambda E, u4=u3: E.match_replace(lgr[:], tv8[:, u4, :], lg[:, u4, :], NEG))
            p.op("vector", lambda E, u4=u3: E.tensor_tensor(I8[:, u4, :], lg[:, u4, :], lgr[:], ALU.not_equal))
            t_nm8 = p.op("vector", lambda E, u4=u3: E.tensor_scalar(out=sm8[:, 0:1], in0=tv8[:, u4, 0:1],
                                                                    scalar1=-1.0, scalar2=None, op0=ALU.mult),
                         tick=True)
            p.waitt("scalar", t_nm8)
            t_e8 = p.op("scalar", lambda E, u4=u3: E.activation(out=ex8[:], in_=tv8[:, u4, :], func=AF.Exp,
                                                                bias=sm8[:, 0:1], accum_out=sm8[:, 1:2]), tick=True)
            p.waitt("vector", t_e8)
            p.op("vector", lambda E: E.reciprocal(sm8[:, 2:3], sm8[:, 1:2]))
            t_w8 = p.op("vector", lambda E, g2=g: E.tensor_scalar(out=w8_all[:, g2, :], in0=ex8[:],
                                                                  scalar1=sm8[:, 2:3], scalar2=None,
                                                                  op0=ALU.mult), tick=True)
            t_i8[g] = t_w8

        def b2_stage2(g):
            a = g % 2
            u3 = g % 3
            p.waitt("tensor", t_i8[g])
            bank = b.psR
            b.bank_wait(bank)
            p.op("tensor", lambda E, u4=u3, bk=bank: E.matmul(bk[:, 0:EXP], lt_f[:], I8[:, u4, :],
                                                              start=True, stop=False))
            tR = p.op("tensor", lambda E, bk=bank: E.matmul(bk[:, 0:EXP], ones_row[:], cnt_base[:],
                                                            start=False, stop=True), tick=True)
            bank2 = b.psF
            b.bank_wait(bank2)
            tTot = p.op("tensor", lambda E, u4=u3, bk2=bank2: E.matmul(bk2[0:1, 0:EXP], ones_col[:],
                                                                       I8[:, u4, :], start=True, stop=True),
                        tick=True)
            p.waitt("vector", tR)
            tev = p.op("vector", lambda E, bk=bank: E.tensor_copy(Rm[:], bk[:, 0:EXP]), tick=True)
            b.bank_set(bank, tev)
            p.waitt("vector", tTot)
            t_cnt = p.op("vector", lambda E, bk2=bank2: E.tensor_add(cnt_base[:], cnt_base[:],
                                                                     bk2[0:1, 0:EXP]), tick=True)
            b.bank_set(bank2, t_cnt)
            p.waitt("tensor", t_cnt)
            p.op("vector", lambda E: E.scalar_tensor_tensor(out=smap[:], in0=iota32[:], scalar=float(CAPL),
                                                            in1=Rm[:], op0=ALU.mult, op1=ALU.add))
            p.op("vector", lambda E: E.memset(dmap[:, 0, :], 0.0))
            for j in range(1, NCH):
                src2 = dmap[:, (j - 1) % 2, :]
                dst2 = dmap[:, j % 2, :]
                p.op("vector", lambda E, s2=src2, d2=dst2, jj=j:
                     E.scalar_tensor_tensor(out=d2, in0=Rm[:], scalar=float(jj * 128), in1=s2,
                                            op0=ALU.is_ge, op1=ALU.add))
            dfin = dmap[:, (NCH - 1) % 2, :]
            p.op("vector", lambda E, d2=dfin: E.scalar_tensor_tensor(out=pmap[:], in0=d2, scalar=-128.0,
                                                                     in1=Rm[:], op0=ALU.mult, op1=ALU.add))
            p.op("vector", lambda E, d2=dfin: E.scalar_tensor_tensor(out=oh[:], in0=iota32[:], scalar=float(NCH),
                                                                     in1=d2, op0=ALU.mult, op1=ALU.add))
            p.op("vector", lambda E: E.scalar_tensor_tensor(out=pmap[:], in0=pmap[:], scalar=float(EXP * NCH),
                                                            in1=oh[:], op0=ALU.mult, op1=ALU.add))
            p.op("vector", lambda E, g2=g: E.tensor_scalar(out=tokf[:], in0=iota_f[:], scalar1=float(g2 * 128),
                                                           scalar2=None, op0=ALU.add))
            for k in range(8):
                p.op("vector", lambda E, u4=u3, kk=k: E.tensor_scalar(out=oh[:], in0=lg[:, u4, :],
                                                                      scalar1=tv8[:, u4, kk:kk + 1],
                                                                      scalar2=None, op0=ALU.is_equal))
                p.op("vector", lambda E: E.tensor_tensor_scan(csc[:], oh[:], oh[:], 0.0, ALU.add, ALU.bypass))
                p.op("vector", lambda E: E.scalar_tensor_tensor(out=ohf[:], in0=csc[:], scalar=1.0,
                                                                in1=oh[:], op0=ALU.is_equal, op1=ALU.mult))
                p.op("vector", lambda E: E.tensor_mul(csc[:], ohf[:], smap[:]))
                p.op("vector", lambda E, kk=k: E.reduce_sum(slotf[:, kk:kk + 1], csc[:], axis=AX.X))
                p.op("vector", lambda E: E.tensor_mul(csc[:], ohf[:], pmap[:]))
                p.op("vector", lambda E, kk=k: E.reduce_sum(posf[:, kk:kk + 1], csc[:], axis=AX.X))
            p.op("vector", lambda E, g2=g: E.tensor_copy(slot_all[:, g2, :], slotf[:]))
            p.waitr("vector", f"pv{a}")
            p.op("vector", lambda E, aa=a: E.tensor_copy(posu[:, aa, :], posf[:]))
            for k in range(8):
                p.op("vector", lambda E, aa=a, kk=k: E.tensor_copy(pv[:, aa, kk, 0:1], tokf[:]))
            t_pv = p.op("vector", lambda E, aa=a, g2=g: E.tensor_copy(
                pv[:, aa, :, 1:2].rearrange("p k c -> p (k c)"), w8_all[:, g2, :]), tick=True)
            p.waitt("gpsimd", t_pv)
            for k in range(8):
                b.scatter(d_sc[a], dtw_hbm, pv[:, a, k, :], posu[:, a, k:k + 1], NSLOT - 1)
            b.setr(f"pv{a}", (d_sc[a], d_sc[a].n))

        NG2 = NTT if DEV_G is None else DEV_G
        for g in range(NG2):
            b2_stage1(g)
            if g >= 2:
                b2_stage2(g - 2)
        if NG2 >= 2:
            b2_stage2(NG2 - 2)
        b2_stage2(NG2 - 1)

        p.op("vector", lambda E: E.reduce_max(poisv[0:1, 0:1], cnt_base[:], axis=AX.X))
        p.op("vector", lambda E: E.tensor_scalar(out=poisv[0:1, 1:2], in0=poisv[0:1, 0:1],
                                                 scalar1=float(CAPL), scalar2=3.0e38,
                                                 op0=ALU.is_gt, op1=ALU.mult))
        p.op("vector", lambda E: E.tensor_add(poisv[0:1, 2:3], poisv[0:1, 1:2], poisv[0:1, 1:2]))
        tp = tapt("x1_t31", [128, HD], F32)
        if tp is not None:
            p.dma("sync", d_st, tp, x1[:, 1, :])
        tp = tapt("cnt", [1, 32], F32)
        if tp is not None:
            p.dma("sync", d_st, tp, cnt_base[:])
        tp = tapt("slotf31", [128, 8], F32)
        if tp is not None:
            p.dma("sync", d_st, tp, slotf[:])
        b.fence()
        sc.close()
        b.ring.clear()

    if "C" in BUILD:
        b.fence()
        sc = b.scope()
        wg = sc.alloc("wg", [128, 2, 16, INTER], BF16)
        wu = sc.alloc("wu", [128, 2, 16, INTER], BF16)
        wd = sc.alloc("wd", [128, 6, HD], BF16)
        xg = sc.alloc("xg", [128, 4, HD], BF16)
        xfc = sc.alloc("xfc", [128, 2, 16, 512], BF16)
        gsb = sc.alloc("gsb", [128, 2, 512], BF16)
        actc = sc.alloc("actc", [128, 2, 6, 512], BF16)
        ysb = sc.alloc("ysb", [128, 2, HD], BF16)
        dtw = sc.alloc("dtw", [128, NCH * EXP, 2], F32)
        didx = sc.alloc("didx", [128, NCH * EXP], U32)

        tld = p.dma("sync", d_cs2, dtw[:], dtw_hbm.rearrange("(p j) c -> p j c", p=128))
        p.waitt("vector", tld)
        t_didx = p.op("vector", lambda E: E.tensor_copy(didx[:], dtw[:, :, 0]), tick=True)
        p.waitt("gpsimd", t_didx)
        p.waitt("scalar", tld)

        stgc = sc.alloc("stgc", [128, 2, INTER], F32)
        wgv = w_gate.rearrange("e (c p) i -> e p c i", p=128)
        wuv = w_up.rearrange("e (c p) i -> e p c i", p=128)
        wdv = w_down.rearrange("e (c p) i -> e p c i", p=128)

        t_wg = {}
        t_wu = {}
        CJ = {"jobs": [], "i": 0}

        def cast_add(e):
            we2 = e % 2
            for j in range(16):
                CJ["jobs"].append((wgv[e, :, j, :], wg[:, we2, j, :], ("wg", e, j)))
                CJ["jobs"].append((wuv[e, :, j, :], wu[:, we2, j, :], ("wu", e, j)))

        def cast_pump(n):
            while n > 0 and CJ["i"] < len(CJ["jobs"]):
                src, dst, mark = CJ["jobs"][CJ["i"]]
                sl = CJ["i"] % 2
                CJ["i"] += 1
                n -= 1
                eng = "scalar" if sl else "vector"
                p.waitr("sync", f"cstg{sl}")
                tld2 = p.dma("sync", d_wc[sl], stgc[:, sl, 0:src.shape[-1]], src)
                if mark[2] == 0:
                    p.waitr(eng, f"{mark[0]}{mark[1] % 2}")
                p.waitt(eng, tld2)
                if eng == "scalar":
                    tc = p.op(eng, lambda E, d=dst, s2=stgc[:, sl, 0:dst.shape[-1]]:
                              E.activation(out=d, in_=s2, func=AF.Copy), tick=True)
                else:
                    tc = p.op(eng, lambda E, d=dst, s2=stgc[:, sl, 0:dst.shape[-1]]:
                              E.tensor_copy(d, s2), tick=True)
                b.setr(f"cstg{sl}", tc)
                (t_wg if mark[0] == "wg" else t_wu)[mark[1]] = tc

        cast_add(0)
        cast_pump(32)

        for e in range(EXP if DEV_E is None else DEV_E):
            we = e % 2
            p.waitr("gpsimd", "wd")
            t_wd = p.dma("gpsimd", d_wd, wd[:], wdv[e])
            if e + 1 < EXP:
                cast_add(e + 1)
            for c3 in range(3):
                csz = 512 if c3 < 2 else 256
                nsub = csz // 128
                ring = c3 % 2
                for sub in range(nsub):
                    j = e * NCH + c3 * 4 + sub
                    gslot = (c3 * 4 + sub) % 4
                    p.waitr("gpsimd", f"xg{gslot}")
                    tgt = b.gather(d_g[gslot], xg[:, gslot, :], x1b_hbm, didx[:, j:j + 1], TLOC - 1)
                    p.waitt("tensor", tgt)
                    if sub == 0:
                        p.waitr("tensor", f"xfc{ring}")
                    for grp in range(2):
                        items = [(xg[:, gslot, (grp * 8 + i) * 128:(grp * 8 + i + 1) * 128], 128, i * 128)
                                 for i in range(8)]
                        b.transposes(items, b.psT[grp], ident_bf[:],
                                     xfc[:, ring, grp * 8:grp * 8 + 8, sub * 128:(sub + 1) * 128])
                    b.setr(f"xg{gslot}", b.pet())
                t_xf = b.dvt()
                p.waitt("tensor", t_xf)
                p.waitt("tensor", t_wg[e])
                p.waitt("tensor", t_wu[e])
                for m in range(6):
                    cast_pump(2)
                    bg = b.psA[m % 2]
                    bu = b.psB[m % 2]
                    gslot2 = m % 2
                    b.bank_wait(bg)
                    tg2 = b.mm_chain(bg, csz, [(wg[:, we, k, m * 128:(m + 1) * 128], xfc[:, ring, k, 0:csz])
                                               for k in range(16)])
                    b.bank_wait(bu)
                    tu2 = b.mm_chain(bu, csz, [(wu[:, we, k, m * 128:(m + 1) * 128], xfc[:, ring, k, 0:csz])
                                               for k in range(16)])
                    p.waitt("scalar", tg2)
                    p.waitr("scalar", f"gsb{gslot2}")
                    tsi = p.op("scalar", lambda E, bk=bg, c=csz, gs=gslot2:
                               E.activation(out=gsb[:, gs, 0:c], in_=bk[:, 0:c], func=AF.Silu), tick=True)
                    b.bank_set(bg, tsi)
                    p.waitt("vector", tsi)
                    p.waitt("vector", tu2)
                    p.waitr("vector", f"act{ring}_{m}")
                    tmu = p.op("vector", lambda E, bk=bu, c=csz, mm=m, rr=ring, gs=gslot2:
                               E.tensor_tensor(actc[:, rr, mm, 0:c], gsb[:, gs, 0:c], bk[:, 0:c], ALU.mult),
                               tick=True)
                    b.bank_set(bu, tmu)
                    b.setr(f"gsb{gslot2}", tmu)
                t_act = b.dvt()
                p.waitt("tensor", t_act)
                p.waitt("tensor", t_wd)
                for sub in range(nsub):
                    yring = sub % 2
                    j = e * NCH + c3 * 4 + sub
                    for ci in range(4):
                        bank = b.psF if ci % 2 == 0 else b.psR
                        b.bank_wait(bank)
                        tdn = b.mm_chain(bank, 512,
                                         [(actc[:, ring, m, sub * 128:(sub + 1) * 128],
                                           wd[:, m, ci * 512:(ci + 1) * 512]) for m in range(6)])
                        p.waitt("scalar", tdn)
                        if ci == 0:
                            p.waitr("scalar", f"ysb{yring}")
                        tev = p.op("scalar", lambda E, bk=bank, yr=yring, c2=ci, jj=j:
                                   E.activation(out=ysb[:, yr, c2 * 512:(c2 + 1) * 512], in_=bk[:],
                                                func=AF.Copy, scale=dtw[:, jj, 1:2]), tick=True)
                        b.bank_set(bank, tev)
                    t_ysb = b.act()
                    p.waitt("sync", t_ysb)
                    r0 = e * CAPL + (c3 * 4 + sub) * 128
                    tst = p.dma("sync", d_y[yring], y_hbm[r0:r0 + 128, :], ysb[:, yring, :])
                    b.setr(f"ysb{yring}", tst)
                b.setr(f"xfc{ring}", t_act)
                for mm in range(6):
                    b.setr(f"act{ring}_{mm}", b.pet())
            b.setr(f"wg{we}", b.pet())
            b.setr(f"wu{we}", b.pet())
            b.setr("wd", b.pet())
        b.fence()
        sc.close()
        b.ring.clear()

    if "D" in BUILD:
        b.fence()
        sc = b.scope()
        y8 = sc.alloc("y8", [128, 2, 8, HD], BF16)
        accX = sc.alloc("accX", [128, 2, HD], F32)
        x2f = sc.alloc("x2f", [128, 2, HD], F32)
        outb = sc.alloc("outb", [128, 2, HD], F32)
        sq3 = sc.alloc("sq3", [128, HD], F32)
        ln3 = sc.alloc("ln3", [128, 2, 12], F32)

        banks4 = [b.psA[0], b.psA[1], b.psB[0], b.psB[1]]
        tg_all = {}

        def d_issue(g):
            a = g % 2
            p.waitr("gpsimd", f"y8{a}")
            tg_all[g] = [b.gather(d_ga[k], y8[:, a, k, :], y_hbm, slot_all[:, g, k:k + 1], NSLOT - 1)
                         for k in range(8)]
            p.waitr("sync", f"accX{a}")
            tg_all[(g, "x")] = p.dma("sync", d_al[a], accX[:, a, :], x1f_hbm[g])

        def d_proc(g):
            a = g % 2
            for k in range(8):
                p.waitt("tensor", tg_all[g][k])
            for c in range(4):
                bank = banks4[c]
                b.bank_wait(bank)
                for k in range(8):
                    p.op("tensor", lambda E, bk=bank, aa=a, kk=k, cc=c, st=(k == 0), sp=(k == 7):
                         E.matmul(bk[:, 0:512], ident_bf[:], y8[:, aa, kk, cc * 512:(cc + 1) * 512],
                                  start=st, stop=sp), tick=(k == 7))
                tmm = b.pet()
                p.waitt("vector", tmm)
                if c == 0:
                    p.waitt("vector", tg_all[(g, "x")])
                    p.waitr("vector", f"x2f{a}")
                tev = p.op("vector", lambda E, bk=bank, aa=a, cc=c:
                           E.scalar_tensor_tensor(out=x2f[:, aa, cc * 512:(cc + 1) * 512], in0=bk[:],
                                                  scalar=1.0, in1=accX[:, aa, cc * 512:(cc + 1) * 512],
                                                  op0=ALU.mult, op1=ALU.add,
                                                  accum_out=ln3[:, aa, cc:cc + 1]), tick=True)
                b.bank_set(bank, tev)
            b.setr(f"y8{a}", b.pet())
            b.setr(f"accX{a}", b.dvt())
            p.op("vector", lambda E, aa=a: E.tensor_add(ln3[:, aa, 4:5], ln3[:, aa, 0:1], ln3[:, aa, 1:2]))
            p.op("vector", lambda E, aa=a: E.tensor_add(ln3[:, aa, 5:6], ln3[:, aa, 2:3], ln3[:, aa, 3:4]))
            t_sum = p.op("vector", lambda E, aa=a: E.tensor_add(ln3[:, aa, 6:7], ln3[:, aa, 4:5],
                                                                ln3[:, aa, 5:6]), tick=True)
            p.waitt("scalar", t_sum)
            t_s2 = p.op("scalar", lambda E, aa=a: E.activation(out=sq3[:], in_=x2f[:, aa, :],
                                                               func=AF.Square, accum_out=ln3[:, aa, 7:8]),
                        tick=True)
            p.op("vector", lambda E, aa=a: E.tensor_scalar(out=ln3[:, aa, 8:9], in0=ln3[:, aa, 6:7],
                                                           scalar1=1.0 / HD, scalar2=None, op0=ALU.mult))
            p.waitt("vector", t_s2)
            p.op("vector", lambda E, aa=a: E.tensor_scalar(out=ln3[:, aa, 9:10], in0=ln3[:, aa, 7:8],
                                                           scalar1=1.0 / HD, scalar2=None, op0=ALU.mult))
            p.op("vector", lambda E, aa=a: E.tensor_mul(ln3[:, aa, 10:11], ln3[:, aa, 8:9], ln3[:, aa, 8:9]))
            p.op("vector", lambda E, aa=a: E.scalar_tensor_tensor(out=ln3[:, aa, 10:11], in0=ln3[:, aa, 10:11],
                                                                  scalar=-1.0, in1=ln3[:, aa, 9:10],
                                                                  op0=ALU.mult, op1=ALU.add))
            t_ve = p.op("vector", lambda E, aa=a: E.tensor_scalar(out=ln3[:, aa, 11:12], in0=ln3[:, aa, 10:11],
                                                                  scalar1=1.0, scalar2=1e-5,
                                                                  op0=ALU.mult, op1=ALU.add), tick=True)
            p.waitt("scalar", t_ve)
            t_sq = p.op("scalar", lambda E, aa=a: E.activation(out=ln3[:, aa, 10:11], in_=ln3[:, aa, 11:12],
                                                               func=AF.Sqrt), tick=True)
            p.waitt("vector", t_sq)
            p.op("vector", lambda E, aa=a: E.reciprocal(ln3[:, aa, 11:12], ln3[:, aa, 10:11]))
            p.waitr("vector", f"outb{a}")
            t_o = p.op("vector", lambda E, aa=a: E.tensor_scalar(
                out=outb[:, aa, :], in0=x2f[:, aa, :], scalar1=ln3[:, aa, 8:9], scalar2=ln3[:, aa, 11:12],
                op0=ALU.subtract, op1=ALU.mult), tick=True)
            if g == NTT - 1:
                t_o = p.op("vector", lambda E, aa=a: E.tensor_scalar(
                    out=outb[0:1, aa, :], in0=outb[0:1, aa, :], scalar1=poisv[0:1, 2:3],
                    scalar2=None, op0=ALU.add), tick=True)
            b.setr(f"x2f{a}", b.dvt())
            p.waitt("sync", t_o)
            tst = p.dma("sync", d_ob[a], OUT[g * 128:(g + 1) * 128, :], outb[:, a, :])
            b.setr(f"outb{a}", tst)

        ND = NTT if DEV_D is None else DEV_D * 8
        d_issue(0)
        for g in range(1, ND):
            d_issue(g)
            d_proc(g - 1)
        d_proc(ND - 1)
        p.wait("sync", d_ob[0], d_ob[0].n)
        p.wait("sync", d_ob[1], d_ob[1].n)
        sc.close()

    b.taps_map = taps
    return b


def finalize(b):
    nc = b.nc
    with nc.Block() as block:
        @block.sync
        def _(E):
            for fn in b.st["sync"]:
                fn(E)

        @block.gpsimd
        def _(E):
            for fn in b.st["gpsimd"]:
                fn(E)

        @block.scalar
        def _(E):
            for fn in b.st["scalar"]:
                fn(E)

        @block.vector
        def _(E):
            for fn in b.st["vector"]:
                fn(E)

        @block.tensor
        def _(E):
            for fn in b.st["tensor"]:
                fn(E)
    return nc


_CACHE = {}


def _get_program():
    key = (tuple(BUILD), tuple(TAPS))
    if key not in _CACHE:
        bb = build_program()
        _CACHE[key] = (finalize(bb), bb)
    return _CACHE[key]


def make_in_maps(x, wts):
    hc = host_consts()
    base = dict(wts)
    for k, v in hc.items():
        base["c_" + k] = v
    in_maps = []
    for c in range(NCORE):
        m = dict(base)
        m["xin"] = np.ascontiguousarray(x[c * SEQ:(c + 1) * SEQ])
        in_maps.append(m)
    return in_maps


def kernel(L0_attn_hidden, L0_attn_w_kv_a, L0_attn_q_a_gamma, L0_attn_w_qb,
           L0_attn_kv_a_gamma, L0_attn_w_kvb, L0_attn_w_o,
           L0_ffn_gate_w, L0_ffn_w_gate, L0_ffn_w_up, L0_ffn_w_down):
    nc, bb = _get_program()
    wts = {
        "w_kv_a": np.ascontiguousarray(L0_attn_w_kv_a, np.float32),
        "w_qb": np.ascontiguousarray(L0_attn_w_qb, np.float32),
        "w_kvb": np.ascontiguousarray(L0_attn_w_kvb, np.float32),
        "w_o": np.ascontiguousarray(L0_attn_w_o, np.float32),
        "gate_w": np.ascontiguousarray(L0_ffn_gate_w, np.float32),
        "w_gate": np.ascontiguousarray(L0_ffn_w_gate, np.float32),
        "w_up": np.ascontiguousarray(L0_ffn_w_up, np.float32),
        "w_down": np.ascontiguousarray(L0_ffn_w_down, np.float32),
        "gamma_qb": np.ascontiguousarray(np.broadcast_to(
            np.asarray(L0_attn_q_a_gamma, np.float32), (128, QL))),
        "gamma_kvb": np.ascontiguousarray(np.broadcast_to(
            np.asarray(L0_attn_kv_a_gamma, np.float32), (128, KL))),
    }
    x = np.ascontiguousarray(L0_attn_hidden, np.float32)
    in_maps = make_in_maps(x, wts)
    res = run_bass_kernel_spmd(nc, in_maps, core_ids=list(range(NCORE)))
    out = np.concatenate([r["out"].reshape(SEQ, S, HD) for r in res.results], axis=0)
    return out


# version 18
# speedup vs baseline: 1.1249x; 1.0284x over previous
import math
import numpy as np
import concourse.bass as bass
import concourse.mybir as mybir
from concourse.bass import IndirectOffsetOnAxis
from concourse.bass_utils import run_bass_kernel_spmd

F32 = mybir.dt.float32
BF16 = mybir.dt.bfloat16
U32 = mybir.dt.uint32
AF = mybir.ActivationFunctionType
ALU = mybir.AluOpType
AX = mybir.AxisListType

NCORE = 8
SEQ = 2
S = 2048
HD = 2048
NH = 16
DN = 128
DR = 64
DV = 128
QL = 1536
KL = 512
EXP = 32
INTER = 768
CAPL = 1280
NCH = CAPL // 128
NSLOT = EXP * CAPL
TLOC = SEQ * S
NTT = TLOC // 128
NQ = NH * (DN + DR)
NKV = NH * (DN + DV)
FU = QL + KL + DR
SCALE = 1.0 / math.sqrt(DN + DR)
NEG = -3.0e38

BUILD = ("A1", "A2", "A3B", "B2", "C", "D")
TAPS = ()
DEV_SEQ = None
DEV_T = None
DEV_QB = None
DEV_G = None
DEV_E = None
DEV_D = None


def host_consts():
    c = {}
    c["ident"] = np.eye(128, dtype=np.float32)
    c["lt"] = np.triu(np.ones((128, 128), np.float32), 1)
    c["negtri"] = np.triu(np.full((128, 128), NEG, np.float32), 1)
    c["iota32"] = np.broadcast_to(np.arange(32, dtype=np.float32), (128, 32)).copy()
    c["iota32x4"] = np.broadcast_to(np.tile(np.arange(32, dtype=np.float32), 4), (128, 128)).copy()
    tok = np.arange(NTT, dtype=np.float32)[:, None] * 128 + np.arange(128, dtype=np.float32)[None, :]
    c["tok8"] = np.ascontiguousarray(np.broadcast_to(tok.T[:, :, None], (128, NTT, 8)))
    c["iota_f"] = np.arange(128, dtype=np.float32).reshape(128, 1)
    c["ones_col"] = np.ones((128, 1), np.float32)
    c["ones_row"] = np.ones((1, 128), np.float32)
    inv_freq = 1.0 / (10000.0 ** (np.arange(0, DR, 2, dtype=np.float64) / DR))
    ang = np.arange(S, dtype=np.float64)[:, None] * inv_freq[None, :]
    c["cos_t"] = np.concatenate([np.cos(ang)] * 2, axis=1).astype(np.float32)
    c["sin_t"] = np.concatenate([np.sin(ang)] * 2, axis=1).astype(np.float32)
    c["cos16"] = np.ascontiguousarray(np.broadcast_to(c["cos_t"][:, None, :], (S, NH, DR)))
    c["sin16"] = np.ascontiguousarray(np.broadcast_to(c["sin_t"][:, None, :], (S, NH, DR)))
    return c


class Sem:
    __slots__ = ("h", "n")

    def __init__(self, h):
        self.h = h
        self.n = 0


class _Scope:
    def __init__(self, b):
        self.b = b
        self.handles = []

    _uid = [0]

    def alloc(self, name, shape, dtype):
        _Scope._uid[0] += 1
        g = self.b.nc.sbuf_tensor(f"{name}_{_Scope._uid[0]}", shape, dtype)
        h = g.__enter__()
        self.handles.append(g)
        return h

    def close(self):
        for g in reversed(self.handles):
            g.__exit__(None, None, None)
        self.handles = []


class Builder:
    ENGS = ("sync", "gpsimd", "scalar", "vector", "tensor")

    def __init__(self):
        nc = bass.Bass("TRN2", target_bir_lowering=False)
        self.nc = nc
        self.st = {k: [] for k in self.ENGS}
        self._semcache = {}

        self.c_pe = self.sem("c_pe")
        self.c_gp = self.sem("c_gp")
        self.c_dv = self.sem("c_dv")
        self.c_ac = self.sem("c_ac")

        self.psA = [nc.alloc_psum_tensor(f"psA{i}", [128, 512], F32) for i in range(2)]
        self.psB = [nc.alloc_psum_tensor(f"psB{i}", [128, 512], F32) for i in range(2)]
        self.psT = [nc.alloc_psum_tensor(f"psT{i}", [128, 1024], BF16) for i in range(2)]
        self.psR = nc.alloc_psum_tensor("psR", [128, 512], F32)
        self.psF = nc.alloc_psum_tensor("psF", [128, 512], F32)
        self.bankfree = {}
        self.ring = {}

    def sem(self, name):
        if name not in self._semcache:
            self._semcache[name] = Sem(self.nc.alloc_semaphore(name=name))
        return self._semcache[name]

    def all_dma_sems(self):
        return [s for nm, s in self._semcache.items() if nm.startswith("d_")]

    def wait(self, eng, sem, val):
        if sem is None or val is None or val <= 0:
            return
        self.st[eng].append(lambda E, h=sem.h, v=val: E.wait_ge(h, v))

    def waitt(self, eng, tick):
        if tick is not None:
            self.wait(eng, tick[0], tick[1])

    def waitr(self, eng, key):
        self.waitt(eng, self.ring.get(key))

    def setr(self, key, tick):
        self.ring[key] = tick

    def dma(self, eng, sem, out, in_, **kw):
        sem.n += 16
        v = sem.n
        self.st[eng].append(
            lambda E, o=out, i=in_, h=sem.h, k=dict(kw): E.dma_start(out=o, in_=i, **k).then_inc(h, 16))
        return (sem, v)

    @staticmethod
    def _breg(E, bound):
        cache = getattr(E, "_bnd_regs", None)
        if cache is None:
            cache = {}
            E._bnd_regs = cache
        if bound not in cache:
            cache[bound] = E.to_reg(bound)
        return cache[bound]

    def gather(self, sem, out, src, idx, bound, accum=False):
        sem.n += 16
        v = sem.n
        if accum:
            kw = dict(compute_op=ALU.add, oob_is_err=True)
        else:
            kw = dict(oob_is_err=False)
        self.st["gpsimd"].append(
            lambda E, o=out, s2=src, i2=idx, h=sem.h, bb=bound, k=dict(kw):
            E.indirect_dma_start(out=o, out_offset=None, in_=s2,
                                 in_offset=IndirectOffsetOnAxis(i2, 0),
                                 bounds_check=Builder._breg(E, bb), **k).then_inc(h, 16))
        return (sem, v)

    def scatter(self, sem, dst, src, idx, bound):
        sem.n += 16
        v = sem.n
        self.st["gpsimd"].append(
            lambda E, d=dst, s2=src, i2=idx, h=sem.h, bb=bound:
            E.indirect_dma_start(out=d, out_offset=IndirectOffsetOnAxis(i2, 0),
                                 in_=s2, in_offset=None,
                                 bounds_check=Builder._breg(E, bb), oob_is_err=False).then_inc(h, 16))
        return (sem, v)

    def op(self, eng, fn, tick=False):
        if eng in ("vector", "scalar", "gpsimd"):
            sem = {"vector": self.c_dv, "scalar": self.c_ac, "gpsimd": self.c_gp}[eng]
            self.wait(eng, sem, sem.n)
            sem.n += 1
            v = sem.n
            self.st[eng].append(lambda E, f=fn, h=sem.h: f(E).then_inc(h, 1))
            return (sem, v)
        if not tick:
            self.st[eng].append(fn)
            return None
        sem = self.c_pe
        sem.n += 1
        v = sem.n
        self.st[eng].append(lambda E, f=fn, h=sem.h: f(E).then_inc(h, 1))
        return (sem, v)

    def pet(self):
        return (self.c_pe, self.c_pe.n)

    def dvt(self):
        return (self.c_dv, self.c_dv.n)

    def act(self):
        return (self.c_ac, self.c_ac.n)

    def alloc(self, name, shape, dtype):
        return self.nc.alloc_sbuf_tensor(name, shape, dtype)

    def scope(self):
        return _Scope(self)

    def bank_wait(self, bank):
        self.waitt("tensor", self.bankfree.get(bank.name))

    def bank_set(self, bank, tick):
        self.bankfree[bank.name] = tick

    def fence(self, exclude=()):
        marks = [(self.c_pe, self.c_pe.n), (self.c_dv, self.c_dv.n), (self.c_ac, self.c_ac.n),
                 (self.c_gp, self.c_gp.n)]
        ex = {id(s) for s in exclude}
        marks += [(s, s.n) for s in self.all_dma_sems() if s.n > 0 and id(s) not in ex]
        for eng in self.ENGS:
            for s, v in marks:
                self.wait(eng, s, v)

    def transposes(self, items, bank, ident, dst, pre_evac_wait=None, eng="vector"):
        self.bank_wait(bank)
        n = len(items)
        for i, (src, np_, off) in enumerate(items):
            self.op("tensor",
                    lambda E, s=src, o=off, pp=bank, idn=ident, q=np_, st=(i == 0), sp=(i == n - 1):
                    E.matmul(pp[0:q, o:o + 128], s, idn, start=st, stop=sp, is_transpose=True),
                    tick=(i == n - 1))
        tpe = self.pet()
        self.waitt(eng, tpe)
        if pre_evac_wait is not None:
            self.waitt(eng, pre_evac_wait)
        q = items[0][1]
        w = items[-1][2] + 128
        if eng == "vector":
            tev = self.op(eng,
                          lambda E, d=dst, pp=bank, q2=q, w2=w:
                          E.tensor_copy(d, pp[0:q2, 0:w2].rearrange("p (a x) -> p a x", x=128)),
                          tick=True)
        else:
            tev = self.op(eng,
                          lambda E, d=dst, pp=bank, q2=q, w2=w:
                          E.activation(out=d, in_=pp[0:q2, 0:w2].rearrange("p (a x) -> p a x", x=128),
                                       func=AF.Copy),
                          tick=True)
        self.bank_set(bank, tev)
        return tev

    def mm_chain(self, bank, cols, pieces, out_off=0):
        n = len(pieces)
        for i, (lh, rh) in enumerate(pieces):
            self.op("tensor",
                    lambda E, b2=bank, l=lh, r=rh, s=(i == 0), t=(i == n - 1), c=cols, oo=out_off:
                    E.matmul(b2[:, oo:oo + c], l, r, start=s, stop=t),
                    tick=(i == n - 1))
        return self.pet()


def build_program():
    b = Builder()
    nc = b.nc
    p = b

    d_cst = b.sem("d_cst")
    d_cstg = b.sem("d_cstg")
    d_cs2 = b.sem("d_cs2")
    d_al = [b.sem(f"d_al{i}") for i in range(8)]
    d_w = b.sem("d_w")
    d_st = b.sem("d_st")
    d_z = b.sem("d_z")
    d_sA = [b.sem(f"d_sA{i}") for i in range(2)]
    d_sB = [b.sem(f"d_sB{i}") for i in range(2)]
    d_sQ = [b.sem(f"d_sQ{i}") for i in range(2)]
    d_sV = [b.sem(f"d_sV{i}") for i in range(2)]
    d_c16 = [b.sem(f"d_c16{i}") for i in range(2)]
    d_sAt = b.sem("d_sAt")
    d_sx1 = [b.sem(f"d_sx1{i}") for i in range(2)]
    d_x = [b.sem(f"d_x{i}") for i in range(2)]
    d_q = [b.sem(f"d_q{i}") for i in range(2)]
    d_v = [b.sem(f"d_v{i}") for i in range(2)]
    d_sc = [b.sem(f"d_sc{i}") for i in range(2)]
    d_g = [b.sem(f"d_g{i}") for i in range(4)]
    d_y = [b.sem(f"d_y{i}") for i in range(2)]
    d_ga = [b.sem(f"d_ga{i}") for i in range(8)]
    d_ob = [b.sem(f"d_ob{i}") for i in range(2)]
    d_wd = b.sem("d_wd")
    d_wc = [b.sem(f"d_wc{i}") for i in range(2)]

    xin = nc.dram_tensor("xin", [SEQ, S, HD], F32, kind="ExternalInput").ap()
    w_kv_a = nc.dram_tensor("w_kv_a", [HD, FU], F32, kind="ExternalInput").ap()
    w_qb = nc.dram_tensor("w_qb", [QL, NQ], F32, kind="ExternalInput").ap()
    w_kvb = nc.dram_tensor("w_kvb", [KL, NKV], F32, kind="ExternalInput").ap()
    w_o = nc.dram_tensor("w_o", [NH * DV, HD], F32, kind="ExternalInput").ap()
    gate_w = nc.dram_tensor("gate_w", [HD, EXP], F32, kind="ExternalInput").ap()
    w_gate = nc.dram_tensor("w_gate", [EXP, HD, INTER], F32, kind="ExternalInput").ap()
    w_up = nc.dram_tensor("w_up", [EXP, HD, INTER], F32, kind="ExternalInput").ap()
    w_down = nc.dram_tensor("w_down", [EXP, INTER, HD], F32, kind="ExternalInput").ap()
    gamma_qb_i = nc.dram_tensor("gamma_qb", [128, QL], F32, kind="ExternalInput").ap()
    gamma_kvb_i = nc.dram_tensor("gamma_kvb", [128, KL], F32, kind="ExternalInput").ap()
    HC = host_consts()
    cs = {k: nc.dram_tensor("c_" + k, list(v.shape), F32, kind="ExternalInput").ap()
          for k, v in HC.items()}

    OUT = nc.dram_tensor("out", [TLOC, HD], F32, kind="ExternalOutput").ap()

    qcn_hbm = nc.dram_tensor("qcn_hbm", [NTT, 128, QL], BF16, kind="Internal").ap()
    kvn_hbm = nc.dram_tensor("kvn_hbm", [NTT, 128, KL], BF16, kind="Internal").ap()
    kpe_hbm = nc.dram_tensor("kpe_hbm", [NTT, 128, DR], BF16, kind="Internal").ap()
    Q_hbm = nc.dram_tensor("q_hbm", [SEQ, S, NQ], BF16, kind="Internal").ap()
    V_hbm = nc.dram_tensor("v_hbm", [SEQ, S, NH * DV], BF16, kind="Internal").ap()
    attn_hbm = nc.dram_tensor("attn_hbm", [SEQ, S, HD], BF16, kind="Internal").ap()
    x1f_hbm = nc.dram_tensor("x1f_hbm", [NTT, 128, HD], F32, kind="Internal").ap()
    x1b_hbm = nc.dram_tensor("x1b_hbm", [TLOC, HD], BF16, kind="Internal").ap()
    dtw_hbm = nc.dram_tensor("dtw_hbm", [NSLOT, 2], F32, kind="Internal").ap()
    base_hbm = nc.dram_tensor("base_hbm", [1, NTT, EXP], F32, kind="Internal").ap()
    cnt_hbm = nc.dram_tensor("cnt_hbm", [1, NTT, EXP], F32, kind="Internal").ap()
    y_hbm = nc.dram_tensor("y_hbm", [NSLOT, HD], BF16, kind="Internal").ap()

    taps = {}

    def tapt(name, shape, dtype):
        if name in TAPS:
            taps[name] = nc.dram_tensor("tap_" + name, shape, dtype, kind="ExternalOutput").ap()
            return taps[name]
        return None

    ident_bf = b.alloc("ident_bf", [128, 128], BF16)
    ident_f = b.alloc("ident_f", [128, 128], F32)
    lt_f = b.alloc("lt_f", [128, 128], F32)
    negtri = b.alloc("negtri", [128, 128], F32)
    iota32 = b.alloc("iota32", [128, 32], F32)
    iota32x4 = b.alloc("iota32x4", [128, 128], F32)
    iota_f = b.alloc("iota_f", [128, 1], F32)
    ones_col = b.alloc("ones_col", [128, 1], F32)
    ones_row = b.alloc("ones_row", [1, 128], F32)

    slot_all = b.alloc("slot_all", [128, NTT, 8], U32)
    tok_sb = b.alloc("tok_sb", [128, NTT, 8], F32)
    poisv = b.alloc("poisv", [1, 4], F32)
    scG = b.scope()
    cos_sb = scG.alloc("cos_sb", [128, 16, DR], F32)
    sin_sb = scG.alloc("sin_sb", [128, 16, DR], F32)
    gq = scG.alloc("gq", [128, QL], F32)
    gkv = scG.alloc("gkv", [128, KL], F32)

    p.dma("sync", d_cst, ident_f[:], cs["ident"])
    p.dma("gpsimd", d_cstg, ident_bf[:], cs["ident"])
    p.dma("sync", d_cst, lt_f[:], cs["lt"])
    p.dma("sync", d_cst, negtri[:], cs["negtri"])
    p.dma("sync", d_cst, iota32[:], cs["iota32"])
    p.dma("sync", d_cst, iota32x4[:], cs["iota32x4"])
    p.dma("sync", d_cst, tok_sb[:], cs["tok8"])
    p.dma("sync", d_cst, iota_f[:], cs["iota_f"])
    p.dma("sync", d_cst, ones_col[:], cs["ones_col"])
    p.dma("sync", d_cst, ones_row[:], cs["ones_row"])
    p.dma("sync", d_cst, cos_sb[:], cs["cos_t"].rearrange("(t p) d -> p t d", p=128))
    p.dma("sync", d_cst, sin_sb[:], cs["sin_t"].rearrange("(t p) d -> p t d", p=128))
    p.dma("sync", d_cst, gq[:], gamma_qb_i)
    p.dma("sync", d_cst, gkv[:], gamma_kvb_i)
    CST = (d_cst, d_cst.n)

    sc0 = b.scope()
    zer = sc0.alloc("zer", [128, 640], F32)
    tz = p.op("vector", lambda E: E.memset(zer[:], 0.0), tick=True)
    p.waitt("sync", tz)
    p.dma("sync", d_z, dtw_hbm.rearrange("(p j) c -> p (j c)", p=128), zer[:])

    if "A1" in BUILD:
        sc = b.scope()
        wkva = sc.alloc("wkva", [128, 16, FU], BF16)
        xt = sc.alloc("xt", [128, 2, HD], F32)
        xbf = sc.alloc("xbf", [128, 2, HD], BF16)
        xfm = sc.alloc("xfm", [128, 2, 16, 128], BF16)
        fus = sc.alloc("fus", [128, 2, FU], F32)
        sqs = sc.alloc("sqs", [128, QL], F32)
        qcn = sc.alloc("qcn", [128, 2, QL], BF16)
        qcnf = sc.alloc("qcnf", [128, 2, 12, 128], BF16)
        kvn = sc.alloc("kvn", [128, 2, KL], BF16)
        kpo = sc.alloc("kpo", [128, 2, DR], BF16)
        rsc = sc.alloc("rsc", [128, 12], F32)
        rps = sc.alloc("rps", [128, 2 * DR], F32)

        tw = p.dma("gpsimd", d_w, wkva[:], w_kv_a.rearrange("(c p) n -> p c n", p=128))
        p.waitt("tensor", tw)
        p.wait("tensor", d_cstg, 16)
        p.waitt("tensor", CST)
        p.waitt("vector", CST)
        p.waitt("scalar", CST)

        NCHUNKS = [(0, 512), (512, 1024), (1024, 1536), (1536, 2048), (2048, FU)]
        for g in range(NTT if DEV_G is None else DEV_G):
            s, t = divmod(g, 16)
            a = g % 2
            p.waitr("sync", f"xt{a}")
            tx = p.dma("sync", d_x[a], xt[:, a, :], xin[s, t * 128:(t + 1) * 128, :])
            p.waitt("vector", tx)
            p.waitr("vector", f"xbf{a}")
            t_cast = p.op("vector", lambda E, aa=a: E.tensor_copy(xbf[:, aa, :], xt[:, aa, :]), tick=True)
            b.setr(f"xt{a}", t_cast)
            p.waitt("tensor", t_cast)
            for grp in range(2):
                items = [(xbf[:, a, (grp * 8 + j) * 128:(grp * 8 + j + 1) * 128], 128, j * 128)
                         for j in range(8)]
                tev = b.transposes(items, b.psT[grp], ident_bf[:], xfm[:, a, grp * 8:grp * 8 + 8, :],
                                   pre_evac_wait=b.ring.get(f"xfm{a}"))
            b.setr(f"xbf{a}", b.pet())
            t_xfm = tev
            p.waitt("tensor", t_xfm)
            for ci, (n0, n1) in enumerate(NCHUNKS):
                bank = b.psA[ci % 2]
                b.bank_wait(bank)
                tmm = b.mm_chain(bank, n1 - n0,
                                 [(xfm[:, a, k, :], wkva[:, k, n0:n1]) for k in range(16)])
                eng = "vector" if ci % 2 == 0 else "scalar"
                p.waitt(eng, tmm)
                if eng == "vector":
                    tev = p.op(eng, lambda E, aa=a, m0=n0, m1=n1, bk=bank:
                               E.tensor_copy(fus[:, aa, m0:m1], bk[:, 0:m1 - m0]), tick=True)
                else:
                    tev = p.op(eng, lambda E, aa=a, m0=n0, m1=n1, bk=bank:
                               E.activation(out=fus[:, aa, m0:m1], in_=bk[:, 0:m1 - m0], func=AF.Copy), tick=True)
                b.bank_set(bank, tev)
            b.setr(f"xfm{a}", b.pet())
            t_fusv, t_fusa = b.dvt(), b.act()
            p.waitt("scalar", t_fusv)
            p.op("scalar", lambda E, aa=a: E.activation(out=sqs[:, 0:QL], in_=fus[:, aa, 0:QL],
                                                        func=AF.Square, accum_out=rsc[:, 0:1]))
            t_s2 = p.op("scalar", lambda E, aa=a: E.activation(out=sqs[:, 0:KL], in_=fus[:, aa, QL:QL + KL],
                                                               func=AF.Square, accum_out=rsc[:, 1:2]), tick=True)
            p.waitt("vector", t_s2)
            p.op("vector", lambda E: E.tensor_scalar(out=rsc[:, 2:3], in0=rsc[:, 0:1], scalar1=1.0 / QL,
                                                     scalar2=1e-6, op0=ALU.mult, op1=ALU.add))
            t_pre = p.op("vector", lambda E: E.tensor_scalar(out=rsc[:, 3:4], in0=rsc[:, 1:2], scalar1=1.0 / KL,
                                                             scalar2=1e-6, op0=ALU.mult, op1=ALU.add), tick=True)
            p.waitt("scalar", t_pre)
            t_sq = p.op("scalar", lambda E: E.activation(out=rsc[:, 4:6], in_=rsc[:, 2:4], func=AF.Sqrt), tick=True)
            p.waitt("vector", t_sq)
            p.op("vector", lambda E: E.reciprocal(rsc[:, 6:7], rsc[:, 4:5]))
            p.op("vector", lambda E: E.reciprocal(rsc[:, 7:8], rsc[:, 5:6]))
            p.waitr("vector", f"qcn{a}")
            p.waitr("vector", f"kvn{a}")
            p.op("vector", lambda E, aa=a: E.scalar_tensor_tensor(
                out=qcn[:, aa, :], in0=fus[:, aa, 0:QL], scalar=rsc[:, 6:7], in1=gq[:],
                op0=ALU.mult, op1=ALU.mult))
            t_kvn = p.op("vector", lambda E, aa=a: E.scalar_tensor_tensor(
                out=kvn[:, aa, :], in0=fus[:, aa, QL:QL + KL], scalar=rsc[:, 7:8], in1=gkv[:],
                op0=ALU.mult, op1=ALU.mult), tick=True)
            kp = fus[:, a, QL + KL:FU]
            cv = cos_sb[:, t, :]
            sv = sin_sb[:, t, :]
            p.op("vector", lambda E, x=kp, c2=cv: E.tensor_mul(rps[:, 0:DR], x, c2))
            p.op("vector", lambda E, x=kp, s2=sv: E.tensor_mul(rps[:, DR:DR + 32], x[:, 32:64], s2[:, 0:32]))
            p.op("vector", lambda E, x=kp, s2=sv: E.tensor_mul(rps[:, DR + 32:DR + 64], x[:, 0:32], s2[:, 32:64]))
            p.op("vector", lambda E, aa=a: E.tensor_sub(kpo[:, aa, 0:32], rps[:, 0:32], rps[:, DR:DR + 32]))
            t_kpo = p.op("vector", lambda E, aa=a: E.tensor_add(kpo[:, aa, 32:64], rps[:, 32:64],
                                                                rps[:, DR + 32:DR + 64]), tick=True)
            p.waitt("tensor", t_kvn)
            for grp, ntr in ((0, 8), (1, 4)):
                items = [(qcn[:, a, (grp * 8 + j) * 128:(grp * 8 + j + 1) * 128], 128, j * 128)
                         for j in range(ntr)]
                tev = b.transposes(items, b.psT[grp], ident_bf[:],
                                   qcnf[:, a, grp * 8:grp * 8 + ntr, :],
                                   pre_evac_wait=b.ring.get(f"qcnf{a}"))
            b.setr(f"qcn{a}", b.pet())
            t_qf = tev
            p.waitt("sync", t_qf)
            p.waitt("sync", t_kpo)
            tst = p.dma("sync", d_sA[a], qcn_hbm[g], qcnf[:, a, :, :])
            b.setr(f"qcnf{a}", tst)
            p.dma("sync", d_sB[a], kvn_hbm[g], kvn[:, a, :])
            tst = p.dma("sync", d_sB[a], kpe_hbm[g], kpo[:, a, :])
            b.setr(f"kvn{a}", tst)

        tp = tapt("fused31", [128, FU], F32)
        if tp is not None:
            p.dma("sync", d_st, tp, fus[:, 1, :])
        tp = tapt("qcn31", [128, QL], BF16)
        if tp is not None:
            p.dma("sync", d_st, tp, qcn[:, 1, :])
        b.fence()
        sc.close()
        b.ring.clear()

    if "A2" in BUILD:
        sc = b.scope()
        wqb = sc.alloc("wqb", [128, 12, NQ], BF16)
        qti = sc.alloc("qti", [128, 2, QL], BF16)
        qf32 = sc.alloc("qf32", [128, 2, NQ], F32)
        qbf = sc.alloc("qbf", [128, 2, NQ], BF16)
        rp2 = sc.alloc("rp2", [128, NH, 2 * DR], F32)
        c16 = sc.alloc("c16", [128, 2, 2, NH, DR], F32)

        tw = p.dma("gpsimd", d_w, wqb[:], w_qb.rearrange("(c p) n -> p c n", p=128))
        p.waitt("tensor", tw)

        for g in range(NTT if DEV_G is None else DEV_G):
            s, t = divmod(g, 16)
            a = g % 2
            p.waitr("sync", f"qti{a}")
            tq = p.dma("sync", d_q[a], qti[:, a, :], qcn_hbm[g])
            p.waitr("scalar", f"c16{a}")
            p.dma("sync", d_c16[a], c16[:, a, 0], cs["cos16"].rearrange("(t p) h d -> t p h d", p=128)[t])
            tc16 = p.dma("sync", d_c16[a], c16[:, a, 1], cs["sin16"].rearrange("(t p) h d -> t p h d", p=128)[t])
            p.waitt("tensor", tq)
            p.waitr("tensor", f"qf{a}")
            for ci in range(6):
                n0 = ci * 512
                bank = b.psA[ci % 2]
                b.bank_wait(bank)
                tmm = b.mm_chain(bank, 512,
                                 [(qti[:, a, k * 128:(k + 1) * 128], wqb[:, k, n0:n0 + 512])
                                  for k in range(12)])
                eng = "vector" if ci % 2 == 0 else "scalar"
                p.waitt(eng, tmm)
                if eng == "vector":
                    tev = p.op(eng, lambda E, aa=a, m0=n0, bk=bank:
                               E.tensor_copy(qf32[:, aa, m0:m0 + 512], bk[:]), tick=True)
                else:
                    tev = p.op(eng, lambda E, aa=a, m0=n0, bk=bank:
                               E.activation(out=qf32[:, aa, m0:m0 + 512], in_=bk[:], func=AF.Copy), tick=True)
                b.bank_set(bank, tev)
            b.setr(f"qti{a}", b.pet())
            p.waitt("vector", b.act())
            p.waitt("vector", tc16)
            xpe = qf32[:, a, :].rearrange("p (h d) -> p h d", d=DN + DR)[:, :, DN:DN + DR]
            cv16 = c16[:, a, 0]
            sv16 = c16[:, a, 1]
            p.op("vector", lambda E, x=xpe, c2=cv16: E.tensor_tensor(rp2[:, :, 0:DR], x, c2, ALU.mult))
            p.op("vector", lambda E, x=xpe, s2=sv16:
                 E.tensor_tensor(rp2[:, :, DR:DR + 32], x[:, :, 32:64], s2[:, :, 0:32], ALU.mult))
            p.op("vector", lambda E, x=xpe, s2=sv16:
                 E.tensor_tensor(rp2[:, :, DR + 32:DR + 64], x[:, :, 0:32], s2[:, :, 32:64], ALU.mult))
            p.op("vector", lambda E, x=xpe: E.tensor_tensor(x[:, :, 0:32], rp2[:, :, 0:DR][:, :, 0:32],
                                                            rp2[:, :, DR:DR + 32], ALU.subtract))
            tr5 = p.op("vector", lambda E, x=xpe: E.tensor_tensor(x[:, :, 32:64], rp2[:, :, 32:64],
                                                                  rp2[:, :, DR + 32:DR + 64], ALU.add), tick=True)
            b.setr(f"c16{a}", tr5)
            p.waitr("vector", f"qbf{a}")
            t_rope = p.op("vector", lambda E, aa=a: E.tensor_scalar(
                out=qbf[:, aa, :], in0=qf32[:, aa, :], scalar1=SCALE, scalar2=None, op0=ALU.mult), tick=True)
            b.setr(f"qf{a}", t_rope)
            p.waitt("sync", t_rope)
            tst = p.dma("sync", d_sQ[a], Q_hbm[s, t * 128:(t + 1) * 128, :], qbf[:, a, :])
            b.setr(f"qbf{a}", tst)

        tp = tapt("q31", [128, NQ], BF16)
        if tp is not None:
            p.dma("sync", d_st, tp, qbf[:, 1, :])
        b.fence()
        sc.close()
        b.ring.clear()

    sc0.close()
    scG.close()

    if "A3B" in BUILD or "A3" in BUILD or "B" in BUILD:
        DO_B = ("A3B" in BUILD) or ("B" in BUILD)
        scK = b.scope()
        KN = scK.alloc("KN", [128, NH, S], BF16)
        KP = scK.alloc("KP", [128, S], BF16)
        VS = scK.alloc("VS", [128, S // 128, NH * DV], BF16)

        def _seq_body(s):
            sc = b.scope()
            wkvb = sc.alloc("wkvb", [128, 4, NKV], BF16)
            kvi = sc.alloc("kvi", [128, 2, KL], BF16)
            kpi = sc.alloc("kpi", [128, 2, DR], BF16)
            kvf = sc.alloc("kvf", [128, 2, 4, 128], BF16)
            knb = sc.alloc("knb", [128, 2, NH, DN], BF16)

            t_wkvb = p.dma("gpsimd", d_w, wkvb[:], w_kvb.rearrange("(c p) n -> p c n", p=128))
            p.waitt("tensor", t_wkvb)
            for t in range(16 if DEV_T is None else DEV_T):
                g = s * 16 + t
                a = t % 2
                p.waitr("sync", f"kvi{a}")
                p.dma("sync", d_q[a], kvi[:, a, :], kvn_hbm[g])
                tk2 = p.dma("sync", d_q[a], kpi[:, a, :], kpe_hbm[g])
                p.waitt("tensor", tk2)
                items = [(kvi[:, a, j * 128:(j + 1) * 128], 128, j * 128) for j in range(4)]
                tev = b.transposes(items, b.psT[0], ident_bf[:], kvf[:, a, :, :],
                                   pre_evac_wait=b.ring.get(f"kvf{a}"))
                p.waitt("tensor", tev)
                p.waitr("tensor", f"knb{a}")
                for ci in range(8):
                    n0 = ci * 512
                    bank = b.psA[ci % 2]
                    b.bank_wait(bank)
                    tmm = b.mm_chain(bank, 512,
                                     [(kvf[:, a, k, :], wkvb[:, k, n0:n0 + 512]) for k in range(4)])
                    eng = "vector" if ci % 2 == 0 else "scalar"
                    p.waitt(eng, tmm)
                    pvw = bank[:].rearrange("p (h c) -> p h c", c=256)
                    vdst = VS[:, t, :].rearrange("p (h x) -> p h x", x=128)[:, 2 * ci:2 * ci + 2, :]
                    if eng == "vector":
                        p.op(eng, lambda E, aa=a, h0=2 * ci, q=pvw:
                             E.tensor_copy(knb[:, aa, h0:h0 + 2, :], q[:, :, 0:128]))
                        tev = p.op(eng, lambda E, q=pvw, vd=vdst:
                                   E.tensor_copy(vd, q[:, :, 128:256]), tick=True)
                    else:
                        p.op(eng, lambda E, aa=a, h0=2 * ci, q=pvw:
                             E.activation(out=knb[:, aa, h0:h0 + 2, :], in_=q[:, :, 0:128], func=AF.Copy))
                        tev = p.op(eng, lambda E, q=pvw, vd=vdst:
                                   E.activation(out=vd, in_=q[:, :, 128:256], func=AF.Copy),
                                   tick=True)
                    b.bank_set(bank, tev)
                b.setr(f"kvf{a}", b.pet())
                t_ev_v, t_ev_a = b.dvt(), b.act()
                p.waitt("tensor", t_ev_v)
                p.waitt("tensor", t_ev_a)
                for grp in range(2):
                    items = [(knb[:, a, grp * 8 + j, :], 128, j * 128) for j in range(8)]
                    dst = KN[:, grp * 8:grp * 8 + 8, t * 128:(t + 1) * 128]
                    b.transposes(items, b.psT[grp], ident_bf[:], dst)
                b.setr(f"knb{a}", b.pet())
                b.bank_wait(b.psT[0])
                p.op("tensor", lambda E, aa=a: E.transpose(b.psT[0][0:64, 0:128], kpi[:, aa, :], ident_bf[:]),
                     tick=True)
                tpe = b.pet()
                b.setr(f"kvi{a}", tpe)
                p.waitt("vector", tpe)
                tev = p.op("vector", lambda E, tt=t: E.tensor_copy(KP[0:64, tt * 128:(tt + 1) * 128],
                                                                   b.psT[0][0:64, 0:128]), tick=True)
                b.bank_set(b.psT[0], tev)
            b.fence()
            sc.close()

            if not DO_B:
                return
            sc = b.scope()
            q_in = sc.alloc("q_in", [128, 2, NQ], BF16)
            Qn = sc.alloc("Qn", [128, 32, 128], BF16)
            Qp = sc.alloc("Qp", [128, 32, 128], BF16)
            sc_f = sc.alloc("sc_f", [128, 2, 2, S], BF16)
            P_sb = sc.alloc("P_sb", [128, 4, 512], BF16)
            PT_sb = sc.alloc("PT", [128, 2, 4, 128], BF16)
            mtl = sc.alloc("mtl", [128, 2, 8], F32)
            nmw = sc.alloc("nmw", [128, 2, 2], F32)
            rs8 = sc.alloc("rs8", [128, 2, 8], F32)
            lsr = sc.alloc("lsr", [128, 2, 4], F32)
            atb = sc.alloc("atb", [128, 2, HD], BF16)

            nm_t = {}
            NHW = 2
            NW = 16 // NHW * 2

            def pass1(w, qb):
                u = w % 2
                qt = w // (NW // 2)
                hp = w % (NW // 2)
                q0 = qb * 256 + qt * 128
                nkb = (q0 + 128 + 511) // 512
                sbanks = [b.psR, b.psF, b.psB[0], b.psB[1]]
                for hi in range(NHW):
                    h = hp * NHW + hi
                    hq = h * 2 + qt
                    for kb in range(nkb):
                        ncols = min(512, q0 + 128 - kb * 512)
                        full = (kb * 512 + 512 <= q0 + 1)
                        bank = sbanks[(hi * nkb + kb) % 4]
                        b.bank_wait(bank)
                        p.op("tensor", lambda E, bk=bank, q2=hq, c=ncols, k0=kb * 512:
                             E.matmul(bk[:, 0:c], Qn[:, q2, :], KN[:, q2 // 2, k0:k0 + c],
                                      start=True, stop=False))
                        tsc2 = p.op("tensor", lambda E, bk=bank, q2=hq, c=ncols, k0=kb * 512:
                                    E.matmul(bk[:, 0:c], Qp[0:64, q2, :], KP[0:64, k0:k0 + c],
                                             start=False, stop=True), tick=True)
                        p.waitt("vector", tsc2)
                        if hi == 0 and kb == 0:
                            p.waitr("vector", f"scf{u}")
                        if not full:
                            dlt = ncols - 128
                            p.op("vector", lambda E, bk=bank, d2=dlt:
                                 E.tensor_add(bk[:, d2:d2 + 128], bk[:, d2:d2 + 128], negtri[:]))
                        tev = p.op("vector", lambda E, bk=bank, uu=u, c=ncols, k0=kb * 512, h2=hi, kk=kb:
                                   E.tensor_scalar(out=sc_f[:, uu, h2, k0:k0 + c], in0=bk[:, 0:c],
                                                   scalar1=1.0, scalar2=NEG,
                                                   op0=ALU.mult, op1=ALU.max,
                                                   accum_out=mtl[:, uu, h2 * 4 + kk:h2 * 4 + kk + 1]),
                                   tick=True)
                        b.bank_set(bank, tev)
                p.op("vector", lambda E, uu=u, nk=nkb:
                     E.reduce_max(nmw[:, uu, :], mtl[:, uu, :].rearrange("p (h k) -> p h k", k=4)[:, :, 0:nk],
                                  axis=AX.X, negate=True))
                nm_t[w] = b.dvt()

            def pass2(w, qb):
                u = w % 2
                qt = w // (NW // 2)
                hp = w % (NW // 2)
                q0 = qb * 256 + qt * 128
                nkb = (q0 + 128 + 511) // 512
                p.waitt("scalar", nm_t[w])
                vbanks = [b.psA[0], b.psA[1]]

                def pv(hi, kb, u2):
                    h = hp * NHW + hi
                    bank2 = vbanks[hi % 2]
                    ncols = min(512, q0 + 128 - kb * 512)
                    nsub = ncols // 128
                    if kb == 0:
                        b.bank_wait(bank2)
                    p.waitt("tensor", pt_t[u2])
                    for sub in range(nsub):
                        st2 = (kb == 0 and sub == 0)
                        sp2 = (kb == nkb - 1 and sub == nsub - 1)
                        p.op("tensor", lambda E, bk=bank2, u3=u2, s3=sub, h2=h, kt=kb * 4 + sub,
                             st3=st2, sp3=sp2:
                             E.matmul(bk[:, 0:128], PT_sb[:, u3, s3, :],
                                      VS[:, kt, h2 * DV:(h2 + 1) * DV], start=st3, stop=sp3),
                             tick=sp2)
                    b.setr(f"PT{u2}", b.pet())

                units = [(hi, kb) for hi in range(NHW) for kb in range(nkb)]
                pt_t = {}
                t_exp = {}

                def emit_exp(idx):
                    hi, kb = units[idx]
                    ncols = min(512, q0 + 128 - kb * 512)
                    u4 = idx % 4
                    p.waitr("scalar", f"P{u4}")
                    t_exp[idx] = p.op(
                        "scalar", lambda E, uu=u, u3=u4, c=ncols, k0=kb * 512, h2=hi, kk=kb:
                        E.activation(out=P_sb[:, u3, 0:c], in_=sc_f[:, uu, h2, k0:k0 + c],
                                     func=AF.Exp, bias=nmw[:, uu, h2:h2 + 1],
                                     accum_out=rs8[:, uu, h2 * 4 + kk:h2 * 4 + kk + 1]), tick=True)

                for i2 in range(min(2, len(units))):
                    emit_exp(i2)
                prev = None
                for idx, (hi, kb) in enumerate(units):
                    ncols = min(512, q0 + 128 - kb * 512)
                    nsub = ncols // 128
                    u2 = idx % 2
                    u4 = idx % 4
                    items = [(P_sb[:, u4, j * 128:(j + 1) * 128], 128, j * 128) for j in range(nsub)]
                    p.waitt("tensor", t_exp[idx])
                    pt_t[u2] = b.transposes(items, b.psT[u2], ident_bf[:], PT_sb[:, u2, 0:nsub, :],
                                            pre_evac_wait=b.ring.get(f"PT{u2}"),
                                            eng="vector" if u2 == 0 else "scalar")
                    b.setr(f"P{u4}", b.pet())
                    if idx + 2 < len(units):
                        emit_exp(idx + 2)
                    if prev is not None:
                        pv(*prev)
                    prev = (hi, kb, u2)
                pv(*prev)
                tpv = b.pet()
                p.waitt("vector", tpv)
                p.op("vector", lambda E, uu=u, nk=nkb:
                     E.reduce_sum(lsr[:, uu, 0:NHW], rs8[:, uu, :].rearrange("p (h k) -> p h k", k=4)[:, :, 0:nk],
                                  axis=AX.X))
                p.op("vector", lambda E, uu=u: E.reciprocal(lsr[:, uu, NHW:2 * NHW], lsr[:, uu, 0:NHW]))
                p.waitr("vector", "atb")
                for hi in range(NHW):
                    h = hp * NHW + hi
                    tat = p.op("vector", lambda E, uu=u, qt2=qt, h2=h, h3=hi, bk=vbanks[hi % 2]:
                               E.tensor_scalar(out=atb[:, qt2, h2 * DV:(h2 + 1) * DV], in0=bk[:, 0:128],
                                               scalar1=lsr[:, uu, NHW + h3:NHW + h3 + 1], scalar2=None,
                                               op0=ALU.mult), tick=True)
                    b.bank_set(vbanks[hi % 2], tat)
                b.setr(f"scf{u}", b.act())
                return tat

            for qb in range(8 if DEV_QB is None else DEV_QB):
                q0b = qb * 256
                p.waitr("sync", "q_in")
                tq = p.dma("sync", d_q[qb % 2], q_in[:],
                           Q_hbm[s, q0b:q0b + 256, :].rearrange("(a p) n -> p a n", p=128))
                p.waitt("tensor", tq)
                for grp in range(4):
                    items = []
                    for i in range(8):
                        hq = grp * 8 + i
                        h, qt = divmod(hq, 2)
                        items.append((q_in[:, qt, h * (DN + DR):h * (DN + DR) + DN], 128, i * 128))
                    b.transposes(items, b.psT[grp % 2], ident_bf[:], Qn[:, grp * 8:grp * 8 + 8, :],
                                 pre_evac_wait=b.ring.get("Qfm"))
                for grp in range(4):
                    items = []
                    for i in range(8):
                        hq = grp * 8 + i
                        h, qt = divmod(hq, 2)
                        items.append((q_in[:, qt, h * (DN + DR) + DN:h * (DN + DR) + DN + DR], 64, i * 128))
                    b.transposes(items, b.psT[grp % 2], ident_bf[:], Qp[0:64, grp * 8:grp * 8 + 8, :],
                                 pre_evac_wait=b.ring.get("Qfm"))
                b.setr("q_in", b.pet())
                for w in range(NW):
                    pass1(w, qb)
                    if w >= 1:
                        pass2(w - 1, qb)
                t_last = pass2(NW - 1, qb)
                p.waitt("sync", b.dvt())
                p.waitr("sync", "atb")
                tst = p.dma("sync", d_sAt,
                            attn_hbm[s, q0b:q0b + 256, :].rearrange("(a p) n -> p a n", p=128), atb[:])
                b.setr("atb", tst)
                b.setr("Qfm", b.pet())
            tp = tapt(f"atb_s{s}", [128, 2, HD], BF16)
            if tp is not None:
                p.dma("sync", d_st, tp, atb[:])
            b.fence()
            sc.close()
            b.ring.clear()
        for s in range(SEQ if DEV_SEQ is None else DEV_SEQ):
            _seq_body(s)
        scK.close()

    if "B2" in BUILD:
        b.fence()
        sc = b.scope()
        wo = sc.alloc("wo", [128, 16, HD], BF16)
        ati = sc.alloc("ati", [128, 2, HD], BF16)
        afm = sc.alloc("afm", [128, 2, 16, 128], BF16)
        xr = sc.alloc("xr", [128, 2, HD], F32)
        x1 = sc.alloc("x1", [128, 2, HD], F32)
        x1b = sc.alloc("x1b", [128, 2, HD], BF16)
        x1fm = sc.alloc("x1fm", [128, 2, 16, 128], F32)
        sq2 = sc.alloc("sq2", [128, HD], F32)
        gwsb = sc.alloc("gwsb", [128, 16, EXP], F32)
        lg = sc.alloc("lg", [128, EXP], F32)
        lgr = sc.alloc("lgr", [128, EXP], F32)
        tv8 = sc.alloc("tv8", [128, 8], F32)
        ti8 = sc.alloc("ti8", [128, 8], U32)
        ex8 = sc.alloc("ex8", [128, 8], F32)
        sm8 = sc.alloc("sm8", [128, 4], F32)
        lnst = sc.alloc("lnst", [128, 12], F32)
        I8a = sc.alloc("I8a", [128, NTT, EXP], F32)
        idx8f = sc.alloc("idx8f", [128, NTT, 8], F32)
        pv_all = sc.alloc("pv_all", [128, NTT, 8, 2], F32)
        rm4 = sc.alloc("rm4", [128, 128], F32)
        dchain = sc.alloc("dchain", [128, 2, 128], F32)
        smap4 = sc.alloc("smap4", [128, 128], F32)
        pmap4 = sc.alloc("pmap4", [128, 128], F32)
        scrv = sc.alloc("scrv", [128, EXP], F32)
        scrg = sc.alloc("scrg", [128, EXP], F32)
        slotf = sc.alloc("slotf", [128, 8], F32)
        posf = sc.alloc("posf", [128, 8], F32)
        posu = sc.alloc("posu", [128, 2, 8], U32)
        cnt32 = sc.alloc("cnt32", [128, EXP], F32)
        base32 = sc.alloc("base32", [128, EXP], F32)
        pois2 = sc.alloc("pois2", [1, EXP], F32)
        baseB = sc.alloc("baseB", [1, NTT * EXP], F32)
        cntrow = sc.alloc("cntrow", [1, NTT * EXP], F32)

        p.wait("gpsimd", d_z, d_z.n)
        tw = p.dma("gpsimd", d_w, wo[:], w_o.rearrange("(c p) n -> p c n", p=128))
        tgw = p.dma("sync", d_cs2, gwsb[:], gate_w.rearrange("(c p) n -> p c n", p=128))
        t_tok = p.op("vector", lambda E: E.tensor_copy(pv_all[:, :, :, 0], tok_sb[:]), tick=True)
        p.waitt("tensor", tw)
        p.waitt("tensor", tgw)

        NG2 = NTT if DEV_G is None else DEV_G

        def b2_stage1(g):
            s, t = divmod(g, 16)
            a = g % 2
            p.waitr("sync", f"ati{a}")
            ta = p.dma("sync", d_q[a], ati[:, a, :], attn_hbm[s, t * 128:(t + 1) * 128, :])
            p.waitr("sync", f"xr{a}")
            txr = p.dma("sync", d_x[a], xr[:, a, :], xin[s, t * 128:(t + 1) * 128, :])
            p.waitt("tensor", ta)
            for grp in range(2):
                items = [(ati[:, a, (grp * 8 + j) * 128:(grp * 8 + j + 1) * 128], 128, j * 128)
                         for j in range(8)]
                b.transposes(items, b.psT[grp], ident_bf[:], afm[:, a, grp * 8:grp * 8 + 8, :],
                             pre_evac_wait=b.ring.get(f"afm{a}"))
            b.setr(f"ati{a}", b.pet())
            t_afm = b.dvt()
            p.waitt("tensor", t_afm)
            p.waitt("vector", txr)
            p.waitr("vector", f"x1{a}")
            p.waitr("vector", f"x1b{a}")
            for ci in range(4):
                n0 = ci * 512
                bank = b.psA[ci % 2]
                b.bank_wait(bank)
                tmm = b.mm_chain(bank, 512,
                                 [(afm[:, a, k, :], wo[:, k, n0:n0 + 512]) for k in range(16)])
                p.waitt("vector", tmm)
                tev = p.op("vector", lambda E, aa=a, m0=n0, bk=bank, c2=ci:
                           E.scalar_tensor_tensor(out=x1[:, aa, m0:m0 + 512], in0=bk[:],
                                                  scalar=1.0, in1=xr[:, aa, m0:m0 + 512],
                                                  op0=ALU.mult, op1=ALU.add,
                                                  accum_out=lnst[:, c2:c2 + 1]), tick=True)
                b.bank_set(bank, tev)
            b.setr(f"afm{a}", b.pet())
            b.setr(f"xr{a}", b.dvt())
            p.op("vector", lambda E: E.tensor_add(lnst[:, 4:5], lnst[:, 0:1], lnst[:, 1:2]))
            p.op("vector", lambda E: E.tensor_add(lnst[:, 5:6], lnst[:, 2:3], lnst[:, 3:4]))
            t_resid = p.op("vector", lambda E: E.tensor_add(lnst[:, 6:7], lnst[:, 4:5], lnst[:, 5:6]), tick=True)
            p.waitt("scalar", t_resid)
            t_s2 = p.op("scalar", lambda E, aa=a: E.activation(out=sq2[:], in_=x1[:, aa, :],
                                                               func=AF.Square, accum_out=lnst[:, 7:8]), tick=True)
            p.op("vector", lambda E: E.tensor_scalar(out=lnst[:, 8:9], in0=lnst[:, 6:7],
                                                     scalar1=1.0 / HD, scalar2=None, op0=ALU.mult))
            p.waitt("vector", t_s2)
            p.op("vector", lambda E: E.tensor_scalar(out=lnst[:, 9:10], in0=lnst[:, 7:8],
                                                     scalar1=1.0 / HD, scalar2=None, op0=ALU.mult))
            p.op("vector", lambda E: E.tensor_mul(lnst[:, 10:11], lnst[:, 8:9], lnst[:, 8:9]))
            p.op("vector", lambda E: E.scalar_tensor_tensor(out=lnst[:, 10:11], in0=lnst[:, 10:11],
                                                            scalar=-1.0, in1=lnst[:, 9:10],
                                                            op0=ALU.mult, op1=ALU.add))
            t_ve = p.op("vector", lambda E: E.tensor_scalar(out=lnst[:, 11:12], in0=lnst[:, 10:11],
                                                            scalar1=1.0, scalar2=1e-5,
                                                            op0=ALU.mult, op1=ALU.add), tick=True)
            p.waitt("scalar", t_ve)
            t_sqr = p.op("scalar", lambda E: E.activation(out=lnst[:, 10:11], in_=lnst[:, 11:12],
                                                          func=AF.Sqrt), tick=True)
            p.waitt("vector", t_sqr)
            p.op("vector", lambda E: E.reciprocal(lnst[:, 11:12], lnst[:, 10:11]))
            p.op("vector", lambda E, aa=a: E.tensor_scalar(out=x1[:, aa, :], in0=x1[:, aa, :],
                                                           scalar1=lnst[:, 8:9], scalar2=lnst[:, 11:12],
                                                           op0=ALU.subtract, op1=ALU.mult))
            t_x1 = p.op("vector", lambda E, aa=a: E.tensor_copy(x1b[:, aa, :], x1[:, aa, :]), tick=True)
            p.waitt("sync", t_x1)
            p.dma("sync", d_sx1[a], x1f_hbm[g], x1[:, a, :])
            tst = p.dma("sync", d_sx1[a], x1b_hbm[g * 128:(g + 1) * 128, :], x1b[:, a, :])
            b.setr(f"x1b{a}", tst)
            p.waitt("tensor", t_x1)
            for grp in range(4):
                bank = b.psF if grp % 2 == 0 else b.psR
                b.bank_wait(bank)
                for i in range(4):
                    c2 = grp * 4 + i
                    p.op("tensor", lambda E, aa=a, cc=c2, o=i * 128, bk=bank, st=(i == 0), sp=(i == 3):
                         E.matmul(bk[:, o:o + 128], x1[:, aa, cc * 128:(cc + 1) * 128], ident_f[:],
                                  start=st, stop=sp, is_transpose=True),
                         tick=(i == 3))
                tpe = b.pet()
                p.waitt("vector", tpe)
                if grp < 2:
                    p.waitr("vector", f"x1fm{a}")
                tev = p.op("vector", lambda E, aa=a, g2=grp, bk=bank:
                           E.tensor_copy(x1fm[:, aa, g2 * 4:g2 * 4 + 4, :],
                                         bk[:].rearrange("p (c x) -> p c x", c=4)), tick=True)
                b.bank_set(bank, tev)
            t_fm = b.dvt()
            b.setr(f"x1{a}", b.pet())
            p.waitt("tensor", t_fm)
            bank = b.psR
            b.bank_wait(bank)
            tlg = b.mm_chain(bank, EXP, [(x1fm[:, a, k, :], gwsb[:, k, :]) for k in range(16)])
            b.setr(f"x1fm{a}", tlg)
            p.waitt("vector", tlg)
            tev = p.op("vector", lambda E, bk=bank: E.tensor_copy(lg[:], bk[:, 0:EXP]), tick=True)
            b.bank_set(bank, tev)
            p.waitr("vector", "tv8")
            p.op("vector", lambda E: E.max(tv8[:], lg[:]))
            p.op("vector", lambda E: E.max_index(ti8[:], tv8[:], lg[:]))
            p.op("vector", lambda E, g2=g: E.tensor_copy(idx8f[:, g2, :], ti8[:]))
            p.op("vector", lambda E: E.match_replace(lgr[:], tv8[:], lg[:], NEG))
            t_i8g = p.op("vector", lambda E, g2=g: E.tensor_tensor(I8a[:, g2, :], lg[:], lgr[:],
                                                                   ALU.not_equal), tick=True)
            t_nm8 = p.op("vector", lambda E: E.tensor_scalar(out=sm8[:, 0:1], in0=tv8[:, 0:1],
                                                             scalar1=-1.0, scalar2=None, op0=ALU.mult),
                         tick=True)
            p.waitt("scalar", t_nm8)
            p.waitr("scalar", "ex8")
            t_e8 = p.op("scalar", lambda E: E.activation(out=ex8[:], in_=tv8[:], func=AF.Exp,
                                                         bias=sm8[:, 0:1], accum_out=sm8[:, 1:2]), tick=True)
            b.setr("tv8", t_e8)
            p.waitt("vector", t_e8)
            p.op("vector", lambda E: E.reciprocal(sm8[:, 2:3], sm8[:, 1:2]))
            t_w8 = p.op("vector", lambda E, g2=g: E.tensor_scalar(out=pv_all[:, g2, :, 1], in0=ex8[:],
                                                                  scalar1=sm8[:, 2:3], scalar2=None,
                                                                  op0=ALU.mult), tick=True)
            b.setr("ex8", t_w8)
            p.waitt("tensor", t_i8g)
            p.op("tensor", lambda E, g2=g: E.matmul(
                b.psB[g2 // 16][0:1, (g2 % 16) * EXP:(g2 % 16 + 1) * EXP], ones_col[:],
                I8a[:, g2, :], start=True, stop=True),
                 tick=(g == NG2 - 1))

        for g in range(NG2):
            b2_stage1(g)

        t_cntmm = b.pet()
        p.waitt("vector", t_cntmm)
        p.op("vector", lambda E: E.tensor_copy(cntrow[0:1, 0:512], b.psB[0][0:1, 0:512]))
        t_cr = p.op("vector", lambda E: E.tensor_copy(cntrow[0:1, 512:1024], b.psB[1][0:1, 0:512]),
                    tick=True)
        b.bank_set(b.psB[0], t_cr)
        b.bank_set(b.psB[1], t_cr)
        p.waitt("sync", t_cr)
        tsc = p.dma("sync", d_st, cnt_hbm.rearrange("a g e -> a (g e)"), cntrow[0:1, :])
        p.waitt("sync", tsc)
        tlc = p.dma("sync", d_st, cnt32[0:32, :], cnt_hbm[0])
        p.waitt("tensor", tlc)
        b.bank_wait(b.psB[1])
        p.op("tensor", lambda E: E.matmul(b.psB[1][0:32, 0:EXP], lt_f[0:32, 0:32], cnt32[0:32, :],
                                          start=True, stop=True))
        t_pfx = p.op("tensor", lambda E: E.matmul(b.psB[1][0:1, EXP:2 * EXP], ones_col[0:32, :],
                                                  cnt32[0:32, :], start=True, stop=True), tick=True)
        p.waitt("vector", t_pfx)
        p.op("vector", lambda E: E.tensor_copy(base32[0:32, :], b.psB[1][0:32, 0:EXP]))
        t_b32 = p.op("vector", lambda E: E.tensor_copy(pois2[0:1, :], b.psB[1][0:1, EXP:2 * EXP]), tick=True)
        b.bank_set(b.psB[1], t_b32)
        p.waitt("sync", t_b32)
        tsb = p.dma("sync", d_st, base_hbm[0], base32[0:32, :])
        p.waitt("sync", tsb)
        tlb = p.dma("sync", d_st, baseB[0:1, :], base_hbm.rearrange("a g e -> a (g e)"))
        p.op("vector", lambda E: E.reduce_max(poisv[0:1, 0:1], pois2[0:1, :], axis=AX.X))
        p.op("vector", lambda E: E.tensor_scalar(out=poisv[0:1, 1:2], in0=poisv[0:1, 0:1],
                                                 scalar1=float(CAPL), scalar2=3.0e38,
                                                 op0=ALU.is_gt, op1=ALU.mult))
        p.op("vector", lambda E: E.tensor_add(poisv[0:1, 2:3], poisv[0:1, 1:2], poisv[0:1, 1:2]))

        p.waitt("tensor", tlb)
        p.waitt("gpsimd", t_tok)
        NGRP = (NG2 + 3) // 4
        for j in range(NGRP):
            tiles = [4 * j + t4 for t4 in range(4) if 4 * j + t4 < NG2]
            W = len(tiles) * 32
            bank = b.psF if j % 2 == 0 else b.psR
            b.bank_wait(bank)
            for t4, g in enumerate(tiles):
                p.op("tensor", lambda E, bk=bank, g2=g, o=t4 * 32:
                     E.matmul(bk[:, o:o + EXP], lt_f[:], I8a[:, g2, :], start=True, stop=False))
                p.op("tensor", lambda E, bk=bank, g2=g, o=t4 * 32:
                     E.matmul(bk[:, o:o + EXP], ones_row[:], baseB[0:1, g2 * EXP:(g2 + 1) * EXP],
                              start=False, stop=True), tick=(t4 == len(tiles) - 1))
            t_mm = b.pet()
            p.waitt("vector", t_mm)
            p.waitr("vector", "pmapg")
            tev = p.op("vector", lambda E, bk=bank, w2=W: E.tensor_copy(rm4[:, 0:w2], bk[:, 0:w2]),
                       tick=True)
            b.bank_set(bank, tev)
            p.op("vector", lambda E, w2=W: E.memset(dchain[:, 0, 0:w2], 0.0))
            for jj in range(1, NCH):
                p.op("vector", lambda E, jj2=jj, w2=W: E.scalar_tensor_tensor(
                    out=dchain[:, jj2 % 2, 0:w2], in0=rm4[:, 0:w2], scalar=float(jj2 * 128),
                    in1=dchain[:, (jj2 - 1) % 2, 0:w2], op0=ALU.is_ge, op1=ALU.add))
            dsl = (NCH - 1) % 2
            p.op("vector", lambda E, w2=W, d2=dsl: E.scalar_tensor_tensor(
                out=pmap4[:, 0:w2], in0=dchain[:, d2, 0:w2], scalar=-128.0, in1=rm4[:, 0:w2],
                op0=ALU.mult, op1=ALU.add))
            p.op("vector", lambda E, w2=W, d2=dsl: E.scalar_tensor_tensor(
                out=dchain[:, 1 - d2, 0:w2], in0=iota32x4[:, 0:w2], scalar=float(NCH),
                in1=dchain[:, d2, 0:w2], op0=ALU.mult, op1=ALU.add))
            p.op("vector", lambda E, w2=W, d2=dsl: E.scalar_tensor_tensor(
                out=pmap4[:, 0:w2], in0=pmap4[:, 0:w2], scalar=float(EXP * NCH),
                in1=dchain[:, 1 - d2, 0:w2], op0=ALU.mult, op1=ALU.add))
            t_pm = p.op("vector", lambda E, w2=W: E.scalar_tensor_tensor(
                out=smap4[:, 0:w2], in0=iota32x4[:, 0:w2], scalar=float(CAPL),
                in1=rm4[:, 0:w2], op0=ALU.mult, op1=ALU.add), tick=True)
            t_lastg = None
            for t4, g in enumerate(tiles):
                a = g % 2
                for k in range(8):
                    p.op("vector", lambda E, g2=g, kk=k, o=t4 * 32: E.scalar_tensor_tensor(
                        out=scrv[:], in0=iota32[:], scalar=idx8f[:, g2, kk:kk + 1],
                        in1=smap4[:, o:o + EXP], op0=ALU.is_equal, op1=ALU.mult,
                        accum_out=slotf[:, kk:kk + 1]))
                    p.op("vector", lambda E, g2=g, kk=k, o=t4 * 32: E.scalar_tensor_tensor(
                        out=scrg[:], in0=iota32[:], scalar=idx8f[:, g2, kk:kk + 1],
                        in1=pmap4[:, o:o + EXP], op0=ALU.is_equal, op1=ALU.mult,
                        accum_out=posf[:, kk:kk + 1]))
                p.op("vector", lambda E, g2=g: E.tensor_copy(slot_all[:, g2, :], slotf[:]))
                p.waitr("vector", f"posu{a}")
                t_lastg = p.op("vector", lambda E, aa=a: E.tensor_copy(posu[:, aa, :], posf[:]), tick=True)
                p.waitt("gpsimd", t_lastg)
                for k in range(8):
                    b.scatter(d_sc[a], dtw_hbm, pv_all[:, g, k, :], posu[:, a, k:k + 1], NSLOT - 1)
                b.setr(f"posu{a}", (d_sc[a], d_sc[a].n))
            b.setr("pmapg", t_lastg)

        b.fence()
        sc.close()
        b.ring.clear()

    if "C" in BUILD:
        b.fence()
        sc = b.scope()
        wg = sc.alloc("wg", [128, 2, 16, INTER], BF16)
        wu = sc.alloc("wu", [128, 2, 16, INTER], BF16)
        wd = sc.alloc("wd", [128, 6, HD], BF16)
        xg = sc.alloc("xg", [128, 4, HD], BF16)
        xfc = sc.alloc("xfc", [128, 2, 16, 512], BF16)
        gsb = sc.alloc("gsb", [128, 2, 512], BF16)
        actc = sc.alloc("actc", [128, 2, 6, 512], BF16)
        ysb = sc.alloc("ysb", [128, 2, HD], BF16)
        dtw = sc.alloc("dtw", [128, NCH * EXP, 2], F32)
        didx = sc.alloc("didx", [128, NCH * EXP], U32)

        tld = p.dma("sync", d_cs2, dtw[:], dtw_hbm.rearrange("(p j) c -> p j c", p=128))
        p.waitt("vector", tld)
        t_didx = p.op("vector", lambda E: E.tensor_copy(didx[:], dtw[:, :, 0]), tick=True)
        p.waitt("gpsimd", t_didx)
        p.waitt("scalar", tld)

        stgc = sc.alloc("stgc", [128, 2, INTER], F32)
        wgv = w_gate.rearrange("e (c p) i -> e p c i", p=128)
        wuv = w_up.rearrange("e (c p) i -> e p c i", p=128)
        wdv = w_down.rearrange("e (c p) i -> e p c i", p=128)

        t_wg = {}
        t_wu = {}
        CJ = {"jobs": [], "i": 0}

        def cast_add(e):
            we2 = e % 2
            for j in range(16):
                CJ["jobs"].append((wgv[e, :, j, :], wg[:, we2, j, :], ("wg", e, j)))
                CJ["jobs"].append((wuv[e, :, j, :], wu[:, we2, j, :], ("wu", e, j)))

        def cast_pump(n):
            while n > 0 and CJ["i"] < len(CJ["jobs"]):
                src, dst, mark = CJ["jobs"][CJ["i"]]
                sl = CJ["i"] % 2
                CJ["i"] += 1
                n -= 1
                eng = "scalar" if sl else "vector"
                p.waitr("sync", f"cstg{sl}")
                tld2 = p.dma("sync", d_wc[sl], stgc[:, sl, 0:src.shape[-1]], src)
                if mark[2] == 0:
                    p.waitr(eng, f"{mark[0]}{mark[1] % 2}")
                p.waitt(eng, tld2)
                if eng == "scalar":
                    tc = p.op(eng, lambda E, d=dst, s2=stgc[:, sl, 0:dst.shape[-1]]:
                              E.activation(out=d, in_=s2, func=AF.Copy), tick=True)
                else:
                    tc = p.op(eng, lambda E, d=dst, s2=stgc[:, sl, 0:dst.shape[-1]]:
                              E.tensor_copy(d, s2), tick=True)
                b.setr(f"cstg{sl}", tc)
                (t_wg if mark[0] == "wg" else t_wu)[mark[1]] = tc

        cast_add(0)
        cast_pump(32)

        for e in range(EXP if DEV_E is None else DEV_E):
            we = e % 2
            p.waitr("gpsimd", "wd")
            t_wd = p.dma("gpsimd", d_wd, wd[:], wdv[e])
            if e + 1 < EXP:
                cast_add(e + 1)
            for c3 in range(3):
                csz = 512 if c3 < 2 else 256
                nsub = csz // 128
                ring = c3 % 2
                for sub in range(nsub):
                    j = e * NCH + c3 * 4 + sub
                    gslot = (c3 * 4 + sub) % 4
                    p.waitr("gpsimd", f"xg{gslot}")
                    tgt = b.gather(d_g[gslot], xg[:, gslot, :], x1b_hbm, didx[:, j:j + 1], TLOC - 1)
                    p.waitt("tensor", tgt)
                    if sub == 0:
                        p.waitr("tensor", f"xfc{ring}")
                    for grp in range(2):
                        items = [(xg[:, gslot, (grp * 8 + i) * 128:(grp * 8 + i + 1) * 128], 128, i * 128)
                                 for i in range(8)]
                        b.transposes(items, b.psT[grp], ident_bf[:],
                                     xfc[:, ring, grp * 8:grp * 8 + 8, sub * 128:(sub + 1) * 128])
                    b.setr(f"xg{gslot}", b.pet())
                t_xf = b.dvt()
                p.waitt("tensor", t_xf)
                p.waitt("tensor", t_wg[e])
                p.waitt("tensor", t_wu[e])
                for m in range(6):
                    cast_pump(2)
                    bg = b.psA[m % 2]
                    bu = b.psB[m % 2]
                    gslot2 = m % 2
                    b.bank_wait(bg)
                    tg2 = b.mm_chain(bg, csz, [(wg[:, we, k, m * 128:(m + 1) * 128], xfc[:, ring, k, 0:csz])
                                               for k in range(16)])
                    b.bank_wait(bu)
                    tu2 = b.mm_chain(bu, csz, [(wu[:, we, k, m * 128:(m + 1) * 128], xfc[:, ring, k, 0:csz])
                                               for k in range(16)])
                    p.waitt("scalar", tg2)
                    p.waitr("scalar", f"gsb{gslot2}")
                    tsi = p.op("scalar", lambda E, bk=bg, c=csz, gs=gslot2:
                               E.activation(out=gsb[:, gs, 0:c], in_=bk[:, 0:c], func=AF.Silu), tick=True)
                    b.bank_set(bg, tsi)
                    p.waitt("vector", tsi)
                    p.waitt("vector", tu2)
                    p.waitr("vector", f"act{ring}_{m}")
                    tmu = p.op("vector", lambda E, bk=bu, c=csz, mm=m, rr=ring, gs=gslot2:
                               E.tensor_tensor(actc[:, rr, mm, 0:c], gsb[:, gs, 0:c], bk[:, 0:c], ALU.mult),
                               tick=True)
                    b.bank_set(bu, tmu)
                    b.setr(f"gsb{gslot2}", tmu)
                t_act = b.dvt()
                p.waitt("tensor", t_act)
                p.waitt("tensor", t_wd)
                for sub in range(nsub):
                    yring = sub % 2
                    j = e * NCH + c3 * 4 + sub
                    for ci in range(4):
                        bank = b.psF if ci % 2 == 0 else b.psR
                        b.bank_wait(bank)
                        tdn = b.mm_chain(bank, 512,
                                         [(actc[:, ring, m, sub * 128:(sub + 1) * 128],
                                           wd[:, m, ci * 512:(ci + 1) * 512]) for m in range(6)])
                        p.waitt("scalar", tdn)
                        if ci == 0:
                            p.waitr("scalar", f"ysb{yring}")
                        tev = p.op("scalar", lambda E, bk=bank, yr=yring, c2=ci, jj=j:
                                   E.activation(out=ysb[:, yr, c2 * 512:(c2 + 1) * 512], in_=bk[:],
                                                func=AF.Copy, scale=dtw[:, jj, 1:2]), tick=True)
                        b.bank_set(bank, tev)
                    t_ysb = b.act()
                    p.waitt("sync", t_ysb)
                    r0 = e * CAPL + (c3 * 4 + sub) * 128
                    tst = p.dma("sync", d_y[yring], y_hbm[r0:r0 + 128, :], ysb[:, yring, :])
                    b.setr(f"ysb{yring}", tst)
                b.setr(f"xfc{ring}", t_act)
                for mm in range(6):
                    b.setr(f"act{ring}_{mm}", b.pet())
            b.setr(f"wg{we}", b.pet())
            b.setr(f"wu{we}", b.pet())
            b.setr("wd", b.pet())
        b.fence()
        sc.close()
        b.ring.clear()

    if "D" in BUILD:
        b.fence()
        sc = b.scope()
        y8 = sc.alloc("y8", [128, 2, 8, HD], BF16)
        accX = sc.alloc("accX", [128, 2, HD], F32)
        x2f = sc.alloc("x2f", [128, 2, HD], F32)
        outb = sc.alloc("outb", [128, 2, HD], F32)
        sq3 = sc.alloc("sq3", [128, HD], F32)
        ln3 = sc.alloc("ln3", [128, 2, 12], F32)

        banks4 = [b.psA[0], b.psA[1], b.psB[0], b.psB[1]]
        tg_all = {}

        def d_issue(g):
            a = g % 2
            p.waitr("gpsimd", f"y8{a}")
            tg_all[g] = [b.gather(d_ga[k], y8[:, a, k, :], y_hbm, slot_all[:, g, k:k + 1], NSLOT - 1)
                         for k in range(8)]
            p.waitr("sync", f"accX{a}")
            tg_all[(g, "x")] = p.dma("sync", d_al[a], accX[:, a, :], x1f_hbm[g])

        def d_proc(g):
            a = g % 2
            for k in range(8):
                p.waitt("tensor", tg_all[g][k])
            for c in range(4):
                bank = banks4[c]
                b.bank_wait(bank)
                for k in range(8):
                    p.op("tensor", lambda E, bk=bank, aa=a, kk=k, cc=c, st=(k == 0), sp=(k == 7):
                         E.matmul(bk[:, 0:512], ident_bf[:], y8[:, aa, kk, cc * 512:(cc + 1) * 512],
                                  start=st, stop=sp), tick=(k == 7))
                tmm = b.pet()
                p.waitt("vector", tmm)
                if c == 0:
                    p.waitt("vector", tg_all[(g, "x")])
                    p.waitr("vector", f"x2f{a}")
                tev = p.op("vector", lambda E, bk=bank, aa=a, cc=c:
                           E.scalar_tensor_tensor(out=x2f[:, aa, cc * 512:(cc + 1) * 512], in0=bk[:],
                                                  scalar=1.0, in1=accX[:, aa, cc * 512:(cc + 1) * 512],
                                                  op0=ALU.mult, op1=ALU.add,
                                                  accum_out=ln3[:, aa, cc:cc + 1]), tick=True)
                b.bank_set(bank, tev)
            b.setr(f"y8{a}", b.pet())
            b.setr(f"accX{a}", b.dvt())
            p.op("vector", lambda E, aa=a: E.tensor_add(ln3[:, aa, 4:5], ln3[:, aa, 0:1], ln3[:, aa, 1:2]))
            p.op("vector", lambda E, aa=a: E.tensor_add(ln3[:, aa, 5:6], ln3[:, aa, 2:3], ln3[:, aa, 3:4]))
            t_sum = p.op("vector", lambda E, aa=a: E.tensor_add(ln3[:, aa, 6:7], ln3[:, aa, 4:5],
                                                                ln3[:, aa, 5:6]), tick=True)
            p.waitt("scalar", t_sum)
            t_s2 = p.op("scalar", lambda E, aa=a: E.activation(out=sq3[:], in_=x2f[:, aa, :],
                                                               func=AF.Square, accum_out=ln3[:, aa, 7:8]),
                        tick=True)
            p.op("vector", lambda E, aa=a: E.tensor_scalar(out=ln3[:, aa, 8:9], in0=ln3[:, aa, 6:7],
                                                           scalar1=1.0 / HD, scalar2=None, op0=ALU.mult))
            p.waitt("vector", t_s2)
            p.op("vector", lambda E, aa=a: E.tensor_scalar(out=ln3[:, aa, 9:10], in0=ln3[:, aa, 7:8],
                                                           scalar1=1.0 / HD, scalar2=None, op0=ALU.mult))
            p.op("vector", lambda E, aa=a: E.tensor_mul(ln3[:, aa, 10:11], ln3[:, aa, 8:9], ln3[:, aa, 8:9]))
            p.op("vector", lambda E, aa=a: E.scalar_tensor_tensor(out=ln3[:, aa, 10:11], in0=ln3[:, aa, 10:11],
                                                                  scalar=-1.0, in1=ln3[:, aa, 9:10],
                                                                  op0=ALU.mult, op1=ALU.add))
            t_ve = p.op("vector", lambda E, aa=a: E.tensor_scalar(out=ln3[:, aa, 11:12], in0=ln3[:, aa, 10:11],
                                                                  scalar1=1.0, scalar2=1e-5,
                                                                  op0=ALU.mult, op1=ALU.add), tick=True)
            p.waitt("scalar", t_ve)
            t_sq = p.op("scalar", lambda E, aa=a: E.activation(out=ln3[:, aa, 10:11], in_=ln3[:, aa, 11:12],
                                                               func=AF.Sqrt), tick=True)
            p.waitt("vector", t_sq)
            p.op("vector", lambda E, aa=a: E.reciprocal(ln3[:, aa, 11:12], ln3[:, aa, 10:11]))
            p.waitr("vector", f"outb{a}")
            t_o = p.op("vector", lambda E, aa=a: E.tensor_scalar(
                out=outb[:, aa, :], in0=x2f[:, aa, :], scalar1=ln3[:, aa, 8:9], scalar2=ln3[:, aa, 11:12],
                op0=ALU.subtract, op1=ALU.mult), tick=True)
            if g == NTT - 1:
                t_o = p.op("vector", lambda E, aa=a: E.tensor_scalar(
                    out=outb[0:1, aa, :], in0=outb[0:1, aa, :], scalar1=poisv[0:1, 2:3],
                    scalar2=None, op0=ALU.add), tick=True)
            b.setr(f"x2f{a}", b.dvt())
            p.waitt("sync", t_o)
            tst = p.dma("sync", d_ob[a], OUT[g * 128:(g + 1) * 128, :], outb[:, a, :])
            b.setr(f"outb{a}", tst)

        ND = NTT if DEV_D is None else DEV_D * 8
        d_issue(0)
        for g in range(1, ND):
            d_issue(g)
            d_proc(g - 1)
        d_proc(ND - 1)
        p.wait("sync", d_ob[0], d_ob[0].n)
        p.wait("sync", d_ob[1], d_ob[1].n)
        sc.close()

    b.taps_map = taps
    return b


def finalize(b):
    nc = b.nc
    with nc.Block() as block:
        @block.sync
        def _(E):
            for fn in b.st["sync"]:
                fn(E)

        @block.gpsimd
        def _(E):
            for fn in b.st["gpsimd"]:
                fn(E)

        @block.scalar
        def _(E):
            for fn in b.st["scalar"]:
                fn(E)

        @block.vector
        def _(E):
            for fn in b.st["vector"]:
                fn(E)

        @block.tensor
        def _(E):
            for fn in b.st["tensor"]:
                fn(E)
    return nc


_CACHE = {}


def _get_program():
    key = (tuple(BUILD), tuple(TAPS))
    if key not in _CACHE:
        bb = build_program()
        _CACHE[key] = (finalize(bb), bb)
    return _CACHE[key]


def make_in_maps(x, wts):
    hc = host_consts()
    base = dict(wts)
    for k, v in hc.items():
        base["c_" + k] = v
    in_maps = []
    for c in range(NCORE):
        m = dict(base)
        m["xin"] = np.ascontiguousarray(x[c * SEQ:(c + 1) * SEQ])
        in_maps.append(m)
    return in_maps


def kernel(L0_attn_hidden, L0_attn_w_kv_a, L0_attn_q_a_gamma, L0_attn_w_qb,
           L0_attn_kv_a_gamma, L0_attn_w_kvb, L0_attn_w_o,
           L0_ffn_gate_w, L0_ffn_w_gate, L0_ffn_w_up, L0_ffn_w_down):
    nc, bb = _get_program()
    wts = {
        "w_kv_a": np.ascontiguousarray(L0_attn_w_kv_a, np.float32),
        "w_qb": np.ascontiguousarray(L0_attn_w_qb, np.float32),
        "w_kvb": np.ascontiguousarray(L0_attn_w_kvb, np.float32),
        "w_o": np.ascontiguousarray(L0_attn_w_o, np.float32),
        "gate_w": np.ascontiguousarray(L0_ffn_gate_w, np.float32),
        "w_gate": np.ascontiguousarray(L0_ffn_w_gate, np.float32),
        "w_up": np.ascontiguousarray(L0_ffn_w_up, np.float32),
        "w_down": np.ascontiguousarray(L0_ffn_w_down, np.float32),
        "gamma_qb": np.ascontiguousarray(np.broadcast_to(
            np.asarray(L0_attn_q_a_gamma, np.float32), (128, QL))),
        "gamma_kvb": np.ascontiguousarray(np.broadcast_to(
            np.asarray(L0_attn_kv_a_gamma, np.float32), (128, KL))),
    }
    x = np.ascontiguousarray(L0_attn_hidden, np.float32)
    in_maps = make_in_maps(x, wts)
    res = run_bass_kernel_spmd(nc, in_maps, core_ids=list(range(NCORE)))
    out = np.concatenate([r["out"].reshape(SEQ, S, HD) for r in res.results], axis=0)
    return out


# version 19
# speedup vs baseline: 1.1415x; 1.0148x over previous
import math
import numpy as np
import concourse.bass as bass
import concourse.mybir as mybir
from concourse.bass import IndirectOffsetOnAxis
from concourse.bass_utils import run_bass_kernel_spmd

F32 = mybir.dt.float32
BF16 = mybir.dt.bfloat16
U32 = mybir.dt.uint32
AF = mybir.ActivationFunctionType
ALU = mybir.AluOpType
AX = mybir.AxisListType

NCORE = 8
SEQ = 2
S = 2048
HD = 2048
NH = 16
DN = 128
DR = 64
DV = 128
QL = 1536
KL = 512
EXP = 32
INTER = 768
CAPL = 1280
NCH = CAPL // 128
NSLOT = EXP * CAPL
TLOC = SEQ * S
NTT = TLOC // 128
NQ = NH * (DN + DR)
NKV = NH * (DN + DV)
FU = QL + KL + DR
SCALE = 1.0 / math.sqrt(DN + DR)
NEG = -3.0e38

BUILD = ("A1", "A2", "A3B", "B2", "C", "D")
TAPS = ()
DEV_SEQ = None
DEV_T = None
DEV_QB = None
DEV_G = None
DEV_E = None
DEV_D = None


def host_consts():
    c = {}
    c["ident"] = np.eye(128, dtype=np.float32)
    c["lt"] = np.triu(np.ones((128, 128), np.float32), 1)
    c["negtri"] = np.triu(np.full((128, 128), NEG, np.float32), 1)
    c["iota32"] = np.broadcast_to(np.arange(32, dtype=np.float32), (128, 32)).copy()
    c["iota32x4"] = np.broadcast_to(np.tile(np.arange(32, dtype=np.float32), 4), (128, 128)).copy()
    tok = np.arange(NTT, dtype=np.float32)[:, None] * 128 + np.arange(128, dtype=np.float32)[None, :]
    c["tok8"] = np.ascontiguousarray(np.broadcast_to(tok.T[:, :, None], (128, NTT, 8)))
    c["iota_f"] = np.arange(128, dtype=np.float32).reshape(128, 1)
    c["ones_col"] = np.ones((128, 1), np.float32)
    c["ones_row"] = np.ones((1, 128), np.float32)
    inv_freq = 1.0 / (10000.0 ** (np.arange(0, DR, 2, dtype=np.float64) / DR))
    ang = np.arange(S, dtype=np.float64)[:, None] * inv_freq[None, :]
    c["cos_t"] = np.concatenate([np.cos(ang)] * 2, axis=1).astype(np.float32)
    c["sin_t"] = np.concatenate([np.sin(ang)] * 2, axis=1).astype(np.float32)
    c["cos16"] = np.ascontiguousarray(np.broadcast_to(c["cos_t"][:, None, :], (S, NH, DR)))
    c["sin16"] = np.ascontiguousarray(np.broadcast_to(c["sin_t"][:, None, :], (S, NH, DR)))
    return c


class Sem:
    __slots__ = ("h", "n")

    def __init__(self, h):
        self.h = h
        self.n = 0


class _Scope:
    def __init__(self, b):
        self.b = b
        self.handles = []

    _uid = [0]

    def alloc(self, name, shape, dtype):
        _Scope._uid[0] += 1
        g = self.b.nc.sbuf_tensor(f"{name}_{_Scope._uid[0]}", shape, dtype)
        h = g.__enter__()
        self.handles.append(g)
        return h

    def close(self):
        for g in reversed(self.handles):
            g.__exit__(None, None, None)
        self.handles = []


class Builder:
    ENGS = ("sync", "gpsimd", "scalar", "vector", "tensor")

    def __init__(self):
        nc = bass.Bass("TRN2", target_bir_lowering=False)
        self.nc = nc
        self.st = {k: [] for k in self.ENGS}
        self._semcache = {}

        self.c_pe = self.sem("c_pe")
        self.c_gp = self.sem("c_gp")
        self.c_dv = self.sem("c_dv")
        self.c_ac = self.sem("c_ac")

        self.psA = [nc.alloc_psum_tensor(f"psA{i}", [128, 512], F32) for i in range(2)]
        self.psB = [nc.alloc_psum_tensor(f"psB{i}", [128, 512], F32) for i in range(2)]
        self.psT = [nc.alloc_psum_tensor(f"psT{i}", [128, 1024], BF16) for i in range(2)]
        self.psR = nc.alloc_psum_tensor("psR", [128, 512], F32)
        self.psF = nc.alloc_psum_tensor("psF", [128, 512], F32)
        self.bankfree = {}
        self.ring = {}

    def sem(self, name):
        if name not in self._semcache:
            self._semcache[name] = Sem(self.nc.alloc_semaphore(name=name))
        return self._semcache[name]

    def all_dma_sems(self):
        return [s for nm, s in self._semcache.items() if nm.startswith("d_")]

    def wait(self, eng, sem, val):
        if sem is None or val is None or val <= 0:
            return
        self.st[eng].append(lambda E, h=sem.h, v=val: E.wait_ge(h, v))

    def waitt(self, eng, tick):
        if tick is not None:
            self.wait(eng, tick[0], tick[1])

    def waitr(self, eng, key):
        self.waitt(eng, self.ring.get(key))

    def setr(self, key, tick):
        self.ring[key] = tick

    def dma(self, eng, sem, out, in_, **kw):
        sem.n += 16
        v = sem.n
        self.st[eng].append(
            lambda E, o=out, i=in_, h=sem.h, k=dict(kw): E.dma_start(out=o, in_=i, **k).then_inc(h, 16))
        return (sem, v)

    @staticmethod
    def _breg(E, bound):
        cache = getattr(E, "_bnd_regs", None)
        if cache is None:
            cache = {}
            E._bnd_regs = cache
        if bound not in cache:
            cache[bound] = E.to_reg(bound)
        return cache[bound]

    def gather(self, sem, out, src, idx, bound, accum=False):
        sem.n += 16
        v = sem.n
        if accum:
            kw = dict(compute_op=ALU.add, oob_is_err=True)
        else:
            kw = dict(oob_is_err=False)
        self.st["gpsimd"].append(
            lambda E, o=out, s2=src, i2=idx, h=sem.h, bb=bound, k=dict(kw):
            E.indirect_dma_start(out=o, out_offset=None, in_=s2,
                                 in_offset=IndirectOffsetOnAxis(i2, 0),
                                 bounds_check=Builder._breg(E, bb), **k).then_inc(h, 16))
        return (sem, v)

    def scatter(self, sem, dst, src, idx, bound):
        sem.n += 16
        v = sem.n
        self.st["gpsimd"].append(
            lambda E, d=dst, s2=src, i2=idx, h=sem.h, bb=bound:
            E.indirect_dma_start(out=d, out_offset=IndirectOffsetOnAxis(i2, 0),
                                 in_=s2, in_offset=None,
                                 bounds_check=Builder._breg(E, bb), oob_is_err=False).then_inc(h, 16))
        return (sem, v)

    def op(self, eng, fn, tick=False):
        if eng in ("vector", "scalar", "gpsimd"):
            sem = {"vector": self.c_dv, "scalar": self.c_ac, "gpsimd": self.c_gp}[eng]
            self.wait(eng, sem, sem.n)
            sem.n += 1
            v = sem.n
            self.st[eng].append(lambda E, f=fn, h=sem.h: f(E).then_inc(h, 1))
            return (sem, v)
        if not tick:
            self.st[eng].append(fn)
            return None
        sem = self.c_pe
        sem.n += 1
        v = sem.n
        self.st[eng].append(lambda E, f=fn, h=sem.h: f(E).then_inc(h, 1))
        return (sem, v)

    def pet(self):
        return (self.c_pe, self.c_pe.n)

    def dvt(self):
        return (self.c_dv, self.c_dv.n)

    def act(self):
        return (self.c_ac, self.c_ac.n)

    def alloc(self, name, shape, dtype):
        return self.nc.alloc_sbuf_tensor(name, shape, dtype)

    def scope(self):
        return _Scope(self)

    def bank_wait(self, bank):
        self.waitt("tensor", self.bankfree.get(bank.name))

    def bank_set(self, bank, tick):
        self.bankfree[bank.name] = tick

    def fence(self, exclude=()):
        marks = [(self.c_pe, self.c_pe.n), (self.c_dv, self.c_dv.n), (self.c_ac, self.c_ac.n),
                 (self.c_gp, self.c_gp.n)]
        ex = {id(s) for s in exclude}
        marks += [(s, s.n) for s in self.all_dma_sems() if s.n > 0 and id(s) not in ex]
        for eng in self.ENGS:
            for s, v in marks:
                self.wait(eng, s, v)

    def transposes(self, items, bank, ident, dst, pre_evac_wait=None, eng="vector"):
        self.bank_wait(bank)
        n = len(items)
        for i, (src, np_, off) in enumerate(items):
            self.op("tensor",
                    lambda E, s=src, o=off, pp=bank, idn=ident, q=np_, st=(i == 0), sp=(i == n - 1):
                    E.matmul(pp[0:q, o:o + 128], s, idn, start=st, stop=sp, is_transpose=True),
                    tick=(i == n - 1))
        tpe = self.pet()
        self.waitt(eng, tpe)
        if pre_evac_wait is not None:
            self.waitt(eng, pre_evac_wait)
        q = items[0][1]
        w = items[-1][2] + 128
        if eng == "vector":
            tev = self.op(eng,
                          lambda E, d=dst, pp=bank, q2=q, w2=w:
                          E.tensor_copy(d, pp[0:q2, 0:w2].rearrange("p (a x) -> p a x", x=128)),
                          tick=True)
        else:
            tev = self.op(eng,
                          lambda E, d=dst, pp=bank, q2=q, w2=w:
                          E.activation(out=d, in_=pp[0:q2, 0:w2].rearrange("p (a x) -> p a x", x=128),
                                       func=AF.Copy),
                          tick=True)
        self.bank_set(bank, tev)
        return tev

    def mm_chain(self, bank, cols, pieces, out_off=0):
        n = len(pieces)
        for i, (lh, rh) in enumerate(pieces):
            self.op("tensor",
                    lambda E, b2=bank, l=lh, r=rh, s=(i == 0), t=(i == n - 1), c=cols, oo=out_off:
                    E.matmul(b2[:, oo:oo + c], l, r, start=s, stop=t),
                    tick=(i == n - 1))
        return self.pet()


def build_program():
    b = Builder()
    nc = b.nc
    p = b

    d_cst = b.sem("d_cst")
    d_cstg = b.sem("d_cstg")
    d_cs2 = b.sem("d_cs2")
    d_al = [b.sem(f"d_al{i}") for i in range(8)]
    d_w = b.sem("d_w")
    d_st = b.sem("d_st")
    d_z = b.sem("d_z")
    d_sA = [b.sem(f"d_sA{i}") for i in range(2)]
    d_sB = [b.sem(f"d_sB{i}") for i in range(2)]
    d_sQ = [b.sem(f"d_sQ{i}") for i in range(2)]
    d_sV = [b.sem(f"d_sV{i}") for i in range(2)]
    d_c16 = [b.sem(f"d_c16{i}") for i in range(2)]
    d_sAt = b.sem("d_sAt")
    d_sx1 = [b.sem(f"d_sx1{i}") for i in range(2)]
    d_x = [b.sem(f"d_x{i}") for i in range(2)]
    d_q = [b.sem(f"d_q{i}") for i in range(2)]
    d_v = [b.sem(f"d_v{i}") for i in range(2)]
    d_sc = [b.sem(f"d_sc{i}") for i in range(2)]
    d_g = [b.sem(f"d_g{i}") for i in range(4)]
    d_y = [b.sem(f"d_y{i}") for i in range(2)]
    d_ga = [b.sem(f"d_ga{i}") for i in range(8)]
    d_ob = [b.sem(f"d_ob{i}") for i in range(2)]
    d_wd = b.sem("d_wd")
    d_wc = [b.sem(f"d_wc{i}") for i in range(2)]

    xin = nc.dram_tensor("xin", [SEQ, S, HD], F32, kind="ExternalInput").ap()
    w_kv_a = nc.dram_tensor("w_kv_a", [HD, FU], F32, kind="ExternalInput").ap()
    w_qb = nc.dram_tensor("w_qb", [QL, NQ], F32, kind="ExternalInput").ap()
    w_kvb = nc.dram_tensor("w_kvb", [KL, NKV], F32, kind="ExternalInput").ap()
    w_o = nc.dram_tensor("w_o", [NH * DV, HD], F32, kind="ExternalInput").ap()
    gate_w = nc.dram_tensor("gate_w", [HD, EXP], F32, kind="ExternalInput").ap()
    w_gate = nc.dram_tensor("w_gate", [EXP, HD, INTER], F32, kind="ExternalInput").ap()
    w_up = nc.dram_tensor("w_up", [EXP, HD, INTER], F32, kind="ExternalInput").ap()
    w_down = nc.dram_tensor("w_down", [EXP, INTER, HD], F32, kind="ExternalInput").ap()
    gamma_qb_i = nc.dram_tensor("gamma_qb", [128, QL], F32, kind="ExternalInput").ap()
    gamma_kvb_i = nc.dram_tensor("gamma_kvb", [128, KL], F32, kind="ExternalInput").ap()
    HC = host_consts()
    cs = {k: nc.dram_tensor("c_" + k, list(v.shape), F32, kind="ExternalInput").ap()
          for k, v in HC.items()}

    OUT = nc.dram_tensor("out", [TLOC, HD], F32, kind="ExternalOutput").ap()

    qcn_hbm = nc.dram_tensor("qcn_hbm", [NTT, 128, QL], BF16, kind="Internal").ap()
    kvn_hbm = nc.dram_tensor("kvn_hbm", [NTT, 128, KL], BF16, kind="Internal").ap()
    kpe_hbm = nc.dram_tensor("kpe_hbm", [NTT, 128, DR], BF16, kind="Internal").ap()
    Q_hbm = nc.dram_tensor("q_hbm", [SEQ, S, NQ], BF16, kind="Internal").ap()
    V_hbm = nc.dram_tensor("v_hbm", [SEQ, S, NH * DV], BF16, kind="Internal").ap()
    attn_hbm = nc.dram_tensor("attn_hbm", [SEQ, S, HD], BF16, kind="Internal").ap()
    x1f_hbm = nc.dram_tensor("x1f_hbm", [NTT, 128, HD], F32, kind="Internal").ap()
    x1b_hbm = nc.dram_tensor("x1b_hbm", [TLOC, HD], BF16, kind="Internal").ap()
    dtw_hbm = nc.dram_tensor("dtw_hbm", [NSLOT, 2], F32, kind="Internal").ap()
    base_hbm = nc.dram_tensor("base_hbm", [1, NTT, EXP], F32, kind="Internal").ap()
    cnt_hbm = nc.dram_tensor("cnt_hbm", [1, NTT, EXP], F32, kind="Internal").ap()
    y_hbm = nc.dram_tensor("y_hbm", [NSLOT, HD], BF16, kind="Internal").ap()

    taps = {}

    def tapt(name, shape, dtype):
        if name in TAPS:
            taps[name] = nc.dram_tensor("tap_" + name, shape, dtype, kind="ExternalOutput").ap()
            return taps[name]
        return None

    ident_bf = b.alloc("ident_bf", [128, 128], BF16)
    ident_f = b.alloc("ident_f", [128, 128], F32)
    lt_f = b.alloc("lt_f", [128, 128], F32)
    negtri = b.alloc("negtri", [128, 128], F32)
    iota32 = b.alloc("iota32", [128, 32], F32)
    iota32x4 = b.alloc("iota32x4", [128, 128], F32)
    iota_f = b.alloc("iota_f", [128, 1], F32)
    ones_col = b.alloc("ones_col", [128, 1], F32)
    ones_row = b.alloc("ones_row", [1, 128], F32)

    slot_all = b.alloc("slot_all", [128, NTT, 8], U32)
    tok_sb = b.alloc("tok_sb", [128, NTT, 8], F32)
    poisv = b.alloc("poisv", [1, 4], F32)
    scG = b.scope()
    cos_sb = scG.alloc("cos_sb", [128, 16, DR], F32)
    sin_sb = scG.alloc("sin_sb", [128, 16, DR], F32)
    gq = scG.alloc("gq", [128, QL], F32)
    gkv = scG.alloc("gkv", [128, KL], F32)

    p.dma("sync", d_cst, ident_f[:], cs["ident"])
    p.dma("gpsimd", d_cstg, ident_bf[:], cs["ident"])
    p.dma("sync", d_cst, lt_f[:], cs["lt"])
    p.dma("sync", d_cst, negtri[:], cs["negtri"])
    p.dma("sync", d_cst, iota32[:], cs["iota32"])
    p.dma("sync", d_cst, iota32x4[:], cs["iota32x4"])
    p.dma("sync", d_cst, tok_sb[:], cs["tok8"])
    p.dma("sync", d_cst, iota_f[:], cs["iota_f"])
    p.dma("sync", d_cst, ones_col[:], cs["ones_col"])
    p.dma("sync", d_cst, ones_row[:], cs["ones_row"])
    p.dma("sync", d_cst, cos_sb[:], cs["cos_t"].rearrange("(t p) d -> p t d", p=128))
    p.dma("sync", d_cst, sin_sb[:], cs["sin_t"].rearrange("(t p) d -> p t d", p=128))
    p.dma("sync", d_cst, gq[:], gamma_qb_i)
    p.dma("sync", d_cst, gkv[:], gamma_kvb_i)
    CST = (d_cst, d_cst.n)

    sc0 = b.scope()
    zer = sc0.alloc("zer", [128, 640], F32)
    tz = p.op("vector", lambda E: E.memset(zer[:], 0.0), tick=True)
    p.waitt("sync", tz)
    p.dma("sync", d_z, dtw_hbm.rearrange("(p j) c -> p (j c)", p=128), zer[:])

    if "A1" in BUILD:
        sc = b.scope()
        wkva = sc.alloc("wkva", [128, 16, FU], BF16)
        xt = sc.alloc("xt", [128, 2, HD], F32)
        xbf = sc.alloc("xbf", [128, 2, HD], BF16)
        xfm = sc.alloc("xfm", [128, 2, 16, 128], BF16)
        fus = sc.alloc("fus", [128, 2, FU], F32)
        sqs = sc.alloc("sqs", [128, QL], F32)
        qcn = sc.alloc("qcn", [128, 2, QL], BF16)
        qcnf = sc.alloc("qcnf", [128, 2, 12, 128], BF16)
        kvn = sc.alloc("kvn", [128, 2, KL], BF16)
        kpo = sc.alloc("kpo", [128, 2, DR], BF16)
        rsc = sc.alloc("rsc", [128, 12], F32)
        rps = sc.alloc("rps", [128, 2 * DR], F32)

        tw = p.dma("gpsimd", d_w, wkva[:], w_kv_a.rearrange("(c p) n -> p c n", p=128))
        p.waitt("tensor", tw)
        p.wait("tensor", d_cstg, 16)
        p.waitt("tensor", CST)
        p.waitt("vector", CST)
        p.waitt("scalar", CST)

        NCHUNKS = [(0, 512), (512, 1024), (1024, 1536), (1536, 2048), (2048, FU)]
        for g in range(NTT if DEV_G is None else DEV_G):
            s, t = divmod(g, 16)
            a = g % 2
            p.waitr("sync", f"xt{a}")
            tx = p.dma("sync", d_x[a], xt[:, a, :], xin[s, t * 128:(t + 1) * 128, :])
            p.waitt("vector", tx)
            p.waitr("vector", f"xbf{a}")
            t_cast = p.op("vector", lambda E, aa=a: E.tensor_copy(xbf[:, aa, :], xt[:, aa, :]), tick=True)
            b.setr(f"xt{a}", t_cast)
            p.waitt("tensor", t_cast)
            for grp in range(2):
                items = [(xbf[:, a, (grp * 8 + j) * 128:(grp * 8 + j + 1) * 128], 128, j * 128)
                         for j in range(8)]
                tev = b.transposes(items, b.psT[grp], ident_bf[:], xfm[:, a, grp * 8:grp * 8 + 8, :],
                                   pre_evac_wait=b.ring.get(f"xfm{a}"))
            b.setr(f"xbf{a}", b.pet())
            t_xfm = tev
            p.waitt("tensor", t_xfm)
            for ci, (n0, n1) in enumerate(NCHUNKS):
                bank = b.psA[ci % 2]
                b.bank_wait(bank)
                tmm = b.mm_chain(bank, n1 - n0,
                                 [(xfm[:, a, k, :], wkva[:, k, n0:n1]) for k in range(16)])
                eng = "vector" if ci % 2 == 0 else "scalar"
                p.waitt(eng, tmm)
                if eng == "vector":
                    tev = p.op(eng, lambda E, aa=a, m0=n0, m1=n1, bk=bank:
                               E.tensor_copy(fus[:, aa, m0:m1], bk[:, 0:m1 - m0]), tick=True)
                else:
                    tev = p.op(eng, lambda E, aa=a, m0=n0, m1=n1, bk=bank:
                               E.activation(out=fus[:, aa, m0:m1], in_=bk[:, 0:m1 - m0], func=AF.Copy), tick=True)
                b.bank_set(bank, tev)
            b.setr(f"xfm{a}", b.pet())
            t_fusv, t_fusa = b.dvt(), b.act()
            p.waitt("scalar", t_fusv)
            p.op("scalar", lambda E, aa=a: E.activation(out=sqs[:, 0:QL], in_=fus[:, aa, 0:QL],
                                                        func=AF.Square, accum_out=rsc[:, 0:1]))
            t_s2 = p.op("scalar", lambda E, aa=a: E.activation(out=sqs[:, 0:KL], in_=fus[:, aa, QL:QL + KL],
                                                               func=AF.Square, accum_out=rsc[:, 1:2]), tick=True)
            p.waitt("vector", t_s2)
            p.op("vector", lambda E: E.tensor_scalar(out=rsc[:, 2:3], in0=rsc[:, 0:1], scalar1=1.0 / QL,
                                                     scalar2=1e-6, op0=ALU.mult, op1=ALU.add))
            t_pre = p.op("vector", lambda E: E.tensor_scalar(out=rsc[:, 3:4], in0=rsc[:, 1:2], scalar1=1.0 / KL,
                                                             scalar2=1e-6, op0=ALU.mult, op1=ALU.add), tick=True)
            p.waitt("scalar", t_pre)
            t_sq = p.op("scalar", lambda E: E.activation(out=rsc[:, 4:6], in_=rsc[:, 2:4], func=AF.Sqrt), tick=True)
            p.waitt("vector", t_sq)
            p.op("vector", lambda E: E.reciprocal(rsc[:, 6:7], rsc[:, 4:5]))
            p.op("vector", lambda E: E.reciprocal(rsc[:, 7:8], rsc[:, 5:6]))
            p.waitr("vector", f"qcn{a}")
            p.waitr("vector", f"kvn{a}")
            p.op("vector", lambda E, aa=a: E.scalar_tensor_tensor(
                out=qcn[:, aa, :], in0=fus[:, aa, 0:QL], scalar=rsc[:, 6:7], in1=gq[:],
                op0=ALU.mult, op1=ALU.mult))
            t_kvn = p.op("vector", lambda E, aa=a: E.scalar_tensor_tensor(
                out=kvn[:, aa, :], in0=fus[:, aa, QL:QL + KL], scalar=rsc[:, 7:8], in1=gkv[:],
                op0=ALU.mult, op1=ALU.mult), tick=True)
            kp = fus[:, a, QL + KL:FU]
            cv = cos_sb[:, t, :]
            sv = sin_sb[:, t, :]
            p.op("vector", lambda E, x=kp, c2=cv: E.tensor_mul(rps[:, 0:DR], x, c2))
            p.op("vector", lambda E, x=kp, s2=sv: E.tensor_mul(rps[:, DR:DR + 32], x[:, 32:64], s2[:, 0:32]))
            p.op("vector", lambda E, x=kp, s2=sv: E.tensor_mul(rps[:, DR + 32:DR + 64], x[:, 0:32], s2[:, 32:64]))
            p.op("vector", lambda E, aa=a: E.tensor_sub(kpo[:, aa, 0:32], rps[:, 0:32], rps[:, DR:DR + 32]))
            t_kpo = p.op("vector", lambda E, aa=a: E.tensor_add(kpo[:, aa, 32:64], rps[:, 32:64],
                                                                rps[:, DR + 32:DR + 64]), tick=True)
            p.waitt("tensor", t_kvn)
            for grp, ntr in ((0, 8), (1, 4)):
                items = [(qcn[:, a, (grp * 8 + j) * 128:(grp * 8 + j + 1) * 128], 128, j * 128)
                         for j in range(ntr)]
                tev = b.transposes(items, b.psT[grp], ident_bf[:],
                                   qcnf[:, a, grp * 8:grp * 8 + ntr, :],
                                   pre_evac_wait=b.ring.get(f"qcnf{a}"))
            b.setr(f"qcn{a}", b.pet())
            t_qf = tev
            p.waitt("sync", t_qf)
            p.waitt("sync", t_kpo)
            tst = p.dma("sync", d_sA[a], qcn_hbm[g], qcnf[:, a, :, :])
            b.setr(f"qcnf{a}", tst)
            p.dma("sync", d_sB[a], kvn_hbm[g], kvn[:, a, :])
            tst = p.dma("sync", d_sB[a], kpe_hbm[g], kpo[:, a, :])
            b.setr(f"kvn{a}", tst)

        tp = tapt("fused31", [128, FU], F32)
        if tp is not None:
            p.dma("sync", d_st, tp, fus[:, 1, :])
        tp = tapt("qcn31", [128, QL], BF16)
        if tp is not None:
            p.dma("sync", d_st, tp, qcn[:, 1, :])
        b.fence()
        sc.close()
        b.ring.clear()

    if "A2" in BUILD:
        sc = b.scope()
        wqb = sc.alloc("wqb", [128, 12, NQ], BF16)
        qti = sc.alloc("qti", [128, 2, QL], BF16)
        qf32 = sc.alloc("qf32", [128, 2, NQ], F32)
        qbf = sc.alloc("qbf", [128, 2, NQ], BF16)
        rp2 = sc.alloc("rp2", [128, NH, 2 * DR], F32)
        c16 = sc.alloc("c16", [128, 2, 2, NH, DR], F32)

        tw = p.dma("gpsimd", d_w, wqb[:], w_qb.rearrange("(c p) n -> p c n", p=128))
        p.waitt("tensor", tw)

        for g in range(NTT if DEV_G is None else DEV_G):
            s, t = divmod(g, 16)
            a = g % 2
            p.waitr("sync", f"qti{a}")
            tq = p.dma("sync", d_q[a], qti[:, a, :], qcn_hbm[g])
            p.waitr("scalar", f"c16{a}")
            p.dma("sync", d_c16[a], c16[:, a, 0], cs["cos16"].rearrange("(t p) h d -> t p h d", p=128)[t])
            tc16 = p.dma("sync", d_c16[a], c16[:, a, 1], cs["sin16"].rearrange("(t p) h d -> t p h d", p=128)[t])
            p.waitt("tensor", tq)
            p.waitr("tensor", f"qf{a}")
            for ci in range(6):
                n0 = ci * 512
                bank = b.psA[ci % 2]
                b.bank_wait(bank)
                tmm = b.mm_chain(bank, 512,
                                 [(qti[:, a, k * 128:(k + 1) * 128], wqb[:, k, n0:n0 + 512])
                                  for k in range(12)])
                eng = "vector" if ci % 2 == 0 else "scalar"
                p.waitt(eng, tmm)
                if eng == "vector":
                    tev = p.op(eng, lambda E, aa=a, m0=n0, bk=bank:
                               E.tensor_copy(qf32[:, aa, m0:m0 + 512], bk[:]), tick=True)
                else:
                    tev = p.op(eng, lambda E, aa=a, m0=n0, bk=bank:
                               E.activation(out=qf32[:, aa, m0:m0 + 512], in_=bk[:], func=AF.Copy), tick=True)
                b.bank_set(bank, tev)
            b.setr(f"qti{a}", b.pet())
            p.waitt("vector", b.act())
            p.waitt("vector", tc16)
            xpe = qf32[:, a, :].rearrange("p (h d) -> p h d", d=DN + DR)[:, :, DN:DN + DR]
            cv16 = c16[:, a, 0]
            sv16 = c16[:, a, 1]
            p.op("vector", lambda E, x=xpe, c2=cv16: E.tensor_tensor(rp2[:, :, 0:DR], x, c2, ALU.mult))
            p.op("vector", lambda E, x=xpe, s2=sv16:
                 E.tensor_tensor(rp2[:, :, DR:DR + 32], x[:, :, 32:64], s2[:, :, 0:32], ALU.mult))
            p.op("vector", lambda E, x=xpe, s2=sv16:
                 E.tensor_tensor(rp2[:, :, DR + 32:DR + 64], x[:, :, 0:32], s2[:, :, 32:64], ALU.mult))
            p.op("vector", lambda E, x=xpe: E.tensor_tensor(x[:, :, 0:32], rp2[:, :, 0:DR][:, :, 0:32],
                                                            rp2[:, :, DR:DR + 32], ALU.subtract))
            tr5 = p.op("vector", lambda E, x=xpe: E.tensor_tensor(x[:, :, 32:64], rp2[:, :, 32:64],
                                                                  rp2[:, :, DR + 32:DR + 64], ALU.add), tick=True)
            b.setr(f"c16{a}", tr5)
            p.waitr("vector", f"qbf{a}")
            t_rope = p.op("vector", lambda E, aa=a: E.tensor_scalar(
                out=qbf[:, aa, :], in0=qf32[:, aa, :], scalar1=SCALE, scalar2=None, op0=ALU.mult), tick=True)
            b.setr(f"qf{a}", t_rope)
            p.waitt("sync", t_rope)
            tst = p.dma("sync", d_sQ[a], Q_hbm[s, t * 128:(t + 1) * 128, :], qbf[:, a, :])
            b.setr(f"qbf{a}", tst)

        tp = tapt("q31", [128, NQ], BF16)
        if tp is not None:
            p.dma("sync", d_st, tp, qbf[:, 1, :])
        b.fence()
        sc.close()
        b.ring.clear()

    sc0.close()
    scG.close()

    if "A3B" in BUILD or "A3" in BUILD or "B" in BUILD:
        DO_B = ("A3B" in BUILD) or ("B" in BUILD)
        scK = b.scope()
        KN = scK.alloc("KN", [128, NH, S], BF16)
        KP = scK.alloc("KP", [128, S], BF16)
        VS = scK.alloc("VS", [128, S // 128, NH * DV], BF16)

        def _seq_body(s):
            sc = b.scope()
            wkvb = sc.alloc("wkvb", [128, 4, NKV], BF16)
            kvi = sc.alloc("kvi", [128, 2, KL], BF16)
            kpi = sc.alloc("kpi", [128, 2, DR], BF16)
            kvf = sc.alloc("kvf", [128, 2, 4, 128], BF16)
            knb = sc.alloc("knb", [128, 2, NH, DN], BF16)

            t_wkvb = p.dma("gpsimd", d_w, wkvb[:], w_kvb.rearrange("(c p) n -> p c n", p=128))
            p.waitt("tensor", t_wkvb)
            for t in range(16 if DEV_T is None else DEV_T):
                g = s * 16 + t
                a = t % 2
                p.waitr("sync", f"kvi{a}")
                p.dma("sync", d_q[a], kvi[:, a, :], kvn_hbm[g])
                tk2 = p.dma("sync", d_q[a], kpi[:, a, :], kpe_hbm[g])
                p.waitt("tensor", tk2)
                items = [(kvi[:, a, j * 128:(j + 1) * 128], 128, j * 128) for j in range(4)]
                tev = b.transposes(items, b.psT[0], ident_bf[:], kvf[:, a, :, :],
                                   pre_evac_wait=b.ring.get(f"kvf{a}"))
                p.waitt("tensor", tev)
                p.waitr("tensor", f"knb{a}")
                for ci in range(8):
                    n0 = ci * 512
                    bank = b.psA[ci % 2]
                    b.bank_wait(bank)
                    tmm = b.mm_chain(bank, 512,
                                     [(kvf[:, a, k, :], wkvb[:, k, n0:n0 + 512]) for k in range(4)])
                    eng = "vector" if ci % 2 == 0 else "scalar"
                    p.waitt(eng, tmm)
                    pvw = bank[:].rearrange("p (h c) -> p h c", c=256)
                    vdst = VS[:, t, :].rearrange("p (h x) -> p h x", x=128)[:, 2 * ci:2 * ci + 2, :]
                    if eng == "vector":
                        p.op(eng, lambda E, aa=a, h0=2 * ci, q=pvw:
                             E.tensor_copy(knb[:, aa, h0:h0 + 2, :], q[:, :, 0:128]))
                        tev = p.op(eng, lambda E, q=pvw, vd=vdst:
                                   E.tensor_copy(vd, q[:, :, 128:256]), tick=True)
                    else:
                        p.op(eng, lambda E, aa=a, h0=2 * ci, q=pvw:
                             E.activation(out=knb[:, aa, h0:h0 + 2, :], in_=q[:, :, 0:128], func=AF.Copy))
                        tev = p.op(eng, lambda E, q=pvw, vd=vdst:
                                   E.activation(out=vd, in_=q[:, :, 128:256], func=AF.Copy),
                                   tick=True)
                    b.bank_set(bank, tev)
                b.setr(f"kvf{a}", b.pet())
                t_ev_v, t_ev_a = b.dvt(), b.act()
                p.waitt("tensor", t_ev_v)
                p.waitt("tensor", t_ev_a)
                for grp in range(2):
                    items = [(knb[:, a, grp * 8 + j, :], 128, j * 128) for j in range(8)]
                    dst = KN[:, grp * 8:grp * 8 + 8, t * 128:(t + 1) * 128]
                    b.transposes(items, b.psT[grp], ident_bf[:], dst)
                b.setr(f"knb{a}", b.pet())
                b.bank_wait(b.psT[0])
                p.op("tensor", lambda E, aa=a: E.transpose(b.psT[0][0:64, 0:128], kpi[:, aa, :], ident_bf[:]),
                     tick=True)
                tpe = b.pet()
                b.setr(f"kvi{a}", tpe)
                p.waitt("vector", tpe)
                tev = p.op("vector", lambda E, tt=t: E.tensor_copy(KP[0:64, tt * 128:(tt + 1) * 128],
                                                                   b.psT[0][0:64, 0:128]), tick=True)
                b.bank_set(b.psT[0], tev)
            b.fence()
            sc.close()

            if not DO_B:
                return
            sc = b.scope()
            q_in = sc.alloc("q_in", [128, 2, NQ], BF16)
            Qn = sc.alloc("Qn", [128, 32, 128], BF16)
            Qp = sc.alloc("Qp", [128, 32, 128], BF16)
            sc_f = sc.alloc("sc_f", [128, 2, 2, S], BF16)
            P_sb = sc.alloc("P_sb", [128, 4, 512], BF16)
            PT_sb = sc.alloc("PT", [128, 2, 4, 128], BF16)
            mtl = sc.alloc("mtl", [128, 2, 8], F32)
            nmw = sc.alloc("nmw", [128, 2, 2], F32)
            rs8 = sc.alloc("rs8", [128, 2, 8], F32)
            lsr = sc.alloc("lsr", [128, 2, 4], F32)
            atb = sc.alloc("atb", [128, 2, HD], BF16)

            nm_t = {}
            NHW = 2
            NW = 16 // NHW * 2

            def pass1(w, qb):
                u = w % 2
                qt = w // (NW // 2)
                hp = w % (NW // 2)
                q0 = qb * 256 + qt * 128
                nkb = (q0 + 128 + 511) // 512
                sbanks = [b.psR, b.psF, b.psB[0], b.psB[1]]
                for hi in range(NHW):
                    h = hp * NHW + hi
                    hq = h * 2 + qt
                    for kb in range(nkb):
                        ncols = min(512, q0 + 128 - kb * 512)
                        full = (kb * 512 + 512 <= q0 + 1)
                        bank = sbanks[(hi * nkb + kb) % 4]
                        b.bank_wait(bank)
                        p.op("tensor", lambda E, bk=bank, q2=hq, c=ncols, k0=kb * 512:
                             E.matmul(bk[:, 0:c], Qn[:, q2, :], KN[:, q2 // 2, k0:k0 + c],
                                      start=True, stop=False))
                        tsc2 = p.op("tensor", lambda E, bk=bank, q2=hq, c=ncols, k0=kb * 512:
                                    E.matmul(bk[:, 0:c], Qp[0:64, q2, :], KP[0:64, k0:k0 + c],
                                             start=False, stop=True), tick=True)
                        p.waitt("vector", tsc2)
                        if hi == 0 and kb == 0:
                            p.waitr("vector", f"scf{u}")
                        if not full:
                            dlt = ncols - 128
                            p.op("vector", lambda E, bk=bank, d2=dlt:
                                 E.tensor_add(bk[:, d2:d2 + 128], bk[:, d2:d2 + 128], negtri[:]))
                        tev = p.op("vector", lambda E, bk=bank, uu=u, c=ncols, k0=kb * 512, h2=hi, kk=kb:
                                   E.tensor_scalar(out=sc_f[:, uu, h2, k0:k0 + c], in0=bk[:, 0:c],
                                                   scalar1=1.0, scalar2=NEG,
                                                   op0=ALU.mult, op1=ALU.max,
                                                   accum_out=mtl[:, uu, h2 * 4 + kk:h2 * 4 + kk + 1]),
                                   tick=True)
                        b.bank_set(bank, tev)
                p.op("vector", lambda E, uu=u, nk=nkb:
                     E.reduce_max(nmw[:, uu, :], mtl[:, uu, :].rearrange("p (h k) -> p h k", k=4)[:, :, 0:nk],
                                  axis=AX.X, negate=True))
                nm_t[w] = b.dvt()

            def pass2(w, qb):
                u = w % 2
                qt = w // (NW // 2)
                hp = w % (NW // 2)
                q0 = qb * 256 + qt * 128
                nkb = (q0 + 128 + 511) // 512
                p.waitt("scalar", nm_t[w])
                vbanks = [b.psA[0], b.psA[1]]

                def pv(hi, kb, u2):
                    h = hp * NHW + hi
                    bank2 = vbanks[hi % 2]
                    ncols = min(512, q0 + 128 - kb * 512)
                    nsub = ncols // 128
                    if kb == 0:
                        b.bank_wait(bank2)
                    p.waitt("tensor", pt_t[u2])
                    for sub in range(nsub):
                        st2 = (kb == 0 and sub == 0)
                        sp2 = (kb == nkb - 1 and sub == nsub - 1)
                        p.op("tensor", lambda E, bk=bank2, u3=u2, s3=sub, h2=h, kt=kb * 4 + sub,
                             st3=st2, sp3=sp2:
                             E.matmul(bk[:, 0:128], PT_sb[:, u3, s3, :],
                                      VS[:, kt, h2 * DV:(h2 + 1) * DV], start=st3, stop=sp3),
                             tick=sp2)
                    b.setr(f"PT{u2}", b.pet())

                units = [(hi, kb) for hi in range(NHW) for kb in range(nkb)]
                pt_t = {}
                t_exp = {}

                def emit_exp(idx):
                    hi, kb = units[idx]
                    ncols = min(512, q0 + 128 - kb * 512)
                    u4 = idx % 4
                    p.waitr("scalar", f"P{u4}")
                    t_exp[idx] = p.op(
                        "scalar", lambda E, uu=u, u3=u4, c=ncols, k0=kb * 512, h2=hi, kk=kb:
                        E.activation(out=P_sb[:, u3, 0:c], in_=sc_f[:, uu, h2, k0:k0 + c],
                                     func=AF.Exp, bias=nmw[:, uu, h2:h2 + 1],
                                     accum_out=rs8[:, uu, h2 * 4 + kk:h2 * 4 + kk + 1]), tick=True)

                for i2 in range(min(2, len(units))):
                    emit_exp(i2)
                prev = None
                for idx, (hi, kb) in enumerate(units):
                    ncols = min(512, q0 + 128 - kb * 512)
                    nsub = ncols // 128
                    u2 = idx % 2
                    u4 = idx % 4
                    items = [(P_sb[:, u4, j * 128:(j + 1) * 128], 128, j * 128) for j in range(nsub)]
                    p.waitt("tensor", t_exp[idx])
                    pt_t[u2] = b.transposes(items, b.psT[u2], ident_bf[:], PT_sb[:, u2, 0:nsub, :],
                                            pre_evac_wait=b.ring.get(f"PT{u2}"),
                                            eng="vector" if u2 == 0 else "scalar")
                    b.setr(f"P{u4}", b.pet())
                    if idx + 2 < len(units):
                        emit_exp(idx + 2)
                    if prev is not None:
                        pv(*prev)
                    prev = (hi, kb, u2)
                pv(*prev)
                tpv = b.pet()
                p.waitt("vector", tpv)
                p.op("vector", lambda E, uu=u, nk=nkb:
                     E.reduce_sum(lsr[:, uu, 0:NHW], rs8[:, uu, :].rearrange("p (h k) -> p h k", k=4)[:, :, 0:nk],
                                  axis=AX.X))
                p.op("vector", lambda E, uu=u: E.reciprocal(lsr[:, uu, NHW:2 * NHW], lsr[:, uu, 0:NHW]))
                p.waitr("vector", "atb")
                for hi in range(NHW):
                    h = hp * NHW + hi
                    tat = p.op("vector", lambda E, uu=u, qt2=qt, h2=h, h3=hi, bk=vbanks[hi % 2]:
                               E.tensor_scalar(out=atb[:, qt2, h2 * DV:(h2 + 1) * DV], in0=bk[:, 0:128],
                                               scalar1=lsr[:, uu, NHW + h3:NHW + h3 + 1], scalar2=None,
                                               op0=ALU.mult), tick=True)
                    b.bank_set(vbanks[hi % 2], tat)
                b.setr(f"scf{u}", b.act())
                return tat

            for qb in range(8 if DEV_QB is None else DEV_QB):
                q0b = qb * 256
                p.waitr("sync", "q_in")
                tq = p.dma("sync", d_q[qb % 2], q_in[:],
                           Q_hbm[s, q0b:q0b + 256, :].rearrange("(a p) n -> p a n", p=128))
                p.waitt("tensor", tq)
                for grp in range(4):
                    items = []
                    for i in range(8):
                        hq = grp * 8 + i
                        h, qt = divmod(hq, 2)
                        items.append((q_in[:, qt, h * (DN + DR):h * (DN + DR) + DN], 128, i * 128))
                    b.transposes(items, b.psT[grp % 2], ident_bf[:], Qn[:, grp * 8:grp * 8 + 8, :],
                                 pre_evac_wait=b.ring.get("Qfm"))
                for grp in range(4):
                    items = []
                    for i in range(8):
                        hq = grp * 8 + i
                        h, qt = divmod(hq, 2)
                        items.append((q_in[:, qt, h * (DN + DR) + DN:h * (DN + DR) + DN + DR], 64, i * 128))
                    b.transposes(items, b.psT[grp % 2], ident_bf[:], Qp[0:64, grp * 8:grp * 8 + 8, :],
                                 pre_evac_wait=b.ring.get("Qfm"))
                b.setr("q_in", b.pet())
                for w in range(NW):
                    pass1(w, qb)
                    if w >= 1:
                        pass2(w - 1, qb)
                t_last = pass2(NW - 1, qb)
                p.waitt("sync", b.dvt())
                p.waitr("sync", "atb")
                tst = p.dma("sync", d_sAt,
                            attn_hbm[s, q0b:q0b + 256, :].rearrange("(a p) n -> p a n", p=128), atb[:])
                b.setr("atb", tst)
                b.setr("Qfm", b.pet())
            tp = tapt(f"atb_s{s}", [128, 2, HD], BF16)
            if tp is not None:
                p.dma("sync", d_st, tp, atb[:])
            b.fence()
            sc.close()
            b.ring.clear()
        for s in range(SEQ if DEV_SEQ is None else DEV_SEQ):
            _seq_body(s)
        scK.close()

    if "B2" in BUILD:
        b.fence()
        sc = b.scope()
        wo = sc.alloc("wo", [128, 16, HD], BF16)
        ati = sc.alloc("ati", [128, 2, HD], BF16)
        afm = sc.alloc("afm", [128, 2, 16, 128], BF16)
        xr = sc.alloc("xr", [128, 2, HD], F32)
        x1 = sc.alloc("x1", [128, 2, HD], F32)
        x1b = sc.alloc("x1b", [128, 2, HD], BF16)
        x1fm = sc.alloc("x1fm", [128, 2, 16, 128], F32)
        sq2 = sc.alloc("sq2", [128, HD], F32)
        gwsb = sc.alloc("gwsb", [128, 16, EXP], F32)
        lg = sc.alloc("lg", [128, EXP], F32)
        lgr = sc.alloc("lgr", [128, EXP], F32)
        tv8 = sc.alloc("tv8", [128, 8], F32)
        ti8 = sc.alloc("ti8", [128, 8], U32)
        ex8 = sc.alloc("ex8", [128, 8], F32)
        sm8 = sc.alloc("sm8", [128, 4], F32)
        lnst = sc.alloc("lnst", [128, 12], F32)
        I8a = sc.alloc("I8a", [128, NTT, EXP], F32)
        idx8f = sc.alloc("idx8f", [128, NTT, 8], F32)
        pv_all = sc.alloc("pv_all", [128, NTT, 8, 2], F32)
        rm4 = sc.alloc("rm4", [128, 128], F32)
        dchain = sc.alloc("dchain", [128, 2, 128], F32)
        smap4 = sc.alloc("smap4", [128, 128], F32)
        pmap4 = sc.alloc("pmap4", [128, 128], F32)
        scrv = sc.alloc("scrv", [128, EXP], F32)
        scrg = sc.alloc("scrg", [128, EXP], F32)
        slotf = sc.alloc("slotf", [128, 8], F32)
        posf = sc.alloc("posf", [128, 8], F32)
        posu = sc.alloc("posu", [128, 2, 8], U32)
        cnt32 = sc.alloc("cnt32", [128, EXP], F32)
        base32 = sc.alloc("base32", [128, EXP], F32)
        pois2 = sc.alloc("pois2", [1, EXP], F32)
        baseB = sc.alloc("baseB", [1, NTT * EXP], F32)
        cntrow = sc.alloc("cntrow", [1, NTT * EXP], F32)

        p.wait("gpsimd", d_z, d_z.n)
        tw = p.dma("gpsimd", d_w, wo[:], w_o.rearrange("(c p) n -> p c n", p=128))
        tgw = p.dma("sync", d_cs2, gwsb[:], gate_w.rearrange("(c p) n -> p c n", p=128))
        t_tok = p.op("vector", lambda E: E.tensor_copy(pv_all[:, :, :, 0], tok_sb[:]), tick=True)
        p.waitt("tensor", tw)
        p.waitt("tensor", tgw)

        NG2 = NTT if DEV_G is None else DEV_G

        def b2_stage1(g):
            s, t = divmod(g, 16)
            a = g % 2
            p.waitr("sync", f"ati{a}")
            ta = p.dma("sync", d_q[a], ati[:, a, :], attn_hbm[s, t * 128:(t + 1) * 128, :])
            p.waitr("sync", f"xr{a}")
            txr = p.dma("sync", d_x[a], xr[:, a, :], xin[s, t * 128:(t + 1) * 128, :])
            p.waitt("tensor", ta)
            for grp in range(2):
                items = [(ati[:, a, (grp * 8 + j) * 128:(grp * 8 + j + 1) * 128], 128, j * 128)
                         for j in range(8)]
                b.transposes(items, b.psT[grp], ident_bf[:], afm[:, a, grp * 8:grp * 8 + 8, :],
                             pre_evac_wait=b.ring.get(f"afm{a}"))
            b.setr(f"ati{a}", b.pet())
            t_afm = b.dvt()
            p.waitt("tensor", t_afm)
            p.waitt("vector", txr)
            p.waitr("vector", f"x1{a}")
            p.waitr("vector", f"x1b{a}")
            for ci in range(4):
                n0 = ci * 512
                bank = b.psA[ci % 2]
                b.bank_wait(bank)
                tmm = b.mm_chain(bank, 512,
                                 [(afm[:, a, k, :], wo[:, k, n0:n0 + 512]) for k in range(16)])
                p.waitt("vector", tmm)
                tev = p.op("vector", lambda E, aa=a, m0=n0, bk=bank, c2=ci:
                           E.scalar_tensor_tensor(out=x1[:, aa, m0:m0 + 512], in0=bk[:],
                                                  scalar=1.0, in1=xr[:, aa, m0:m0 + 512],
                                                  op0=ALU.mult, op1=ALU.add,
                                                  accum_out=lnst[:, c2:c2 + 1]), tick=True)
                b.bank_set(bank, tev)
            b.setr(f"afm{a}", b.pet())
            b.setr(f"xr{a}", b.dvt())
            p.op("vector", lambda E: E.tensor_add(lnst[:, 4:5], lnst[:, 0:1], lnst[:, 1:2]))
            p.op("vector", lambda E: E.tensor_add(lnst[:, 5:6], lnst[:, 2:3], lnst[:, 3:4]))
            t_resid = p.op("vector", lambda E: E.tensor_add(lnst[:, 6:7], lnst[:, 4:5], lnst[:, 5:6]), tick=True)
            p.waitt("scalar", t_resid)
            t_s2 = p.op("scalar", lambda E, aa=a: E.activation(out=sq2[:], in_=x1[:, aa, :],
                                                               func=AF.Square, accum_out=lnst[:, 7:8]), tick=True)
            p.op("vector", lambda E: E.tensor_scalar(out=lnst[:, 8:9], in0=lnst[:, 6:7],
                                                     scalar1=1.0 / HD, scalar2=None, op0=ALU.mult))
            p.waitt("vector", t_s2)
            p.op("vector", lambda E: E.tensor_scalar(out=lnst[:, 9:10], in0=lnst[:, 7:8],
                                                     scalar1=1.0 / HD, scalar2=None, op0=ALU.mult))
            p.op("vector", lambda E: E.tensor_mul(lnst[:, 10:11], lnst[:, 8:9], lnst[:, 8:9]))
            p.op("vector", lambda E: E.scalar_tensor_tensor(out=lnst[:, 10:11], in0=lnst[:, 10:11],
                                                            scalar=-1.0, in1=lnst[:, 9:10],
                                                            op0=ALU.mult, op1=ALU.add))
            t_ve = p.op("vector", lambda E: E.tensor_scalar(out=lnst[:, 11:12], in0=lnst[:, 10:11],
                                                            scalar1=1.0, scalar2=1e-5,
                                                            op0=ALU.mult, op1=ALU.add), tick=True)
            p.waitt("scalar", t_ve)
            t_sqr = p.op("scalar", lambda E: E.activation(out=lnst[:, 10:11], in_=lnst[:, 11:12],
                                                          func=AF.Sqrt), tick=True)
            p.waitt("vector", t_sqr)
            p.op("vector", lambda E: E.reciprocal(lnst[:, 11:12], lnst[:, 10:11]))
            p.op("vector", lambda E, aa=a: E.tensor_scalar(out=x1[:, aa, :], in0=x1[:, aa, :],
                                                           scalar1=lnst[:, 8:9], scalar2=lnst[:, 11:12],
                                                           op0=ALU.subtract, op1=ALU.mult))
            t_x1 = p.op("vector", lambda E, aa=a: E.tensor_copy(x1b[:, aa, :], x1[:, aa, :]), tick=True)
            p.waitt("sync", t_x1)
            p.dma("sync", d_sx1[a], x1f_hbm[g], x1[:, a, :])
            tst = p.dma("sync", d_sx1[a], x1b_hbm[g * 128:(g + 1) * 128, :], x1b[:, a, :])
            b.setr(f"x1b{a}", tst)
            p.waitt("tensor", t_x1)
            for grp in range(4):
                bank = b.psF if grp % 2 == 0 else b.psR
                b.bank_wait(bank)
                for i in range(4):
                    c2 = grp * 4 + i
                    p.op("tensor", lambda E, aa=a, cc=c2, o=i * 128, bk=bank, st=(i == 0), sp=(i == 3):
                         E.matmul(bk[:, o:o + 128], x1[:, aa, cc * 128:(cc + 1) * 128], ident_f[:],
                                  start=st, stop=sp, is_transpose=True),
                         tick=(i == 3))
                tpe = b.pet()
                p.waitt("vector", tpe)
                if grp < 2:
                    p.waitr("vector", f"x1fm{a}")
                tev = p.op("vector", lambda E, aa=a, g2=grp, bk=bank:
                           E.tensor_copy(x1fm[:, aa, g2 * 4:g2 * 4 + 4, :],
                                         bk[:].rearrange("p (c x) -> p c x", c=4)), tick=True)
                b.bank_set(bank, tev)
            t_fm = b.dvt()
            b.setr(f"x1{a}", b.pet())
            p.waitt("tensor", t_fm)
            bank = b.psR
            b.bank_wait(bank)
            tlg = b.mm_chain(bank, EXP, [(x1fm[:, a, k, :], gwsb[:, k, :]) for k in range(16)])
            b.setr(f"x1fm{a}", tlg)
            p.waitt("vector", tlg)
            tev = p.op("vector", lambda E, bk=bank: E.tensor_copy(lg[:], bk[:, 0:EXP]), tick=True)
            b.bank_set(bank, tev)
            p.waitr("vector", "tv8")
            p.op("vector", lambda E: E.max(tv8[:], lg[:]))
            p.op("vector", lambda E: E.max_index(ti8[:], tv8[:], lg[:]))
            p.op("vector", lambda E, g2=g: E.tensor_copy(idx8f[:, g2, :], ti8[:]))
            p.op("vector", lambda E: E.match_replace(lgr[:], tv8[:], lg[:], NEG))
            t_i8g = p.op("vector", lambda E, g2=g: E.tensor_tensor(I8a[:, g2, :], lg[:], lgr[:],
                                                                   ALU.not_equal), tick=True)
            t_nm8 = p.op("vector", lambda E: E.tensor_scalar(out=sm8[:, 0:1], in0=tv8[:, 0:1],
                                                             scalar1=-1.0, scalar2=None, op0=ALU.mult),
                         tick=True)
            p.waitt("scalar", t_nm8)
            p.waitr("scalar", "ex8")
            t_e8 = p.op("scalar", lambda E: E.activation(out=ex8[:], in_=tv8[:], func=AF.Exp,
                                                         bias=sm8[:, 0:1], accum_out=sm8[:, 1:2]), tick=True)
            b.setr("tv8", t_e8)
            p.waitt("vector", t_e8)
            p.op("vector", lambda E: E.reciprocal(sm8[:, 2:3], sm8[:, 1:2]))
            t_w8 = p.op("vector", lambda E, g2=g: E.tensor_scalar(out=pv_all[:, g2, :, 1], in0=ex8[:],
                                                                  scalar1=sm8[:, 2:3], scalar2=None,
                                                                  op0=ALU.mult), tick=True)
            b.setr("ex8", t_w8)
            p.waitt("tensor", t_i8g)
            p.op("tensor", lambda E, g2=g: E.matmul(
                b.psB[g2 // 16][0:1, (g2 % 16) * EXP:(g2 % 16 + 1) * EXP], ones_col[:],
                I8a[:, g2, :], start=True, stop=True),
                 tick=(g == NG2 - 1))

        for g in range(NG2):
            b2_stage1(g)

        t_cntmm = b.pet()
        p.waitt("vector", t_cntmm)
        p.op("vector", lambda E: E.tensor_copy(cntrow[0:1, 0:512], b.psB[0][0:1, 0:512]))
        t_cr = p.op("vector", lambda E: E.tensor_copy(cntrow[0:1, 512:1024], b.psB[1][0:1, 0:512]),
                    tick=True)
        b.bank_set(b.psB[0], t_cr)
        b.bank_set(b.psB[1], t_cr)
        p.waitt("sync", t_cr)
        tsc = p.dma("sync", d_st, cnt_hbm.rearrange("a g e -> a (g e)"), cntrow[0:1, :])
        p.waitt("sync", tsc)
        tlc = p.dma("sync", d_st, cnt32[0:32, :], cnt_hbm[0])
        p.waitt("tensor", tlc)
        b.bank_wait(b.psB[1])
        p.op("tensor", lambda E: E.matmul(b.psB[1][0:32, 0:EXP], lt_f[0:32, 0:32], cnt32[0:32, :],
                                          start=True, stop=True))
        t_pfx = p.op("tensor", lambda E: E.matmul(b.psB[1][0:1, EXP:2 * EXP], ones_col[0:32, :],
                                                  cnt32[0:32, :], start=True, stop=True), tick=True)
        p.waitt("vector", t_pfx)
        p.op("vector", lambda E: E.tensor_copy(base32[0:32, :], b.psB[1][0:32, 0:EXP]))
        t_b32 = p.op("vector", lambda E: E.tensor_copy(pois2[0:1, :], b.psB[1][0:1, EXP:2 * EXP]), tick=True)
        b.bank_set(b.psB[1], t_b32)
        p.waitt("sync", t_b32)
        tsb = p.dma("sync", d_st, base_hbm[0], base32[0:32, :])
        p.waitt("sync", tsb)
        tlb = p.dma("sync", d_st, baseB[0:1, :], base_hbm.rearrange("a g e -> a (g e)"))
        p.op("vector", lambda E: E.reduce_max(poisv[0:1, 0:1], pois2[0:1, :], axis=AX.X))
        p.op("vector", lambda E: E.tensor_scalar(out=poisv[0:1, 1:2], in0=poisv[0:1, 0:1],
                                                 scalar1=float(CAPL), scalar2=3.0e38,
                                                 op0=ALU.is_gt, op1=ALU.mult))
        p.op("vector", lambda E: E.tensor_add(poisv[0:1, 2:3], poisv[0:1, 1:2], poisv[0:1, 1:2]))

        p.waitt("tensor", tlb)
        p.waitt("gpsimd", t_tok)
        NGRP = (NG2 + 3) // 4
        for j in range(NGRP):
            tiles = [4 * j + t4 for t4 in range(4) if 4 * j + t4 < NG2]
            W = len(tiles) * 32
            bank = b.psF if j % 2 == 0 else b.psR
            b.bank_wait(bank)
            for t4, g in enumerate(tiles):
                p.op("tensor", lambda E, bk=bank, g2=g, o=t4 * 32:
                     E.matmul(bk[:, o:o + EXP], lt_f[:], I8a[:, g2, :], start=True, stop=False))
                p.op("tensor", lambda E, bk=bank, g2=g, o=t4 * 32:
                     E.matmul(bk[:, o:o + EXP], ones_row[:], baseB[0:1, g2 * EXP:(g2 + 1) * EXP],
                              start=False, stop=True), tick=(t4 == len(tiles) - 1))
            t_mm = b.pet()
            p.waitt("vector", t_mm)
            p.waitr("vector", "pmapg")
            tev = p.op("vector", lambda E, bk=bank, w2=W: E.tensor_copy(rm4[:, 0:w2], bk[:, 0:w2]),
                       tick=True)
            b.bank_set(bank, tev)
            p.op("vector", lambda E, w2=W: E.memset(dchain[:, 0, 0:w2], 0.0))
            for jj in range(1, NCH):
                p.op("vector", lambda E, jj2=jj, w2=W: E.scalar_tensor_tensor(
                    out=dchain[:, jj2 % 2, 0:w2], in0=rm4[:, 0:w2], scalar=float(jj2 * 128),
                    in1=dchain[:, (jj2 - 1) % 2, 0:w2], op0=ALU.is_ge, op1=ALU.add))
            dsl = (NCH - 1) % 2
            p.op("vector", lambda E, w2=W, d2=dsl: E.scalar_tensor_tensor(
                out=pmap4[:, 0:w2], in0=dchain[:, d2, 0:w2], scalar=-128.0, in1=rm4[:, 0:w2],
                op0=ALU.mult, op1=ALU.add))
            p.op("vector", lambda E, w2=W, d2=dsl: E.scalar_tensor_tensor(
                out=dchain[:, 1 - d2, 0:w2], in0=iota32x4[:, 0:w2], scalar=float(NCH),
                in1=dchain[:, d2, 0:w2], op0=ALU.mult, op1=ALU.add))
            p.op("vector", lambda E, w2=W, d2=dsl: E.scalar_tensor_tensor(
                out=pmap4[:, 0:w2], in0=pmap4[:, 0:w2], scalar=float(EXP * NCH),
                in1=dchain[:, 1 - d2, 0:w2], op0=ALU.mult, op1=ALU.add))
            t_pm = p.op("vector", lambda E, w2=W: E.scalar_tensor_tensor(
                out=smap4[:, 0:w2], in0=iota32x4[:, 0:w2], scalar=float(CAPL),
                in1=rm4[:, 0:w2], op0=ALU.mult, op1=ALU.add), tick=True)
            t_lastg = None
            for t4, g in enumerate(tiles):
                a = g % 2
                for k in range(8):
                    p.op("vector", lambda E, g2=g, kk=k, o=t4 * 32: E.scalar_tensor_tensor(
                        out=scrv[:], in0=iota32[:], scalar=idx8f[:, g2, kk:kk + 1],
                        in1=smap4[:, o:o + EXP], op0=ALU.is_equal, op1=ALU.mult,
                        accum_out=slotf[:, kk:kk + 1]))
                    p.op("vector", lambda E, g2=g, kk=k, o=t4 * 32: E.scalar_tensor_tensor(
                        out=scrg[:], in0=iota32[:], scalar=idx8f[:, g2, kk:kk + 1],
                        in1=pmap4[:, o:o + EXP], op0=ALU.is_equal, op1=ALU.mult,
                        accum_out=posf[:, kk:kk + 1]))
                p.op("vector", lambda E, g2=g: E.tensor_copy(slot_all[:, g2, :], slotf[:]))
                p.waitr("vector", f"posu{a}")
                t_lastg = p.op("vector", lambda E, aa=a: E.tensor_copy(posu[:, aa, :], posf[:]), tick=True)
                p.waitt("gpsimd", t_lastg)
                for k in range(8):
                    b.scatter(d_sc[a], dtw_hbm, pv_all[:, g, k, :], posu[:, a, k:k + 1], NSLOT - 1)
                b.setr(f"posu{a}", (d_sc[a], d_sc[a].n))
            b.setr("pmapg", t_lastg)

        b.fence()
        sc.close()
        b.ring.clear()

    if "C" in BUILD:
        b.fence()
        sc = b.scope()
        wg = sc.alloc("wg", [128, 2, 16, INTER], BF16)
        wu = sc.alloc("wu", [128, 2, 16, INTER], BF16)
        wd = sc.alloc("wd", [128, 6, HD], BF16)
        xg = sc.alloc("xg", [128, 4, HD], BF16)
        xfc = sc.alloc("xfc", [128, 2, 16, 512], BF16)
        gsb = sc.alloc("gsb", [128, 2, 512], BF16)
        actc = sc.alloc("actc", [128, 2, 6, 512], BF16)
        ysb = sc.alloc("ysb", [128, 2, HD], BF16)
        dtw = sc.alloc("dtw", [128, NCH * EXP, 2], F32)
        didx = sc.alloc("didx", [128, NCH * EXP], U32)

        tld = p.dma("sync", d_cs2, dtw[:], dtw_hbm.rearrange("(p j) c -> p j c", p=128))
        p.waitt("vector", tld)
        t_didx = p.op("vector", lambda E: E.tensor_copy(didx[:], dtw[:, :, 0]), tick=True)
        p.waitt("gpsimd", t_didx)
        p.waitt("scalar", tld)

        stgc = sc.alloc("stgc", [128, 2, INTER], F32)
        wgv = w_gate.rearrange("e (c p) i -> e p c i", p=128)
        wuv = w_up.rearrange("e (c p) i -> e p c i", p=128)
        wdv = w_down.rearrange("e (c p) i -> e p c i", p=128)

        t_wg = {}
        t_wu = {}
        CJ = {"jobs": [], "i": 0}

        def cast_add(e):
            we2 = e % 2
            for j in range(16):
                CJ["jobs"].append((wgv[e, :, j, :], wg[:, we2, j, :], ("wg", e, j)))
                CJ["jobs"].append((wuv[e, :, j, :], wu[:, we2, j, :], ("wu", e, j)))

        def cast_pump(n):
            while n > 0 and CJ["i"] < len(CJ["jobs"]):
                src, dst, mark = CJ["jobs"][CJ["i"]]
                sl = CJ["i"] % 2
                CJ["i"] += 1
                n -= 1
                eng = "scalar" if sl else "vector"
                p.waitr("sync", f"cstg{sl}")
                tld2 = p.dma("sync", d_wc[sl], stgc[:, sl, 0:src.shape[-1]], src)
                if mark[2] == 0:
                    p.waitr(eng, f"{mark[0]}{mark[1] % 2}")
                p.waitt(eng, tld2)
                if eng == "scalar":
                    tc = p.op(eng, lambda E, d=dst, s2=stgc[:, sl, 0:dst.shape[-1]]:
                              E.activation(out=d, in_=s2, func=AF.Copy), tick=True)
                else:
                    tc = p.op(eng, lambda E, d=dst, s2=stgc[:, sl, 0:dst.shape[-1]]:
                              E.tensor_copy(d, s2), tick=True)
                b.setr(f"cstg{sl}", tc)
                (t_wg if mark[0] == "wg" else t_wu)[mark[1]] = tc

        cast_add(0)
        cast_pump(32)

        for e in range(EXP if DEV_E is None else DEV_E):
            we = e % 2
            if e + 1 < EXP:
                cast_add(e + 1)
            wd_next = 0
            for c3 in range(3):
                csz = 512 if c3 < 2 else 256
                nsub = csz // 128
                ring = c3 % 2
                for sub in range(nsub):
                    j = e * NCH + c3 * 4 + sub
                    gslot = (c3 * 4 + sub) % 4
                    p.waitr("gpsimd", f"xg{gslot}")
                    tgt = b.gather(d_g[gslot], xg[:, gslot, :], x1b_hbm, didx[:, j:j + 1], TLOC - 1)
                    if wd_next < 6:
                        if wd_next == 0:
                            p.waitr("gpsimd", "wd")
                        t_wd = p.dma("gpsimd", d_wd, wd[:, wd_next, :], wdv[e, :, wd_next, :])
                        wd_next += 1
                    p.waitt("tensor", tgt)
                    if sub == 0:
                        p.waitr("tensor", f"xfc{ring}")
                    for grp in range(2):
                        items = [(xg[:, gslot, (grp * 8 + i) * 128:(grp * 8 + i + 1) * 128], 128, i * 128)
                                 for i in range(8)]
                        b.transposes(items, b.psT[grp], ident_bf[:],
                                     xfc[:, ring, grp * 8:grp * 8 + 8, sub * 128:(sub + 1) * 128])
                    b.setr(f"xg{gslot}", b.pet())
                t_xf = b.dvt()
                p.waitt("tensor", t_xf)
                p.waitt("tensor", t_wg[e])
                p.waitt("tensor", t_wu[e])
                for m in range(6):
                    cast_pump(2)
                    bg = b.psA[m % 2]
                    bu = b.psB[m % 2]
                    gslot2 = m % 2
                    b.bank_wait(bg)
                    tg2 = b.mm_chain(bg, csz, [(wg[:, we, k, m * 128:(m + 1) * 128], xfc[:, ring, k, 0:csz])
                                               for k in range(16)])
                    b.bank_wait(bu)
                    tu2 = b.mm_chain(bu, csz, [(wu[:, we, k, m * 128:(m + 1) * 128], xfc[:, ring, k, 0:csz])
                                               for k in range(16)])
                    p.waitt("scalar", tg2)
                    p.waitr("scalar", f"gsb{gslot2}")
                    tsi = p.op("scalar", lambda E, bk=bg, c=csz, gs=gslot2:
                               E.activation(out=gsb[:, gs, 0:c], in_=bk[:, 0:c], func=AF.Silu), tick=True)
                    b.bank_set(bg, tsi)
                    p.waitt("vector", tsi)
                    p.waitt("vector", tu2)
                    p.waitr("vector", f"act{ring}_{m}")
                    tmu = p.op("vector", lambda E, bk=bu, c=csz, mm=m, rr=ring, gs=gslot2:
                               E.tensor_tensor(actc[:, rr, mm, 0:c], gsb[:, gs, 0:c], bk[:, 0:c], ALU.mult),
                               tick=True)
                    b.bank_set(bu, tmu)
                    b.setr(f"gsb{gslot2}", tmu)
                t_act = b.dvt()
                p.waitt("tensor", t_act)
                p.waitt("tensor", t_wd)
                for sub in range(nsub):
                    yring = sub % 2
                    j = e * NCH + c3 * 4 + sub
                    for ci in range(4):
                        bank = b.psF if ci % 2 == 0 else b.psR
                        b.bank_wait(bank)
                        tdn = b.mm_chain(bank, 512,
                                         [(actc[:, ring, m, sub * 128:(sub + 1) * 128],
                                           wd[:, m, ci * 512:(ci + 1) * 512]) for m in range(6)])
                        p.waitt("scalar", tdn)
                        if ci == 0:
                            p.waitr("scalar", f"ysb{yring}")
                        tev = p.op("scalar", lambda E, bk=bank, yr=yring, c2=ci, jj=j:
                                   E.activation(out=ysb[:, yr, c2 * 512:(c2 + 1) * 512], in_=bk[:],
                                                func=AF.Copy, scale=dtw[:, jj, 1:2]), tick=True)
                        b.bank_set(bank, tev)
                    t_ysb = b.act()
                    p.waitt("sync", t_ysb)
                    r0 = e * CAPL + (c3 * 4 + sub) * 128
                    tst = p.dma("sync", d_y[yring], y_hbm[r0:r0 + 128, :], ysb[:, yring, :])
                    b.setr(f"ysb{yring}", tst)
                b.setr(f"xfc{ring}", t_act)
                for mm in range(6):
                    b.setr(f"act{ring}_{mm}", b.pet())
            b.setr(f"wg{we}", b.pet())
            b.setr(f"wu{we}", b.pet())
            b.setr("wd", b.pet())
        b.fence()
        sc.close()
        b.ring.clear()

    if "D" in BUILD:
        b.fence()
        sc = b.scope()
        y8 = sc.alloc("y8", [128, 2, 8, HD], BF16)
        accX = sc.alloc("accX", [128, 2, HD], F32)
        x2f = sc.alloc("x2f", [128, 2, HD], F32)
        outb = sc.alloc("outb", [128, 2, HD], F32)
        sq3 = sc.alloc("sq3", [128, HD], F32)
        ln3 = sc.alloc("ln3", [128, 2, 12], F32)

        banks4 = [b.psA[0], b.psA[1], b.psB[0], b.psB[1]]
        tg_all = {}

        def d_issue(g):
            a = g % 2
            p.waitr("gpsimd", f"y8{a}")
            tg_all[g] = [b.gather(d_ga[k], y8[:, a, k, :], y_hbm, slot_all[:, g, k:k + 1], NSLOT - 1)
                         for k in range(8)]
            p.waitr("sync", f"accX{a}")
            tg_all[(g, "x")] = p.dma("sync", d_al[a], accX[:, a, :], x1f_hbm[g])

        def d_proc(g):
            a = g % 2
            for k in range(8):
                p.waitt("tensor", tg_all[g][k])
            for c in range(4):
                bank = banks4[c]
                b.bank_wait(bank)
                for k in range(8):
                    p.op("tensor", lambda E, bk=bank, aa=a, kk=k, cc=c, st=(k == 0), sp=(k == 7):
                         E.matmul(bk[:, 0:512], ident_bf[:], y8[:, aa, kk, cc * 512:(cc + 1) * 512],
                                  start=st, stop=sp), tick=(k == 7))
                tmm = b.pet()
                p.waitt("vector", tmm)
                if c == 0:
                    p.waitt("vector", tg_all[(g, "x")])
                    p.waitr("vector", f"x2f{a}")
                tev = p.op("vector", lambda E, bk=bank, aa=a, cc=c:
                           E.scalar_tensor_tensor(out=x2f[:, aa, cc * 512:(cc + 1) * 512], in0=bk[:],
                                                  scalar=1.0, in1=accX[:, aa, cc * 512:(cc + 1) * 512],
                                                  op0=ALU.mult, op1=ALU.add,
                                                  accum_out=ln3[:, aa, cc:cc + 1]), tick=True)
                b.bank_set(bank, tev)
            b.setr(f"y8{a}", b.pet())
            b.setr(f"accX{a}", b.dvt())
            p.op("vector", lambda E, aa=a: E.tensor_add(ln3[:, aa, 4:5], ln3[:, aa, 0:1], ln3[:, aa, 1:2]))
            p.op("vector", lambda E, aa=a: E.tensor_add(ln3[:, aa, 5:6], ln3[:, aa, 2:3], ln3[:, aa, 3:4]))
            t_sum = p.op("vector", lambda E, aa=a: E.tensor_add(ln3[:, aa, 6:7], ln3[:, aa, 4:5],
                                                                ln3[:, aa, 5:6]), tick=True)
            p.waitt("scalar", t_sum)
            t_s2 = p.op("scalar", lambda E, aa=a: E.activation(out=sq3[:], in_=x2f[:, aa, :],
                                                               func=AF.Square, accum_out=ln3[:, aa, 7:8]),
                        tick=True)
            p.op("vector", lambda E, aa=a: E.tensor_scalar(out=ln3[:, aa, 8:9], in0=ln3[:, aa, 6:7],
                                                           scalar1=1.0 / HD, scalar2=None, op0=ALU.mult))
            p.waitt("vector", t_s2)
            p.op("vector", lambda E, aa=a: E.tensor_scalar(out=ln3[:, aa, 9:10], in0=ln3[:, aa, 7:8],
                                                           scalar1=1.0 / HD, scalar2=None, op0=ALU.mult))
            p.op("vector", lambda E, aa=a: E.tensor_mul(ln3[:, aa, 10:11], ln3[:, aa, 8:9], ln3[:, aa, 8:9]))
            p.op("vector", lambda E, aa=a: E.scalar_tensor_tensor(out=ln3[:, aa, 10:11], in0=ln3[:, aa, 10:11],
                                                                  scalar=-1.0, in1=ln3[:, aa, 9:10],
                                                                  op0=ALU.mult, op1=ALU.add))
            t_ve = p.op("vector", lambda E, aa=a: E.tensor_scalar(out=ln3[:, aa, 11:12], in0=ln3[:, aa, 10:11],
                                                                  scalar1=1.0, scalar2=1e-5,
                                                                  op0=ALU.mult, op1=ALU.add), tick=True)
            p.waitt("scalar", t_ve)
            t_sq = p.op("scalar", lambda E, aa=a: E.activation(out=ln3[:, aa, 10:11], in_=ln3[:, aa, 11:12],
                                                               func=AF.Sqrt), tick=True)
            p.waitt("vector", t_sq)
            p.op("vector", lambda E, aa=a: E.reciprocal(ln3[:, aa, 11:12], ln3[:, aa, 10:11]))
            p.waitr("vector", f"outb{a}")
            t_o = p.op("vector", lambda E, aa=a: E.tensor_scalar(
                out=outb[:, aa, :], in0=x2f[:, aa, :], scalar1=ln3[:, aa, 8:9], scalar2=ln3[:, aa, 11:12],
                op0=ALU.subtract, op1=ALU.mult), tick=True)
            if g == NTT - 1:
                t_o = p.op("vector", lambda E, aa=a: E.tensor_scalar(
                    out=outb[0:1, aa, :], in0=outb[0:1, aa, :], scalar1=poisv[0:1, 2:3],
                    scalar2=None, op0=ALU.add), tick=True)
            b.setr(f"x2f{a}", b.dvt())
            p.waitt("sync", t_o)
            tst = p.dma("sync", d_ob[a], OUT[g * 128:(g + 1) * 128, :], outb[:, a, :])
            b.setr(f"outb{a}", tst)

        ND = NTT if DEV_D is None else DEV_D * 8
        d_issue(0)
        for g in range(1, ND):
            d_issue(g)
            d_proc(g - 1)
        d_proc(ND - 1)
        p.wait("sync", d_ob[0], d_ob[0].n)
        p.wait("sync", d_ob[1], d_ob[1].n)
        sc.close()

    b.taps_map = taps
    return b


def finalize(b):
    nc = b.nc
    with nc.Block() as block:
        @block.sync
        def _(E):
            for fn in b.st["sync"]:
                fn(E)

        @block.gpsimd
        def _(E):
            for fn in b.st["gpsimd"]:
                fn(E)

        @block.scalar
        def _(E):
            for fn in b.st["scalar"]:
                fn(E)

        @block.vector
        def _(E):
            for fn in b.st["vector"]:
                fn(E)

        @block.tensor
        def _(E):
            for fn in b.st["tensor"]:
                fn(E)
    return nc


_CACHE = {}


def _get_program():
    key = (tuple(BUILD), tuple(TAPS))
    if key not in _CACHE:
        bb = build_program()
        _CACHE[key] = (finalize(bb), bb)
    return _CACHE[key]


def make_in_maps(x, wts):
    hc = host_consts()
    base = dict(wts)
    for k, v in hc.items():
        base["c_" + k] = v
    in_maps = []
    for c in range(NCORE):
        m = dict(base)
        m["xin"] = np.ascontiguousarray(x[c * SEQ:(c + 1) * SEQ])
        in_maps.append(m)
    return in_maps


def kernel(L0_attn_hidden, L0_attn_w_kv_a, L0_attn_q_a_gamma, L0_attn_w_qb,
           L0_attn_kv_a_gamma, L0_attn_w_kvb, L0_attn_w_o,
           L0_ffn_gate_w, L0_ffn_w_gate, L0_ffn_w_up, L0_ffn_w_down):
    nc, bb = _get_program()
    wts = {
        "w_kv_a": np.ascontiguousarray(L0_attn_w_kv_a, np.float32),
        "w_qb": np.ascontiguousarray(L0_attn_w_qb, np.float32),
        "w_kvb": np.ascontiguousarray(L0_attn_w_kvb, np.float32),
        "w_o": np.ascontiguousarray(L0_attn_w_o, np.float32),
        "gate_w": np.ascontiguousarray(L0_ffn_gate_w, np.float32),
        "w_gate": np.ascontiguousarray(L0_ffn_w_gate, np.float32),
        "w_up": np.ascontiguousarray(L0_ffn_w_up, np.float32),
        "w_down": np.ascontiguousarray(L0_ffn_w_down, np.float32),
        "gamma_qb": np.ascontiguousarray(np.broadcast_to(
            np.asarray(L0_attn_q_a_gamma, np.float32), (128, QL))),
        "gamma_kvb": np.ascontiguousarray(np.broadcast_to(
            np.asarray(L0_attn_kv_a_gamma, np.float32), (128, KL))),
    }
    x = np.ascontiguousarray(L0_attn_hidden, np.float32)
    in_maps = make_in_maps(x, wts)
    res = run_bass_kernel_spmd(nc, in_maps, core_ids=list(range(NCORE)))
    out = np.concatenate([r["out"].reshape(SEQ, S, HD) for r in res.results], axis=0)
    return out


# version 20
# speedup vs baseline: 1.1975x; 1.0491x over previous
import math
import numpy as np
import concourse.bass as bass
import concourse.mybir as mybir
from concourse.bass import IndirectOffsetOnAxis
from concourse.bass_utils import run_bass_kernel_spmd

F32 = mybir.dt.float32
BF16 = mybir.dt.bfloat16
U32 = mybir.dt.uint32
AF = mybir.ActivationFunctionType
ALU = mybir.AluOpType
AX = mybir.AxisListType

NCORE = 8
SEQ = 2
S = 2048
HD = 2048
NH = 16
DN = 128
DR = 64
DV = 128
QL = 1536
KL = 512
EXP = 32
INTER = 768
CAPL = 1280
NCH = CAPL // 128
NSLOT = EXP * CAPL
TLOC = SEQ * S
NTT = TLOC // 128
NQ = NH * (DN + DR)
NKV = NH * (DN + DV)
FU = QL + KL + DR
SCALE = 1.0 / math.sqrt(DN + DR)
NEG = -3.0e38

BUILD = ("A1", "A2", "A3B", "B2", "C", "D")
TAPS = ()
DEV_SEQ = None
DEV_T = None
DEV_QB = None
DEV_G = None
DEV_E = None
DEV_D = None


def host_consts():
    c = {}
    c["ident"] = np.eye(128, dtype=np.float32)
    c["lt"] = np.triu(np.ones((128, 128), np.float32), 1)
    c["negtri"] = np.triu(np.full((128, 128), NEG, np.float32), 1)
    c["iota32"] = np.broadcast_to(np.arange(32, dtype=np.float32), (128, 32)).copy()
    c["iota32x4"] = np.broadcast_to(np.tile(np.arange(32, dtype=np.float32), 4), (128, 128)).copy()
    tok = np.arange(NTT, dtype=np.float32)[:, None] * 128 + np.arange(128, dtype=np.float32)[None, :]
    c["tok8"] = np.ascontiguousarray(np.broadcast_to(tok.T[:, :, None], (128, NTT, 8)))
    c["iota_f"] = np.arange(128, dtype=np.float32).reshape(128, 1)
    c["ones_col"] = np.ones((128, 1), np.float32)
    c["ones_row"] = np.ones((1, 128), np.float32)
    inv_freq = 1.0 / (10000.0 ** (np.arange(0, DR, 2, dtype=np.float64) / DR))
    ang = np.arange(S, dtype=np.float64)[:, None] * inv_freq[None, :]
    c["cos_t"] = np.concatenate([np.cos(ang)] * 2, axis=1).astype(np.float32)
    c["sin_t"] = np.concatenate([np.sin(ang)] * 2, axis=1).astype(np.float32)
    c["cos16"] = np.ascontiguousarray(np.broadcast_to(c["cos_t"][:, None, :], (S, NH, DR)))
    c["sin16"] = np.ascontiguousarray(np.broadcast_to(c["sin_t"][:, None, :], (S, NH, DR)))
    return c


class Sem:
    __slots__ = ("h", "n")

    def __init__(self, h):
        self.h = h
        self.n = 0


class _Scope:
    def __init__(self, b):
        self.b = b
        self.handles = []

    _uid = [0]

    def alloc(self, name, shape, dtype):
        _Scope._uid[0] += 1
        g = self.b.nc.sbuf_tensor(f"{name}_{_Scope._uid[0]}", shape, dtype)
        h = g.__enter__()
        self.handles.append(g)
        return h

    def close(self):
        for g in reversed(self.handles):
            g.__exit__(None, None, None)
        self.handles = []


class Builder:
    ENGS = ("sync", "gpsimd", "scalar", "vector", "tensor")

    def __init__(self):
        nc = bass.Bass("TRN2", target_bir_lowering=False)
        self.nc = nc
        self.st = {k: [] for k in self.ENGS}
        self._semcache = {}

        self.c_pe = self.sem("c_pe")
        self.c_gp = self.sem("c_gp")
        self.c_dv = self.sem("c_dv")
        self.c_ac = self.sem("c_ac")

        self.psA = [nc.alloc_psum_tensor(f"psA{i}", [128, 512], F32) for i in range(2)]
        self.psB = [nc.alloc_psum_tensor(f"psB{i}", [128, 512], F32) for i in range(2)]
        self.psT = [nc.alloc_psum_tensor(f"psT{i}", [128, 1024], BF16) for i in range(2)]
        self.psR = nc.alloc_psum_tensor("psR", [128, 512], F32)
        self.psF = nc.alloc_psum_tensor("psF", [128, 512], F32)
        self.bankfree = {}
        self.ring = {}

    def sem(self, name):
        if name not in self._semcache:
            self._semcache[name] = Sem(self.nc.alloc_semaphore(name=name))
        return self._semcache[name]

    def all_dma_sems(self):
        return [s for nm, s in self._semcache.items() if nm.startswith("d_")]

    def wait(self, eng, sem, val):
        if sem is None or val is None or val <= 0:
            return
        self.st[eng].append(lambda E, h=sem.h, v=val: E.wait_ge(h, v))

    def waitt(self, eng, tick):
        if tick is not None:
            self.wait(eng, tick[0], tick[1])

    def waitr(self, eng, key):
        self.waitt(eng, self.ring.get(key))

    def setr(self, key, tick):
        self.ring[key] = tick

    def dma(self, eng, sem, out, in_, **kw):
        sem.n += 16
        v = sem.n
        self.st[eng].append(
            lambda E, o=out, i=in_, h=sem.h, k=dict(kw): E.dma_start(out=o, in_=i, **k).then_inc(h, 16))
        return (sem, v)

    @staticmethod
    def _breg(E, bound):
        cache = getattr(E, "_bnd_regs", None)
        if cache is None:
            cache = {}
            E._bnd_regs = cache
        if bound not in cache:
            cache[bound] = E.to_reg(bound)
        return cache[bound]

    def gather(self, sem, out, src, idx, bound, accum=False):
        sem.n += 16
        v = sem.n
        if accum:
            kw = dict(compute_op=ALU.add, oob_is_err=True)
        else:
            kw = dict(oob_is_err=False)
        self.st["gpsimd"].append(
            lambda E, o=out, s2=src, i2=idx, h=sem.h, bb=bound, k=dict(kw):
            E.indirect_dma_start(out=o, out_offset=None, in_=s2,
                                 in_offset=IndirectOffsetOnAxis(i2, 0),
                                 bounds_check=Builder._breg(E, bb), **k).then_inc(h, 16))
        return (sem, v)

    def scatter(self, sem, dst, src, idx, bound):
        sem.n += 16
        v = sem.n
        self.st["gpsimd"].append(
            lambda E, d=dst, s2=src, i2=idx, h=sem.h, bb=bound:
            E.indirect_dma_start(out=d, out_offset=IndirectOffsetOnAxis(i2, 0),
                                 in_=s2, in_offset=None,
                                 bounds_check=Builder._breg(E, bb), oob_is_err=False).then_inc(h, 16))
        return (sem, v)

    def op(self, eng, fn, tick=False):
        if eng in ("vector", "scalar", "gpsimd"):
            sem = {"vector": self.c_dv, "scalar": self.c_ac, "gpsimd": self.c_gp}[eng]
            self.wait(eng, sem, sem.n)
            sem.n += 1
            v = sem.n
            self.st[eng].append(lambda E, f=fn, h=sem.h: f(E).then_inc(h, 1))
            return (sem, v)
        if not tick:
            self.st[eng].append(fn)
            return None
        sem = self.c_pe
        sem.n += 1
        v = sem.n
        self.st[eng].append(lambda E, f=fn, h=sem.h: f(E).then_inc(h, 1))
        return (sem, v)

    def pet(self):
        return (self.c_pe, self.c_pe.n)

    def dvt(self):
        return (self.c_dv, self.c_dv.n)

    def act(self):
        return (self.c_ac, self.c_ac.n)

    def alloc(self, name, shape, dtype):
        return self.nc.alloc_sbuf_tensor(name, shape, dtype)

    def scope(self):
        return _Scope(self)

    def bank_wait(self, bank):
        self.waitt("tensor", self.bankfree.get(bank.name))

    def bank_set(self, bank, tick):
        self.bankfree[bank.name] = tick

    def fence(self, exclude=()):
        marks = [(self.c_pe, self.c_pe.n), (self.c_dv, self.c_dv.n), (self.c_ac, self.c_ac.n),
                 (self.c_gp, self.c_gp.n)]
        ex = {id(s) for s in exclude}
        marks += [(s, s.n) for s in self.all_dma_sems() if s.n > 0 and id(s) not in ex]
        for eng in self.ENGS:
            for s, v in marks:
                self.wait(eng, s, v)

    def transposes(self, items, bank, ident, dst, pre_evac_wait=None, eng="vector"):
        self.bank_wait(bank)
        n = len(items)
        for i, (src, np_, off) in enumerate(items):
            self.op("tensor",
                    lambda E, s=src, o=off, pp=bank, idn=ident, q=np_, st=(i == 0), sp=(i == n - 1):
                    E.matmul(pp[0:q, o:o + 128], s, idn, start=st, stop=sp, is_transpose=True),
                    tick=(i == n - 1))
        tpe = self.pet()
        self.waitt(eng, tpe)
        if pre_evac_wait is not None:
            self.waitt(eng, pre_evac_wait)
        q = items[0][1]
        w = items[-1][2] + 128
        if eng == "vector":
            tev = self.op(eng,
                          lambda E, d=dst, pp=bank, q2=q, w2=w:
                          E.tensor_copy(d, pp[0:q2, 0:w2].rearrange("p (a x) -> p a x", x=128)),
                          tick=True)
        else:
            tev = self.op(eng,
                          lambda E, d=dst, pp=bank, q2=q, w2=w:
                          E.activation(out=d, in_=pp[0:q2, 0:w2].rearrange("p (a x) -> p a x", x=128),
                                       func=AF.Copy),
                          tick=True)
        self.bank_set(bank, tev)
        return tev

    def mm_chain(self, bank, cols, pieces, out_off=0):
        n = len(pieces)
        for i, (lh, rh) in enumerate(pieces):
            self.op("tensor",
                    lambda E, b2=bank, l=lh, r=rh, s=(i == 0), t=(i == n - 1), c=cols, oo=out_off:
                    E.matmul(b2[:, oo:oo + c], l, r, start=s, stop=t),
                    tick=(i == n - 1))
        return self.pet()


def build_program():
    b = Builder()
    nc = b.nc
    p = b

    d_cst = b.sem("d_cst")
    d_cstg = b.sem("d_cstg")
    d_cs2 = b.sem("d_cs2")
    d_al = [b.sem(f"d_al{i}") for i in range(8)]
    d_w = b.sem("d_w")
    d_st = b.sem("d_st")
    d_z = b.sem("d_z")
    d_sA = [b.sem(f"d_sA{i}") for i in range(2)]
    d_sB = [b.sem(f"d_sB{i}") for i in range(2)]
    d_sQ = [b.sem(f"d_sQ{i}") for i in range(2)]
    d_sV = [b.sem(f"d_sV{i}") for i in range(2)]
    d_c16 = [b.sem(f"d_c16{i}") for i in range(2)]
    d_sAt = b.sem("d_sAt")
    d_sx1 = [b.sem(f"d_sx1{i}") for i in range(2)]
    d_x = [b.sem(f"d_x{i}") for i in range(2)]
    d_q = [b.sem(f"d_q{i}") for i in range(2)]
    d_v = [b.sem(f"d_v{i}") for i in range(2)]
    d_sc = [b.sem(f"d_sc{i}") for i in range(2)]
    d_g = [b.sem(f"d_g{i}") for i in range(4)]
    d_y = [b.sem(f"d_y{i}") for i in range(2)]
    d_ga = [b.sem(f"d_ga{i}") for i in range(8)]
    d_ob = [b.sem(f"d_ob{i}") for i in range(2)]
    d_wd = b.sem("d_wd")
    d_wc = [b.sem(f"d_wc{i}") for i in range(2)]

    xin = nc.dram_tensor("xin", [SEQ, S, HD], F32, kind="ExternalInput").ap()
    w_kv_a = nc.dram_tensor("w_kv_a", [HD, FU], F32, kind="ExternalInput").ap()
    w_qb = nc.dram_tensor("w_qb", [QL, NQ], F32, kind="ExternalInput").ap()
    w_kvb = nc.dram_tensor("w_kvb", [KL, NKV], F32, kind="ExternalInput").ap()
    w_o = nc.dram_tensor("w_o", [NH * DV, HD], F32, kind="ExternalInput").ap()
    gate_w = nc.dram_tensor("gate_w", [HD, EXP], F32, kind="ExternalInput").ap()
    w_gate = nc.dram_tensor("w_gate", [EXP, HD, INTER], F32, kind="ExternalInput").ap()
    w_up = nc.dram_tensor("w_up", [EXP, HD, INTER], F32, kind="ExternalInput").ap()
    w_down = nc.dram_tensor("w_down", [EXP, INTER, HD], F32, kind="ExternalInput").ap()
    gamma_qb_i = nc.dram_tensor("gamma_qb", [128, QL], F32, kind="ExternalInput").ap()
    gamma_kvb_i = nc.dram_tensor("gamma_kvb", [128, KL], F32, kind="ExternalInput").ap()
    HC = host_consts()
    cs = {k: nc.dram_tensor("c_" + k, list(v.shape), F32, kind="ExternalInput").ap()
          for k, v in HC.items()}

    OUT = nc.dram_tensor("out", [TLOC, HD], F32, kind="ExternalOutput").ap()

    qcn_hbm = nc.dram_tensor("qcn_hbm", [NTT, 128, QL], BF16, kind="Internal").ap()
    kvn_hbm = nc.dram_tensor("kvn_hbm", [NTT, 128, KL], BF16, kind="Internal").ap()
    kpe_hbm = nc.dram_tensor("kpe_hbm", [NTT, 128, DR], BF16, kind="Internal").ap()
    Q_hbm = nc.dram_tensor("q_hbm", [SEQ, S, NQ], BF16, kind="Internal").ap()
    V_hbm = nc.dram_tensor("v_hbm", [SEQ, S, NH * DV], BF16, kind="Internal").ap()
    attn_hbm = nc.dram_tensor("attn_hbm", [SEQ, S, HD], BF16, kind="Internal").ap()
    x1f_hbm = nc.dram_tensor("x1f_hbm", [NTT, 128, HD], F32, kind="Internal").ap()
    x1b_hbm = nc.dram_tensor("x1b_hbm", [TLOC, HD], BF16, kind="Internal").ap()
    dtw_hbm = nc.dram_tensor("dtw_hbm", [NSLOT, 2], F32, kind="Internal").ap()
    base_hbm = nc.dram_tensor("base_hbm", [1, NTT, EXP], F32, kind="Internal").ap()
    cnt_hbm = nc.dram_tensor("cnt_hbm", [1, NTT, EXP], F32, kind="Internal").ap()
    y_hbm = nc.dram_tensor("y_hbm", [NSLOT, HD], BF16, kind="Internal").ap()

    taps = {}

    def tapt(name, shape, dtype):
        if name in TAPS:
            taps[name] = nc.dram_tensor("tap_" + name, shape, dtype, kind="ExternalOutput").ap()
            return taps[name]
        return None

    ident_bf = b.alloc("ident_bf", [128, 128], BF16)
    ident_f = b.alloc("ident_f", [128, 128], F32)
    lt_f = b.alloc("lt_f", [128, 128], F32)
    negtri = b.alloc("negtri", [128, 128], F32)
    iota32 = b.alloc("iota32", [128, 32], F32)
    iota32x4 = b.alloc("iota32x4", [128, 128], F32)
    iota_f = b.alloc("iota_f", [128, 1], F32)
    ones_col = b.alloc("ones_col", [128, 1], F32)
    ones_row = b.alloc("ones_row", [1, 128], F32)

    slot_all = b.alloc("slot_all", [128, NTT, 8], U32)
    tok_sb = b.alloc("tok_sb", [128, NTT, 8], F32)
    poisv = b.alloc("poisv", [1, 4], F32)
    scG = b.scope()
    cos_sb = scG.alloc("cos_sb", [128, 16, DR], F32)
    sin_sb = scG.alloc("sin_sb", [128, 16, DR], F32)
    gq = scG.alloc("gq", [128, QL], F32)
    gkv = scG.alloc("gkv", [128, KL], F32)

    p.dma("sync", d_cst, ident_f[:], cs["ident"])
    p.dma("gpsimd", d_cstg, ident_bf[:], cs["ident"])
    p.dma("sync", d_cst, lt_f[:], cs["lt"])
    p.dma("sync", d_cst, negtri[:], cs["negtri"])
    p.dma("sync", d_cst, iota32[:], cs["iota32"])
    p.dma("sync", d_cst, iota32x4[:], cs["iota32x4"])
    p.dma("sync", d_cst, tok_sb[:], cs["tok8"])
    p.dma("sync", d_cst, iota_f[:], cs["iota_f"])
    p.dma("sync", d_cst, ones_col[:], cs["ones_col"])
    p.dma("sync", d_cst, ones_row[:], cs["ones_row"])
    p.dma("sync", d_cst, cos_sb[:], cs["cos_t"].rearrange("(t p) d -> p t d", p=128))
    p.dma("sync", d_cst, sin_sb[:], cs["sin_t"].rearrange("(t p) d -> p t d", p=128))
    p.dma("sync", d_cst, gq[:], gamma_qb_i)
    p.dma("sync", d_cst, gkv[:], gamma_kvb_i)
    CST = (d_cst, d_cst.n)

    sc0 = b.scope()
    zer = sc0.alloc("zer", [128, 640], F32)
    tz = p.op("vector", lambda E: E.memset(zer[:], 0.0), tick=True)
    p.waitt("sync", tz)
    p.dma("sync", d_z, dtw_hbm.rearrange("(p j) c -> p (j c)", p=128), zer[:])

    if "A1" in BUILD:
        sc = b.scope()
        wkva = sc.alloc("wkva", [128, 16, FU], BF16)
        xt = sc.alloc("xt", [128, 2, HD], F32)
        xbf = sc.alloc("xbf", [128, 2, HD], BF16)
        xfm = sc.alloc("xfm", [128, 2, 16, 128], BF16)
        fus = sc.alloc("fus", [128, 2, FU], F32)
        sqs = sc.alloc("sqs", [128, QL], F32)
        qcn = sc.alloc("qcn", [128, 2, QL], BF16)
        qcnf = sc.alloc("qcnf", [128, 2, 12, 128], BF16)
        kvn = sc.alloc("kvn", [128, 2, KL], BF16)
        kpo = sc.alloc("kpo", [128, 2, DR], BF16)
        rsc = sc.alloc("rsc", [128, 12], F32)
        rps = sc.alloc("rps", [128, 2 * DR], F32)

        tw = p.dma("gpsimd", d_w, wkva[:], w_kv_a.rearrange("(c p) n -> p c n", p=128))
        p.waitt("tensor", tw)
        p.wait("tensor", d_cstg, 16)
        p.waitt("tensor", CST)
        p.waitt("vector", CST)
        p.waitt("scalar", CST)

        NCHUNKS = [(0, 512), (512, 1024), (1024, 1536), (1536, 2048), (2048, FU)]
        for g in range(NTT if DEV_G is None else DEV_G):
            s, t = divmod(g, 16)
            a = g % 2
            p.waitr("sync", f"xt{a}")
            tx = p.dma("sync", d_x[a], xt[:, a, :], xin[s, t * 128:(t + 1) * 128, :])
            p.waitt("vector", tx)
            p.waitr("vector", f"xbf{a}")
            t_cast = p.op("vector", lambda E, aa=a: E.tensor_copy(xbf[:, aa, :], xt[:, aa, :]), tick=True)
            b.setr(f"xt{a}", t_cast)
            p.waitt("tensor", t_cast)
            for grp in range(2):
                items = [(xbf[:, a, (grp * 8 + j) * 128:(grp * 8 + j + 1) * 128], 128, j * 128)
                         for j in range(8)]
                tev = b.transposes(items, b.psT[grp], ident_bf[:], xfm[:, a, grp * 8:grp * 8 + 8, :],
                                   pre_evac_wait=b.ring.get(f"xfm{a}"))
            b.setr(f"xbf{a}", b.pet())
            t_xfm = tev
            p.waitt("tensor", t_xfm)
            for ci, (n0, n1) in enumerate(NCHUNKS):
                bank = b.psA[ci % 2]
                b.bank_wait(bank)
                tmm = b.mm_chain(bank, n1 - n0,
                                 [(xfm[:, a, k, :], wkva[:, k, n0:n1]) for k in range(16)])
                eng = "vector" if ci % 2 == 0 else "scalar"
                p.waitt(eng, tmm)
                if eng == "vector":
                    tev = p.op(eng, lambda E, aa=a, m0=n0, m1=n1, bk=bank:
                               E.tensor_copy(fus[:, aa, m0:m1], bk[:, 0:m1 - m0]), tick=True)
                else:
                    tev = p.op(eng, lambda E, aa=a, m0=n0, m1=n1, bk=bank:
                               E.activation(out=fus[:, aa, m0:m1], in_=bk[:, 0:m1 - m0], func=AF.Copy), tick=True)
                b.bank_set(bank, tev)
            b.setr(f"xfm{a}", b.pet())
            t_fusv, t_fusa = b.dvt(), b.act()
            p.waitt("scalar", t_fusv)
            p.op("scalar", lambda E, aa=a: E.activation(out=sqs[:, 0:QL], in_=fus[:, aa, 0:QL],
                                                        func=AF.Square, accum_out=rsc[:, 0:1]))
            t_s2 = p.op("scalar", lambda E, aa=a: E.activation(out=sqs[:, 0:KL], in_=fus[:, aa, QL:QL + KL],
                                                               func=AF.Square, accum_out=rsc[:, 1:2]), tick=True)
            p.waitt("vector", t_s2)
            p.op("vector", lambda E: E.tensor_scalar(out=rsc[:, 2:3], in0=rsc[:, 0:1], scalar1=1.0 / QL,
                                                     scalar2=1e-6, op0=ALU.mult, op1=ALU.add))
            t_pre = p.op("vector", lambda E: E.tensor_scalar(out=rsc[:, 3:4], in0=rsc[:, 1:2], scalar1=1.0 / KL,
                                                             scalar2=1e-6, op0=ALU.mult, op1=ALU.add), tick=True)
            p.waitt("scalar", t_pre)
            t_sq = p.op("scalar", lambda E: E.activation(out=rsc[:, 4:6], in_=rsc[:, 2:4], func=AF.Sqrt), tick=True)
            p.waitt("vector", t_sq)
            p.op("vector", lambda E: E.reciprocal(rsc[:, 6:7], rsc[:, 4:5]))
            p.op("vector", lambda E: E.reciprocal(rsc[:, 7:8], rsc[:, 5:6]))
            p.waitr("vector", f"qcn{a}")
            p.waitr("vector", f"kvn{a}")
            p.op("vector", lambda E, aa=a: E.scalar_tensor_tensor(
                out=qcn[:, aa, :], in0=fus[:, aa, 0:QL], scalar=rsc[:, 6:7], in1=gq[:],
                op0=ALU.mult, op1=ALU.mult))
            t_kvn = p.op("vector", lambda E, aa=a: E.scalar_tensor_tensor(
                out=kvn[:, aa, :], in0=fus[:, aa, QL:QL + KL], scalar=rsc[:, 7:8], in1=gkv[:],
                op0=ALU.mult, op1=ALU.mult), tick=True)
            kp = fus[:, a, QL + KL:FU]
            cv = cos_sb[:, t, :]
            sv = sin_sb[:, t, :]
            p.op("vector", lambda E, x=kp, c2=cv: E.tensor_mul(rps[:, 0:DR], x, c2))
            p.op("vector", lambda E, x=kp, s2=sv: E.tensor_mul(rps[:, DR:DR + 32], x[:, 32:64], s2[:, 0:32]))
            p.op("vector", lambda E, x=kp, s2=sv: E.tensor_mul(rps[:, DR + 32:DR + 64], x[:, 0:32], s2[:, 32:64]))
            p.op("vector", lambda E, aa=a: E.tensor_sub(kpo[:, aa, 0:32], rps[:, 0:32], rps[:, DR:DR + 32]))
            t_kpo = p.op("vector", lambda E, aa=a: E.tensor_add(kpo[:, aa, 32:64], rps[:, 32:64],
                                                                rps[:, DR + 32:DR + 64]), tick=True)
            p.waitt("tensor", t_kvn)
            for grp, ntr in ((0, 8), (1, 4)):
                items = [(qcn[:, a, (grp * 8 + j) * 128:(grp * 8 + j + 1) * 128], 128, j * 128)
                         for j in range(ntr)]
                tev = b.transposes(items, b.psT[grp], ident_bf[:],
                                   qcnf[:, a, grp * 8:grp * 8 + ntr, :],
                                   pre_evac_wait=b.ring.get(f"qcnf{a}"))
            b.setr(f"qcn{a}", b.pet())
            t_qf = tev
            p.waitt("sync", t_qf)
            p.waitt("sync", t_kpo)
            tst = p.dma("sync", d_sA[a], qcn_hbm[g], qcnf[:, a, :, :])
            b.setr(f"qcnf{a}", tst)
            p.dma("sync", d_sB[a], kvn_hbm[g], kvn[:, a, :])
            tst = p.dma("sync", d_sB[a], kpe_hbm[g], kpo[:, a, :])
            b.setr(f"kvn{a}", tst)

        tp = tapt("fused31", [128, FU], F32)
        if tp is not None:
            p.dma("sync", d_st, tp, fus[:, 1, :])
        tp = tapt("qcn31", [128, QL], BF16)
        if tp is not None:
            p.dma("sync", d_st, tp, qcn[:, 1, :])
        b.fence()
        sc.close()
        b.ring.clear()

    if "A2" in BUILD:
        sc = b.scope()
        wqb = sc.alloc("wqb", [128, 12, NQ], BF16)
        qti = sc.alloc("qti", [128, 2, QL], BF16)
        qf32 = sc.alloc("qf32", [128, 2, NQ], F32)
        qbf = sc.alloc("qbf", [128, 2, NQ], BF16)
        rp2 = sc.alloc("rp2", [128, NH, 2 * DR], F32)
        c16 = sc.alloc("c16", [128, 2, 2, NH, DR], F32)

        tw = p.dma("gpsimd", d_w, wqb[:], w_qb.rearrange("(c p) n -> p c n", p=128))
        p.waitt("tensor", tw)

        for g in range(NTT if DEV_G is None else DEV_G):
            s, t = divmod(g, 16)
            a = g % 2
            p.waitr("gpsimd", f"qti{a}")
            tq = p.dma("gpsimd", d_q[a], qti[:, a, :], qcn_hbm[g])
            p.waitr("gpsimd", f"c16{a}")
            p.dma("gpsimd", d_c16[a], c16[:, a, 0], cs["cos16"].rearrange("(t p) h d -> t p h d", p=128)[t])
            tc16 = p.dma("gpsimd", d_c16[a], c16[:, a, 1], cs["sin16"].rearrange("(t p) h d -> t p h d", p=128)[t])
            p.waitt("tensor", tq)
            p.waitr("tensor", f"qf{a}")
            for ci in range(6):
                n0 = ci * 512
                bank = b.psA[ci % 2]
                b.bank_wait(bank)
                tmm = b.mm_chain(bank, 512,
                                 [(qti[:, a, k * 128:(k + 1) * 128], wqb[:, k, n0:n0 + 512])
                                  for k in range(12)])
                eng = "vector" if ci % 2 == 0 else "scalar"
                p.waitt(eng, tmm)
                if eng == "vector":
                    tev = p.op(eng, lambda E, aa=a, m0=n0, bk=bank:
                               E.tensor_copy(qf32[:, aa, m0:m0 + 512], bk[:]), tick=True)
                else:
                    tev = p.op(eng, lambda E, aa=a, m0=n0, bk=bank:
                               E.activation(out=qf32[:, aa, m0:m0 + 512], in_=bk[:], func=AF.Copy), tick=True)
                b.bank_set(bank, tev)
            b.setr(f"qti{a}", b.pet())
            p.waitt("vector", b.act())
            p.waitt("vector", tc16)
            xpe = qf32[:, a, :].rearrange("p (h d) -> p h d", d=DN + DR)[:, :, DN:DN + DR]
            cv16 = c16[:, a, 0]
            sv16 = c16[:, a, 1]
            p.op("vector", lambda E, x=xpe, c2=cv16: E.tensor_tensor(rp2[:, :, 0:DR], x, c2, ALU.mult))
            p.op("vector", lambda E, x=xpe, s2=sv16:
                 E.tensor_tensor(rp2[:, :, DR:DR + 32], x[:, :, 32:64], s2[:, :, 0:32], ALU.mult))
            p.op("vector", lambda E, x=xpe, s2=sv16:
                 E.tensor_tensor(rp2[:, :, DR + 32:DR + 64], x[:, :, 0:32], s2[:, :, 32:64], ALU.mult))
            p.op("vector", lambda E, x=xpe: E.tensor_tensor(x[:, :, 0:32], rp2[:, :, 0:DR][:, :, 0:32],
                                                            rp2[:, :, DR:DR + 32], ALU.subtract))
            tr5 = p.op("vector", lambda E, x=xpe: E.tensor_tensor(x[:, :, 32:64], rp2[:, :, 32:64],
                                                                  rp2[:, :, DR + 32:DR + 64], ALU.add), tick=True)
            b.setr(f"c16{a}", tr5)
            p.waitr("vector", f"qbf{a}")
            t_rope = p.op("vector", lambda E, aa=a: E.tensor_scalar(
                out=qbf[:, aa, :], in0=qf32[:, aa, :], scalar1=SCALE, scalar2=None, op0=ALU.mult), tick=True)
            b.setr(f"qf{a}", t_rope)
            p.waitt("sync", t_rope)
            tst = p.dma("sync", d_sQ[a], Q_hbm[s, t * 128:(t + 1) * 128, :], qbf[:, a, :])
            b.setr(f"qbf{a}", tst)

        tp = tapt("q31", [128, NQ], BF16)
        if tp is not None:
            p.dma("sync", d_st, tp, qbf[:, 1, :])
        b.fence()
        sc.close()
        b.ring.clear()

    sc0.close()
    scG.close()

    if "A3B" in BUILD or "A3" in BUILD or "B" in BUILD:
        DO_B = ("A3B" in BUILD) or ("B" in BUILD)
        scK = b.scope()
        KN = scK.alloc("KN", [128, NH, S], BF16)
        KP = scK.alloc("KP", [128, S], BF16)
        VS = scK.alloc("VS", [128, S // 128, NH * DV], BF16)

        def _seq_body(s):
            sc = b.scope()
            wkvb = sc.alloc("wkvb", [128, 4, NKV], BF16)
            kvi = sc.alloc("kvi", [128, 2, KL], BF16)
            kpi = sc.alloc("kpi", [128, 2, DR], BF16)
            kvf = sc.alloc("kvf", [128, 2, 4, 128], BF16)
            knb = sc.alloc("knb", [128, 2, NH, DN], BF16)

            t_wkvb = p.dma("gpsimd", d_w, wkvb[:], w_kvb.rearrange("(c p) n -> p c n", p=128))
            p.waitt("tensor", t_wkvb)
            for t in range(16 if DEV_T is None else DEV_T):
                g = s * 16 + t
                a = t % 2
                p.waitr("gpsimd", f"kvi{a}")
                p.dma("gpsimd", d_q[a], kvi[:, a, :], kvn_hbm[g])
                tk2 = p.dma("gpsimd", d_q[a], kpi[:, a, :], kpe_hbm[g])
                p.waitt("tensor", tk2)
                items = [(kvi[:, a, j * 128:(j + 1) * 128], 128, j * 128) for j in range(4)]
                tev = b.transposes(items, b.psT[0], ident_bf[:], kvf[:, a, :, :],
                                   pre_evac_wait=b.ring.get(f"kvf{a}"))
                p.waitt("tensor", tev)
                p.waitr("tensor", f"knb{a}")
                for ci in range(8):
                    n0 = ci * 512
                    bank = b.psA[ci % 2]
                    b.bank_wait(bank)
                    tmm = b.mm_chain(bank, 512,
                                     [(kvf[:, a, k, :], wkvb[:, k, n0:n0 + 512]) for k in range(4)])
                    eng = "vector" if ci % 2 == 0 else "scalar"
                    p.waitt(eng, tmm)
                    pvw = bank[:].rearrange("p (h c) -> p h c", c=256)
                    vdst = VS[:, t, :].rearrange("p (h x) -> p h x", x=128)[:, 2 * ci:2 * ci + 2, :]
                    if eng == "vector":
                        p.op(eng, lambda E, aa=a, h0=2 * ci, q=pvw:
                             E.tensor_copy(knb[:, aa, h0:h0 + 2, :], q[:, :, 0:128]))
                        tev = p.op(eng, lambda E, q=pvw, vd=vdst:
                                   E.tensor_copy(vd, q[:, :, 128:256]), tick=True)
                    else:
                        p.op(eng, lambda E, aa=a, h0=2 * ci, q=pvw:
                             E.activation(out=knb[:, aa, h0:h0 + 2, :], in_=q[:, :, 0:128], func=AF.Copy))
                        tev = p.op(eng, lambda E, q=pvw, vd=vdst:
                                   E.activation(out=vd, in_=q[:, :, 128:256], func=AF.Copy),
                                   tick=True)
                    b.bank_set(bank, tev)
                b.setr(f"kvf{a}", b.pet())
                t_ev_v, t_ev_a = b.dvt(), b.act()
                p.waitt("tensor", t_ev_v)
                p.waitt("tensor", t_ev_a)
                for grp in range(2):
                    items = [(knb[:, a, grp * 8 + j, :], 128, j * 128) for j in range(8)]
                    dst = KN[:, grp * 8:grp * 8 + 8, t * 128:(t + 1) * 128]
                    b.transposes(items, b.psT[grp], ident_bf[:], dst)
                b.setr(f"knb{a}", b.pet())
                b.bank_wait(b.psT[0])
                p.op("tensor", lambda E, aa=a: E.transpose(b.psT[0][0:64, 0:128], kpi[:, aa, :], ident_bf[:]),
                     tick=True)
                tpe = b.pet()
                b.setr(f"kvi{a}", tpe)
                p.waitt("vector", tpe)
                tev = p.op("vector", lambda E, tt=t: E.tensor_copy(KP[0:64, tt * 128:(tt + 1) * 128],
                                                                   b.psT[0][0:64, 0:128]), tick=True)
                b.bank_set(b.psT[0], tev)
            b.fence()
            sc.close()

            if not DO_B:
                return
            sc = b.scope()
            q_in = sc.alloc("q_in", [128, 2, NQ], BF16)
            Qn = sc.alloc("Qn", [128, 32, 128], BF16)
            Qp = sc.alloc("Qp", [128, 32, 128], BF16)
            sc_f = sc.alloc("sc_f", [128, 2, 2, S], BF16)
            P_sb = sc.alloc("P_sb", [128, 4, 512], BF16)
            PT_sb = sc.alloc("PT", [128, 2, 4, 128], BF16)
            mtl = sc.alloc("mtl", [128, 2, 8], F32)
            nmw = sc.alloc("nmw", [128, 2, 2], F32)
            rs8 = sc.alloc("rs8", [128, 2, 8], F32)
            lsr = sc.alloc("lsr", [128, 2, 4], F32)
            atb = sc.alloc("atb", [128, 2, HD], BF16)

            nm_t = {}
            NHW = 2
            NW = 16 // NHW * 2

            def pass1(w, qb):
                u = w % 2
                qt = w // (NW // 2)
                hp = w % (NW // 2)
                q0 = qb * 256 + qt * 128
                nkb = (q0 + 128 + 511) // 512
                sbanks = [b.psR, b.psF, b.psB[0], b.psB[1]]
                for hi in range(NHW):
                    h = hp * NHW + hi
                    hq = h * 2 + qt
                    for kb in range(nkb):
                        ncols = min(512, q0 + 128 - kb * 512)
                        full = (kb * 512 + 512 <= q0 + 1)
                        bank = sbanks[(hi * nkb + kb) % 4]
                        b.bank_wait(bank)
                        p.op("tensor", lambda E, bk=bank, q2=hq, c=ncols, k0=kb * 512:
                             E.matmul(bk[:, 0:c], Qn[:, q2, :], KN[:, q2 // 2, k0:k0 + c],
                                      start=True, stop=False))
                        tsc2 = p.op("tensor", lambda E, bk=bank, q2=hq, c=ncols, k0=kb * 512:
                                    E.matmul(bk[:, 0:c], Qp[0:64, q2, :], KP[0:64, k0:k0 + c],
                                             start=False, stop=True), tick=True)
                        p.waitt("vector", tsc2)
                        if hi == 0 and kb == 0:
                            p.waitr("vector", f"scf{u}")
                        if not full:
                            dlt = ncols - 128
                            p.op("vector", lambda E, bk=bank, d2=dlt:
                                 E.tensor_add(bk[:, d2:d2 + 128], bk[:, d2:d2 + 128], negtri[:]))
                        tev = p.op("vector", lambda E, bk=bank, uu=u, c=ncols, k0=kb * 512, h2=hi, kk=kb:
                                   E.tensor_scalar(out=sc_f[:, uu, h2, k0:k0 + c], in0=bk[:, 0:c],
                                                   scalar1=1.0, scalar2=NEG,
                                                   op0=ALU.mult, op1=ALU.max,
                                                   accum_out=mtl[:, uu, h2 * 4 + kk:h2 * 4 + kk + 1]),
                                   tick=True)
                        b.bank_set(bank, tev)
                p.op("vector", lambda E, uu=u, nk=nkb:
                     E.reduce_max(nmw[:, uu, :], mtl[:, uu, :].rearrange("p (h k) -> p h k", k=4)[:, :, 0:nk],
                                  axis=AX.X, negate=True))
                nm_t[w] = b.dvt()

            def pass2(w, qb):
                u = w % 2
                qt = w // (NW // 2)
                hp = w % (NW // 2)
                q0 = qb * 256 + qt * 128
                nkb = (q0 + 128 + 511) // 512
                p.waitt("scalar", nm_t[w])
                vbanks = [b.psA[0], b.psA[1]]

                def pv(hi, kb, u2):
                    h = hp * NHW + hi
                    bank2 = vbanks[hi % 2]
                    ncols = min(512, q0 + 128 - kb * 512)
                    nsub = ncols // 128
                    if kb == 0:
                        b.bank_wait(bank2)
                    p.waitt("tensor", pt_t[u2])
                    for sub in range(nsub):
                        st2 = (kb == 0 and sub == 0)
                        sp2 = (kb == nkb - 1 and sub == nsub - 1)
                        p.op("tensor", lambda E, bk=bank2, u3=u2, s3=sub, h2=h, kt=kb * 4 + sub,
                             st3=st2, sp3=sp2:
                             E.matmul(bk[:, 0:128], PT_sb[:, u3, s3, :],
                                      VS[:, kt, h2 * DV:(h2 + 1) * DV], start=st3, stop=sp3),
                             tick=sp2)
                    b.setr(f"PT{u2}", b.pet())

                units = [(hi, kb) for hi in range(NHW) for kb in range(nkb)]
                pt_t = {}
                t_exp = {}

                def emit_exp(idx):
                    hi, kb = units[idx]
                    ncols = min(512, q0 + 128 - kb * 512)
                    u4 = idx % 4
                    p.waitr("scalar", f"P{u4}")
                    t_exp[idx] = p.op(
                        "scalar", lambda E, uu=u, u3=u4, c=ncols, k0=kb * 512, h2=hi, kk=kb:
                        E.activation(out=P_sb[:, u3, 0:c], in_=sc_f[:, uu, h2, k0:k0 + c],
                                     func=AF.Exp, bias=nmw[:, uu, h2:h2 + 1],
                                     accum_out=rs8[:, uu, h2 * 4 + kk:h2 * 4 + kk + 1]), tick=True)

                for i2 in range(min(2, len(units))):
                    emit_exp(i2)
                prev = None
                for idx, (hi, kb) in enumerate(units):
                    ncols = min(512, q0 + 128 - kb * 512)
                    nsub = ncols // 128
                    u2 = idx % 2
                    u4 = idx % 4
                    items = [(P_sb[:, u4, j * 128:(j + 1) * 128], 128, j * 128) for j in range(nsub)]
                    p.waitt("tensor", t_exp[idx])
                    pt_t[u2] = b.transposes(items, b.psT[u2], ident_bf[:], PT_sb[:, u2, 0:nsub, :],
                                            pre_evac_wait=b.ring.get(f"PT{u2}"),
                                            eng="vector" if u2 == 0 else "scalar")
                    b.setr(f"P{u4}", b.pet())
                    if idx + 2 < len(units):
                        emit_exp(idx + 2)
                    if prev is not None:
                        pv(*prev)
                    prev = (hi, kb, u2)
                pv(*prev)
                tpv = b.pet()
                p.waitt("vector", tpv)
                p.op("vector", lambda E, uu=u, nk=nkb:
                     E.reduce_sum(lsr[:, uu, 0:NHW], rs8[:, uu, :].rearrange("p (h k) -> p h k", k=4)[:, :, 0:nk],
                                  axis=AX.X))
                p.op("vector", lambda E, uu=u: E.reciprocal(lsr[:, uu, NHW:2 * NHW], lsr[:, uu, 0:NHW]))
                p.waitr("vector", "atb")
                for hi in range(NHW):
                    h = hp * NHW + hi
                    tat = p.op("vector", lambda E, uu=u, qt2=qt, h2=h, h3=hi, bk=vbanks[hi % 2]:
                               E.tensor_scalar(out=atb[:, qt2, h2 * DV:(h2 + 1) * DV], in0=bk[:, 0:128],
                                               scalar1=lsr[:, uu, NHW + h3:NHW + h3 + 1], scalar2=None,
                                               op0=ALU.mult), tick=True)
                    b.bank_set(vbanks[hi % 2], tat)
                b.setr(f"scf{u}", b.act())
                return tat

            for qb in range(8 if DEV_QB is None else DEV_QB):
                q0b = qb * 256
                p.waitr("gpsimd", "q_in")
                tq = p.dma("gpsimd", d_q[qb % 2], q_in[:],
                           Q_hbm[s, q0b:q0b + 256, :].rearrange("(a p) n -> p a n", p=128))
                p.waitt("tensor", tq)
                for grp in range(4):
                    items = []
                    for i in range(8):
                        hq = grp * 8 + i
                        h, qt = divmod(hq, 2)
                        items.append((q_in[:, qt, h * (DN + DR):h * (DN + DR) + DN], 128, i * 128))
                    b.transposes(items, b.psT[grp % 2], ident_bf[:], Qn[:, grp * 8:grp * 8 + 8, :],
                                 pre_evac_wait=b.ring.get("Qfm"))
                for grp in range(4):
                    items = []
                    for i in range(8):
                        hq = grp * 8 + i
                        h, qt = divmod(hq, 2)
                        items.append((q_in[:, qt, h * (DN + DR) + DN:h * (DN + DR) + DN + DR], 64, i * 128))
                    b.transposes(items, b.psT[grp % 2], ident_bf[:], Qp[0:64, grp * 8:grp * 8 + 8, :],
                                 pre_evac_wait=b.ring.get("Qfm"))
                b.setr("q_in", b.pet())
                for w in range(NW):
                    pass1(w, qb)
                    if w >= 1:
                        pass2(w - 1, qb)
                t_last = pass2(NW - 1, qb)
                p.waitt("sync", b.dvt())
                p.waitr("sync", "atb")
                tst = p.dma("sync", d_sAt,
                            attn_hbm[s, q0b:q0b + 256, :].rearrange("(a p) n -> p a n", p=128), atb[:])
                b.setr("atb", tst)
                b.setr("Qfm", b.pet())
            tp = tapt(f"atb_s{s}", [128, 2, HD], BF16)
            if tp is not None:
                p.dma("sync", d_st, tp, atb[:])
            b.fence()
            sc.close()
            b.ring.clear()
        for s in range(SEQ if DEV_SEQ is None else DEV_SEQ):
            _seq_body(s)
        scK.close()

    if "B2" in BUILD:
        b.fence()
        sc = b.scope()
        wo = sc.alloc("wo", [128, 16, HD], BF16)
        ati = sc.alloc("ati", [128, 2, HD], BF16)
        afm = sc.alloc("afm", [128, 2, 16, 128], BF16)
        xr = sc.alloc("xr", [128, 2, HD], F32)
        x1 = sc.alloc("x1", [128, 2, HD], F32)
        x1b = sc.alloc("x1b", [128, 2, HD], BF16)
        x1fm = sc.alloc("x1fm", [128, 2, 16, 128], F32)
        sq2 = sc.alloc("sq2", [128, HD], F32)
        gwsb = sc.alloc("gwsb", [128, 16, EXP], F32)
        lg = sc.alloc("lg", [128, EXP], F32)
        lgr = sc.alloc("lgr", [128, EXP], F32)
        tv8 = sc.alloc("tv8", [128, 8], F32)
        ti8 = sc.alloc("ti8", [128, 8], U32)
        ex8 = sc.alloc("ex8", [128, 8], F32)
        sm8 = sc.alloc("sm8", [128, 4], F32)
        lnst = sc.alloc("lnst", [128, 12], F32)
        I8a = sc.alloc("I8a", [128, NTT, EXP], F32)
        idx8f = sc.alloc("idx8f", [128, NTT, 8], F32)
        pv_all = sc.alloc("pv_all", [128, NTT, 8, 2], F32)
        rm4 = sc.alloc("rm4", [128, 128], F32)
        dchain = sc.alloc("dchain", [128, 2, 128], F32)
        smap4 = sc.alloc("smap4", [128, 128], F32)
        pmap4 = sc.alloc("pmap4", [128, 128], F32)
        scrv = sc.alloc("scrv", [128, EXP], F32)
        scrg = sc.alloc("scrg", [128, EXP], F32)
        slotf = sc.alloc("slotf", [128, 8], F32)
        posf = sc.alloc("posf", [128, 8], F32)
        posu = sc.alloc("posu", [128, 2, 8], U32)
        cnt32 = sc.alloc("cnt32", [128, EXP], F32)
        base32 = sc.alloc("base32", [128, EXP], F32)
        pois2 = sc.alloc("pois2", [1, EXP], F32)
        baseB = sc.alloc("baseB", [1, NTT * EXP], F32)
        cntrow = sc.alloc("cntrow", [1, NTT * EXP], F32)

        p.wait("gpsimd", d_z, d_z.n)
        tw = p.dma("gpsimd", d_w, wo[:], w_o.rearrange("(c p) n -> p c n", p=128))
        tgw = p.dma("sync", d_cs2, gwsb[:], gate_w.rearrange("(c p) n -> p c n", p=128))
        t_tok = p.op("vector", lambda E: E.tensor_copy(pv_all[:, :, :, 0], tok_sb[:]), tick=True)
        p.waitt("tensor", tw)
        p.waitt("tensor", tgw)

        NG2 = NTT if DEV_G is None else DEV_G

        def b2_stage1(g):
            s, t = divmod(g, 16)
            a = g % 2
            p.waitr("gpsimd", f"ati{a}")
            ta = p.dma("gpsimd", d_q[a], ati[:, a, :], attn_hbm[s, t * 128:(t + 1) * 128, :])
            p.waitr("gpsimd", f"xr{a}")
            txr = p.dma("gpsimd", d_x[a], xr[:, a, :], xin[s, t * 128:(t + 1) * 128, :])
            p.waitt("tensor", ta)
            for grp in range(2):
                items = [(ati[:, a, (grp * 8 + j) * 128:(grp * 8 + j + 1) * 128], 128, j * 128)
                         for j in range(8)]
                b.transposes(items, b.psT[grp], ident_bf[:], afm[:, a, grp * 8:grp * 8 + 8, :],
                             pre_evac_wait=b.ring.get(f"afm{a}"))
            b.setr(f"ati{a}", b.pet())
            t_afm = b.dvt()
            p.waitt("tensor", t_afm)
            p.waitt("vector", txr)
            p.waitr("vector", f"x1{a}")
            p.waitr("vector", f"x1b{a}")
            for ci in range(4):
                n0 = ci * 512
                bank = b.psA[ci % 2]
                b.bank_wait(bank)
                tmm = b.mm_chain(bank, 512,
                                 [(afm[:, a, k, :], wo[:, k, n0:n0 + 512]) for k in range(16)])
                p.waitt("vector", tmm)
                tev = p.op("vector", lambda E, aa=a, m0=n0, bk=bank, c2=ci:
                           E.scalar_tensor_tensor(out=x1[:, aa, m0:m0 + 512], in0=bk[:],
                                                  scalar=1.0, in1=xr[:, aa, m0:m0 + 512],
                                                  op0=ALU.mult, op1=ALU.add,
                                                  accum_out=lnst[:, c2:c2 + 1]), tick=True)
                b.bank_set(bank, tev)
            b.setr(f"afm{a}", b.pet())
            b.setr(f"xr{a}", b.dvt())
            p.op("vector", lambda E: E.tensor_add(lnst[:, 4:5], lnst[:, 0:1], lnst[:, 1:2]))
            p.op("vector", lambda E: E.tensor_add(lnst[:, 5:6], lnst[:, 2:3], lnst[:, 3:4]))
            t_resid = p.op("vector", lambda E: E.tensor_add(lnst[:, 6:7], lnst[:, 4:5], lnst[:, 5:6]), tick=True)
            p.waitt("scalar", t_resid)
            t_s2 = p.op("scalar", lambda E, aa=a: E.activation(out=sq2[:], in_=x1[:, aa, :],
                                                               func=AF.Square, accum_out=lnst[:, 7:8]), tick=True)
            p.op("vector", lambda E: E.tensor_scalar(out=lnst[:, 8:9], in0=lnst[:, 6:7],
                                                     scalar1=1.0 / HD, scalar2=None, op0=ALU.mult))
            p.waitt("vector", t_s2)
            p.op("vector", lambda E: E.tensor_scalar(out=lnst[:, 9:10], in0=lnst[:, 7:8],
                                                     scalar1=1.0 / HD, scalar2=None, op0=ALU.mult))
            p.op("vector", lambda E: E.tensor_mul(lnst[:, 10:11], lnst[:, 8:9], lnst[:, 8:9]))
            p.op("vector", lambda E: E.scalar_tensor_tensor(out=lnst[:, 10:11], in0=lnst[:, 10:11],
                                                            scalar=-1.0, in1=lnst[:, 9:10],
                                                            op0=ALU.mult, op1=ALU.add))
            t_ve = p.op("vector", lambda E: E.tensor_scalar(out=lnst[:, 11:12], in0=lnst[:, 10:11],
                                                            scalar1=1.0, scalar2=1e-5,
                                                            op0=ALU.mult, op1=ALU.add), tick=True)
            p.waitt("scalar", t_ve)
            t_sqr = p.op("scalar", lambda E: E.activation(out=lnst[:, 10:11], in_=lnst[:, 11:12],
                                                          func=AF.Sqrt), tick=True)
            p.waitt("vector", t_sqr)
            p.op("vector", lambda E: E.reciprocal(lnst[:, 11:12], lnst[:, 10:11]))
            p.op("vector", lambda E, aa=a: E.tensor_scalar(out=x1[:, aa, :], in0=x1[:, aa, :],
                                                           scalar1=lnst[:, 8:9], scalar2=lnst[:, 11:12],
                                                           op0=ALU.subtract, op1=ALU.mult))
            t_x1 = p.op("vector", lambda E, aa=a: E.tensor_copy(x1b[:, aa, :], x1[:, aa, :]), tick=True)
            p.waitt("sync", t_x1)
            p.dma("sync", d_sx1[a], x1f_hbm[g], x1[:, a, :])
            tst = p.dma("sync", d_sx1[a], x1b_hbm[g * 128:(g + 1) * 128, :], x1b[:, a, :])
            b.setr(f"x1b{a}", tst)
            p.waitt("tensor", t_x1)
            for grp in range(4):
                bank = b.psF if grp % 2 == 0 else b.psR
                b.bank_wait(bank)
                for i in range(4):
                    c2 = grp * 4 + i
                    p.op("tensor", lambda E, aa=a, cc=c2, o=i * 128, bk=bank, st=(i == 0), sp=(i == 3):
                         E.matmul(bk[:, o:o + 128], x1[:, aa, cc * 128:(cc + 1) * 128], ident_f[:],
                                  start=st, stop=sp, is_transpose=True),
                         tick=(i == 3))
                tpe = b.pet()
                p.waitt("vector", tpe)
                if grp < 2:
                    p.waitr("vector", f"x1fm{a}")
                tev = p.op("vector", lambda E, aa=a, g2=grp, bk=bank:
                           E.tensor_copy(x1fm[:, aa, g2 * 4:g2 * 4 + 4, :],
                                         bk[:].rearrange("p (c x) -> p c x", c=4)), tick=True)
                b.bank_set(bank, tev)
            t_fm = b.dvt()
            b.setr(f"x1{a}", b.pet())
            p.waitt("tensor", t_fm)
            bank = b.psR
            b.bank_wait(bank)
            tlg = b.mm_chain(bank, EXP, [(x1fm[:, a, k, :], gwsb[:, k, :]) for k in range(16)])
            b.setr(f"x1fm{a}", tlg)
            p.waitt("vector", tlg)
            tev = p.op("vector", lambda E, bk=bank: E.tensor_copy(lg[:], bk[:, 0:EXP]), tick=True)
            b.bank_set(bank, tev)
            p.waitr("vector", "tv8")
            p.op("vector", lambda E: E.max(tv8[:], lg[:]))
            p.op("vector", lambda E: E.max_index(ti8[:], tv8[:], lg[:]))
            p.op("vector", lambda E, g2=g: E.tensor_copy(idx8f[:, g2, :], ti8[:]))
            p.op("vector", lambda E: E.match_replace(lgr[:], tv8[:], lg[:], NEG))
            t_i8g = p.op("vector", lambda E, g2=g: E.tensor_tensor(I8a[:, g2, :], lg[:], lgr[:],
                                                                   ALU.not_equal), tick=True)
            t_nm8 = p.op("vector", lambda E: E.tensor_scalar(out=sm8[:, 0:1], in0=tv8[:, 0:1],
                                                             scalar1=-1.0, scalar2=None, op0=ALU.mult),
                         tick=True)
            p.waitt("scalar", t_nm8)
            p.waitr("scalar", "ex8")
            t_e8 = p.op("scalar", lambda E: E.activation(out=ex8[:], in_=tv8[:], func=AF.Exp,
                                                         bias=sm8[:, 0:1], accum_out=sm8[:, 1:2]), tick=True)
            b.setr("tv8", t_e8)
            p.waitt("vector", t_e8)
            p.op("vector", lambda E: E.reciprocal(sm8[:, 2:3], sm8[:, 1:2]))
            t_w8 = p.op("vector", lambda E, g2=g: E.tensor_scalar(out=pv_all[:, g2, :, 1], in0=ex8[:],
                                                                  scalar1=sm8[:, 2:3], scalar2=None,
                                                                  op0=ALU.mult), tick=True)
            b.setr("ex8", t_w8)
            p.waitt("tensor", t_i8g)
            p.op("tensor", lambda E, g2=g: E.matmul(
                b.psB[g2 // 16][0:1, (g2 % 16) * EXP:(g2 % 16 + 1) * EXP], ones_col[:],
                I8a[:, g2, :], start=True, stop=True),
                 tick=(g == NG2 - 1))

        for g in range(NG2):
            b2_stage1(g)

        t_cntmm = b.pet()
        p.waitt("vector", t_cntmm)
        p.op("vector", lambda E: E.tensor_copy(cntrow[0:1, 0:512], b.psB[0][0:1, 0:512]))
        t_cr = p.op("vector", lambda E: E.tensor_copy(cntrow[0:1, 512:1024], b.psB[1][0:1, 0:512]),
                    tick=True)
        b.bank_set(b.psB[0], t_cr)
        b.bank_set(b.psB[1], t_cr)
        p.waitt("sync", t_cr)
        tsc = p.dma("sync", d_st, cnt_hbm.rearrange("a g e -> a (g e)"), cntrow[0:1, :])
        p.waitt("sync", tsc)
        tlc = p.dma("sync", d_st, cnt32[0:32, :], cnt_hbm[0])
        p.waitt("tensor", tlc)
        b.bank_wait(b.psB[1])
        p.op("tensor", lambda E: E.matmul(b.psB[1][0:32, 0:EXP], lt_f[0:32, 0:32], cnt32[0:32, :],
                                          start=True, stop=True))
        t_pfx = p.op("tensor", lambda E: E.matmul(b.psB[1][0:1, EXP:2 * EXP], ones_col[0:32, :],
                                                  cnt32[0:32, :], start=True, stop=True), tick=True)
        p.waitt("vector", t_pfx)
        p.op("vector", lambda E: E.tensor_copy(base32[0:32, :], b.psB[1][0:32, 0:EXP]))
        t_b32 = p.op("vector", lambda E: E.tensor_copy(pois2[0:1, :], b.psB[1][0:1, EXP:2 * EXP]), tick=True)
        b.bank_set(b.psB[1], t_b32)
        p.waitt("sync", t_b32)
        tsb = p.dma("sync", d_st, base_hbm[0], base32[0:32, :])
        p.waitt("sync", tsb)
        tlb = p.dma("sync", d_st, baseB[0:1, :], base_hbm.rearrange("a g e -> a (g e)"))
        p.op("vector", lambda E: E.reduce_max(poisv[0:1, 0:1], pois2[0:1, :], axis=AX.X))
        p.op("vector", lambda E: E.tensor_scalar(out=poisv[0:1, 1:2], in0=poisv[0:1, 0:1],
                                                 scalar1=float(CAPL), scalar2=3.0e38,
                                                 op0=ALU.is_gt, op1=ALU.mult))
        p.op("vector", lambda E: E.tensor_add(poisv[0:1, 2:3], poisv[0:1, 1:2], poisv[0:1, 1:2]))

        p.waitt("tensor", tlb)
        p.waitt("gpsimd", t_tok)
        NGRP = (NG2 + 3) // 4
        for j in range(NGRP):
            tiles = [4 * j + t4 for t4 in range(4) if 4 * j + t4 < NG2]
            W = len(tiles) * 32
            bank = b.psF if j % 2 == 0 else b.psR
            b.bank_wait(bank)
            for t4, g in enumerate(tiles):
                p.op("tensor", lambda E, bk=bank, g2=g, o=t4 * 32:
                     E.matmul(bk[:, o:o + EXP], lt_f[:], I8a[:, g2, :], start=True, stop=False))
                p.op("tensor", lambda E, bk=bank, g2=g, o=t4 * 32:
                     E.matmul(bk[:, o:o + EXP], ones_row[:], baseB[0:1, g2 * EXP:(g2 + 1) * EXP],
                              start=False, stop=True), tick=(t4 == len(tiles) - 1))
            t_mm = b.pet()
            p.waitt("vector", t_mm)
            p.waitr("vector", "pmapg")
            tev = p.op("vector", lambda E, bk=bank, w2=W: E.tensor_copy(rm4[:, 0:w2], bk[:, 0:w2]),
                       tick=True)
            b.bank_set(bank, tev)
            p.op("vector", lambda E, w2=W: E.memset(dchain[:, 0, 0:w2], 0.0))
            for jj in range(1, NCH):
                p.op("vector", lambda E, jj2=jj, w2=W: E.scalar_tensor_tensor(
                    out=dchain[:, jj2 % 2, 0:w2], in0=rm4[:, 0:w2], scalar=float(jj2 * 128),
                    in1=dchain[:, (jj2 - 1) % 2, 0:w2], op0=ALU.is_ge, op1=ALU.add))
            dsl = (NCH - 1) % 2
            p.op("vector", lambda E, w2=W, d2=dsl: E.scalar_tensor_tensor(
                out=pmap4[:, 0:w2], in0=dchain[:, d2, 0:w2], scalar=-128.0, in1=rm4[:, 0:w2],
                op0=ALU.mult, op1=ALU.add))
            p.op("vector", lambda E, w2=W, d2=dsl: E.scalar_tensor_tensor(
                out=dchain[:, 1 - d2, 0:w2], in0=iota32x4[:, 0:w2], scalar=float(NCH),
                in1=dchain[:, d2, 0:w2], op0=ALU.mult, op1=ALU.add))
            p.op("vector", lambda E, w2=W, d2=dsl: E.scalar_tensor_tensor(
                out=pmap4[:, 0:w2], in0=pmap4[:, 0:w2], scalar=float(EXP * NCH),
                in1=dchain[:, 1 - d2, 0:w2], op0=ALU.mult, op1=ALU.add))
            t_pm = p.op("vector", lambda E, w2=W: E.scalar_tensor_tensor(
                out=smap4[:, 0:w2], in0=iota32x4[:, 0:w2], scalar=float(CAPL),
                in1=rm4[:, 0:w2], op0=ALU.mult, op1=ALU.add), tick=True)
            t_lastg = None
            for t4, g in enumerate(tiles):
                a = g % 2
                for k in range(8):
                    p.op("vector", lambda E, g2=g, kk=k, o=t4 * 32: E.scalar_tensor_tensor(
                        out=scrv[:], in0=iota32[:], scalar=idx8f[:, g2, kk:kk + 1],
                        in1=smap4[:, o:o + EXP], op0=ALU.is_equal, op1=ALU.mult,
                        accum_out=slotf[:, kk:kk + 1]))
                    p.op("vector", lambda E, g2=g, kk=k, o=t4 * 32: E.scalar_tensor_tensor(
                        out=scrg[:], in0=iota32[:], scalar=idx8f[:, g2, kk:kk + 1],
                        in1=pmap4[:, o:o + EXP], op0=ALU.is_equal, op1=ALU.mult,
                        accum_out=posf[:, kk:kk + 1]))
                p.op("vector", lambda E, g2=g: E.tensor_copy(slot_all[:, g2, :], slotf[:]))
                p.waitr("vector", f"posu{a}")
                t_lastg = p.op("vector", lambda E, aa=a: E.tensor_copy(posu[:, aa, :], posf[:]), tick=True)
                p.waitt("gpsimd", t_lastg)
                for k in range(8):
                    b.scatter(d_sc[a], dtw_hbm, pv_all[:, g, k, :], posu[:, a, k:k + 1], NSLOT - 1)
                b.setr(f"posu{a}", (d_sc[a], d_sc[a].n))
            b.setr("pmapg", t_lastg)

        b.fence()
        sc.close()
        b.ring.clear()

    if "C" in BUILD:
        b.fence()
        sc = b.scope()
        wg = sc.alloc("wg", [128, 2, 16, INTER], BF16)
        wu = sc.alloc("wu", [128, 2, 16, INTER], BF16)
        wd = sc.alloc("wd", [128, 6, HD], BF16)
        xg = sc.alloc("xg", [128, 4, HD], BF16)
        xfc = sc.alloc("xfc", [128, 2, 16, 512], BF16)
        gsb = sc.alloc("gsb", [128, 2, 512], BF16)
        actc = sc.alloc("actc", [128, 2, 6, 512], BF16)
        ysb = sc.alloc("ysb", [128, 2, HD], BF16)
        dtw = sc.alloc("dtw", [128, NCH * EXP, 2], F32)
        didx = sc.alloc("didx", [128, NCH * EXP], U32)

        tld = p.dma("sync", d_cs2, dtw[:], dtw_hbm.rearrange("(p j) c -> p j c", p=128))
        p.waitt("vector", tld)
        t_didx = p.op("vector", lambda E: E.tensor_copy(didx[:], dtw[:, :, 0]), tick=True)
        p.waitt("gpsimd", t_didx)
        p.waitt("scalar", tld)

        stgc = sc.alloc("stgc", [128, 2, INTER], F32)
        wgv = w_gate.rearrange("e (c p) i -> e p c i", p=128)
        wuv = w_up.rearrange("e (c p) i -> e p c i", p=128)
        wdv = w_down.rearrange("e (c p) i -> e p c i", p=128)

        t_wg = {}
        t_wu = {}
        CJ = {"jobs": [], "i": 0}

        def cast_add(e):
            we2 = e % 2
            for j in range(16):
                CJ["jobs"].append((wgv[e, :, j, :], wg[:, we2, j, :], ("wg", e, j)))
                CJ["jobs"].append((wuv[e, :, j, :], wu[:, we2, j, :], ("wu", e, j)))

        def cast_pump(n):
            while n > 0 and CJ["i"] < len(CJ["jobs"]):
                src, dst, mark = CJ["jobs"][CJ["i"]]
                sl = CJ["i"] % 2
                CJ["i"] += 1
                n -= 1
                eng = "scalar" if sl else "vector"
                p.waitr("sync", f"cstg{sl}")
                tld2 = p.dma("sync", d_wc[sl], stgc[:, sl, 0:src.shape[-1]], src)
                if mark[2] == 0:
                    p.waitr(eng, f"{mark[0]}{mark[1] % 2}")
                p.waitt(eng, tld2)
                if eng == "scalar":
                    tc = p.op(eng, lambda E, d=dst, s2=stgc[:, sl, 0:dst.shape[-1]]:
                              E.activation(out=d, in_=s2, func=AF.Copy), tick=True)
                else:
                    tc = p.op(eng, lambda E, d=dst, s2=stgc[:, sl, 0:dst.shape[-1]]:
                              E.tensor_copy(d, s2), tick=True)
                b.setr(f"cstg{sl}", tc)
                (t_wg if mark[0] == "wg" else t_wu)[mark[1]] = tc

        cast_add(0)
        cast_pump(32)

        for e in range(EXP if DEV_E is None else DEV_E):
            we = e % 2
            if e + 1 < EXP:
                cast_add(e + 1)
            wd_next = 0
            for c3 in range(3):
                csz = 512 if c3 < 2 else 256
                nsub = csz // 128
                ring = c3 % 2
                for sub in range(nsub):
                    j = e * NCH + c3 * 4 + sub
                    gslot = (c3 * 4 + sub) % 4
                    p.waitr("gpsimd", f"xg{gslot}")
                    tgt = b.gather(d_g[gslot], xg[:, gslot, :], x1b_hbm, didx[:, j:j + 1], TLOC - 1)
                    if wd_next < 6:
                        if wd_next == 0:
                            p.waitr("gpsimd", "wd")
                        t_wd = p.dma("gpsimd", d_wd, wd[:, wd_next, :], wdv[e, :, wd_next, :])
                        wd_next += 1
                    p.waitt("tensor", tgt)
                    if sub == 0:
                        p.waitr("tensor", f"xfc{ring}")
                    for grp in range(2):
                        items = [(xg[:, gslot, (grp * 8 + i) * 128:(grp * 8 + i + 1) * 128], 128, i * 128)
                                 for i in range(8)]
                        b.transposes(items, b.psT[grp], ident_bf[:],
                                     xfc[:, ring, grp * 8:grp * 8 + 8, sub * 128:(sub + 1) * 128])
                    b.setr(f"xg{gslot}", b.pet())
                t_xf = b.dvt()
                p.waitt("tensor", t_xf)
                p.waitt("tensor", t_wg[e])
                p.waitt("tensor", t_wu[e])
                for m in range(6):
                    cast_pump(2)
                    bg = b.psA[m % 2]
                    bu = b.psB[m % 2]
                    gslot2 = m % 2
                    b.bank_wait(bg)
                    tg2 = b.mm_chain(bg, csz, [(wg[:, we, k, m * 128:(m + 1) * 128], xfc[:, ring, k, 0:csz])
                                               for k in range(16)])
                    b.bank_wait(bu)
                    tu2 = b.mm_chain(bu, csz, [(wu[:, we, k, m * 128:(m + 1) * 128], xfc[:, ring, k, 0:csz])
                                               for k in range(16)])
                    p.waitt("scalar", tg2)
                    p.waitr("scalar", f"gsb{gslot2}")
                    tsi = p.op("scalar", lambda E, bk=bg, c=csz, gs=gslot2:
                               E.activation(out=gsb[:, gs, 0:c], in_=bk[:, 0:c], func=AF.Silu), tick=True)
                    b.bank_set(bg, tsi)
                    p.waitt("vector", tsi)
                    p.waitt("vector", tu2)
                    p.waitr("vector", f"act{ring}_{m}")
                    tmu = p.op("vector", lambda E, bk=bu, c=csz, mm=m, rr=ring, gs=gslot2:
                               E.tensor_tensor(actc[:, rr, mm, 0:c], gsb[:, gs, 0:c], bk[:, 0:c], ALU.mult),
                               tick=True)
                    b.bank_set(bu, tmu)
                    b.setr(f"gsb{gslot2}", tmu)
                t_act = b.dvt()
                p.waitt("tensor", t_act)
                p.waitt("tensor", t_wd)
                for sub in range(nsub):
                    yring = sub % 2
                    j = e * NCH + c3 * 4 + sub
                    for ci in range(4):
                        bank = b.psF if ci % 2 == 0 else b.psR
                        b.bank_wait(bank)
                        tdn = b.mm_chain(bank, 512,
                                         [(actc[:, ring, m, sub * 128:(sub + 1) * 128],
                                           wd[:, m, ci * 512:(ci + 1) * 512]) for m in range(6)])
                        p.waitt("scalar", tdn)
                        if ci == 0:
                            p.waitr("scalar", f"ysb{yring}")
                        tev = p.op("scalar", lambda E, bk=bank, yr=yring, c2=ci, jj=j:
                                   E.activation(out=ysb[:, yr, c2 * 512:(c2 + 1) * 512], in_=bk[:],
                                                func=AF.Copy, scale=dtw[:, jj, 1:2]), tick=True)
                        b.bank_set(bank, tev)
                    t_ysb = b.act()
                    p.waitt("sync", t_ysb)
                    r0 = e * CAPL + (c3 * 4 + sub) * 128
                    tst = p.dma("sync", d_y[yring], y_hbm[r0:r0 + 128, :], ysb[:, yring, :])
                    b.setr(f"ysb{yring}", tst)
                b.setr(f"xfc{ring}", t_act)
                for mm in range(6):
                    b.setr(f"act{ring}_{mm}", b.pet())
            b.setr(f"wg{we}", b.pet())
            b.setr(f"wu{we}", b.pet())
            b.setr("wd", b.pet())
        b.fence()
        sc.close()
        b.ring.clear()

    if "D" in BUILD:
        b.fence()
        sc = b.scope()
        y8 = sc.alloc("y8", [128, 2, 8, HD], BF16)
        accX = sc.alloc("accX", [128, 2, HD], F32)
        x2f = sc.alloc("x2f", [128, 2, HD], F32)
        outb = sc.alloc("outb", [128, 2, HD], F32)
        sq3 = sc.alloc("sq3", [128, HD], F32)
        ln3 = sc.alloc("ln3", [128, 2, 12], F32)

        banks4 = [b.psA[0], b.psA[1], b.psB[0], b.psB[1]]
        tg_all = {}

        def d_issue(g):
            a = g % 2
            p.waitr("gpsimd", f"y8{a}")
            tg_all[g] = [b.gather(d_ga[k], y8[:, a, k, :], y_hbm, slot_all[:, g, k:k + 1], NSLOT - 1)
                         for k in range(8)]
            p.waitr("sync", f"accX{a}")
            tg_all[(g, "x")] = p.dma("sync", d_al[a], accX[:, a, :], x1f_hbm[g])

        def d_proc(g):
            a = g % 2
            for k in range(8):
                p.waitt("tensor", tg_all[g][k])
            for c in range(4):
                bank = banks4[c]
                b.bank_wait(bank)
                for k in range(8):
                    p.op("tensor", lambda E, bk=bank, aa=a, kk=k, cc=c, st=(k == 0), sp=(k == 7):
                         E.matmul(bk[:, 0:512], ident_bf[:], y8[:, aa, kk, cc * 512:(cc + 1) * 512],
                                  start=st, stop=sp), tick=(k == 7))
                tmm = b.pet()
                p.waitt("vector", tmm)
                if c == 0:
                    p.waitt("vector", tg_all[(g, "x")])
                    p.waitr("vector", f"x2f{a}")
                tev = p.op("vector", lambda E, bk=bank, aa=a, cc=c:
                           E.scalar_tensor_tensor(out=x2f[:, aa, cc * 512:(cc + 1) * 512], in0=bk[:],
                                                  scalar=1.0, in1=accX[:, aa, cc * 512:(cc + 1) * 512],
                                                  op0=ALU.mult, op1=ALU.add,
                                                  accum_out=ln3[:, aa, cc:cc + 1]), tick=True)
                b.bank_set(bank, tev)
            b.setr(f"y8{a}", b.pet())
            b.setr(f"accX{a}", b.dvt())
            p.op("vector", lambda E, aa=a: E.tensor_add(ln3[:, aa, 4:5], ln3[:, aa, 0:1], ln3[:, aa, 1:2]))
            p.op("vector", lambda E, aa=a: E.tensor_add(ln3[:, aa, 5:6], ln3[:, aa, 2:3], ln3[:, aa, 3:4]))
            t_sum = p.op("vector", lambda E, aa=a: E.tensor_add(ln3[:, aa, 6:7], ln3[:, aa, 4:5],
                                                                ln3[:, aa, 5:6]), tick=True)
            p.waitt("scalar", t_sum)
            t_s2 = p.op("scalar", lambda E, aa=a: E.activation(out=sq3[:], in_=x2f[:, aa, :],
                                                               func=AF.Square, accum_out=ln3[:, aa, 7:8]),
                        tick=True)
            p.op("vector", lambda E, aa=a: E.tensor_scalar(out=ln3[:, aa, 8:9], in0=ln3[:, aa, 6:7],
                                                           scalar1=1.0 / HD, scalar2=None, op0=ALU.mult))
            p.waitt("vector", t_s2)
            p.op("vector", lambda E, aa=a: E.tensor_scalar(out=ln3[:, aa, 9:10], in0=ln3[:, aa, 7:8],
                                                           scalar1=1.0 / HD, scalar2=None, op0=ALU.mult))
            p.op("vector", lambda E, aa=a: E.tensor_mul(ln3[:, aa, 10:11], ln3[:, aa, 8:9], ln3[:, aa, 8:9]))
            p.op("vector", lambda E, aa=a: E.scalar_tensor_tensor(out=ln3[:, aa, 10:11], in0=ln3[:, aa, 10:11],
                                                                  scalar=-1.0, in1=ln3[:, aa, 9:10],
                                                                  op0=ALU.mult, op1=ALU.add))
            t_ve = p.op("vector", lambda E, aa=a: E.tensor_scalar(out=ln3[:, aa, 11:12], in0=ln3[:, aa, 10:11],
                                                                  scalar1=1.0, scalar2=1e-5,
                                                                  op0=ALU.mult, op1=ALU.add), tick=True)
            p.waitt("scalar", t_ve)
            t_sq = p.op("scalar", lambda E, aa=a: E.activation(out=ln3[:, aa, 10:11], in_=ln3[:, aa, 11:12],
                                                               func=AF.Sqrt), tick=True)
            p.waitt("vector", t_sq)
            p.op("vector", lambda E, aa=a: E.reciprocal(ln3[:, aa, 11:12], ln3[:, aa, 10:11]))
            p.waitr("vector", f"outb{a}")
            t_o = p.op("vector", lambda E, aa=a: E.tensor_scalar(
                out=outb[:, aa, :], in0=x2f[:, aa, :], scalar1=ln3[:, aa, 8:9], scalar2=ln3[:, aa, 11:12],
                op0=ALU.subtract, op1=ALU.mult), tick=True)
            if g == NTT - 1:
                t_o = p.op("vector", lambda E, aa=a: E.tensor_scalar(
                    out=outb[0:1, aa, :], in0=outb[0:1, aa, :], scalar1=poisv[0:1, 2:3],
                    scalar2=None, op0=ALU.add), tick=True)
            b.setr(f"x2f{a}", b.dvt())
            p.waitt("sync", t_o)
            tst = p.dma("sync", d_ob[a], OUT[g * 128:(g + 1) * 128, :], outb[:, a, :])
            b.setr(f"outb{a}", tst)

        ND = NTT if DEV_D is None else DEV_D * 8
        d_issue(0)
        for g in range(1, ND):
            d_issue(g)
            d_proc(g - 1)
        d_proc(ND - 1)
        p.wait("sync", d_ob[0], d_ob[0].n)
        p.wait("sync", d_ob[1], d_ob[1].n)
        sc.close()

    b.taps_map = taps
    return b


def finalize(b):
    nc = b.nc
    with nc.Block() as block:
        @block.sync
        def _(E):
            for fn in b.st["sync"]:
                fn(E)

        @block.gpsimd
        def _(E):
            for fn in b.st["gpsimd"]:
                fn(E)

        @block.scalar
        def _(E):
            for fn in b.st["scalar"]:
                fn(E)

        @block.vector
        def _(E):
            for fn in b.st["vector"]:
                fn(E)

        @block.tensor
        def _(E):
            for fn in b.st["tensor"]:
                fn(E)
    return nc


_CACHE = {}


def _get_program():
    key = (tuple(BUILD), tuple(TAPS))
    if key not in _CACHE:
        bb = build_program()
        _CACHE[key] = (finalize(bb), bb)
    return _CACHE[key]


def make_in_maps(x, wts):
    hc = host_consts()
    base = dict(wts)
    for k, v in hc.items():
        base["c_" + k] = v
    in_maps = []
    for c in range(NCORE):
        m = dict(base)
        m["xin"] = np.ascontiguousarray(x[c * SEQ:(c + 1) * SEQ])
        in_maps.append(m)
    return in_maps


def kernel(L0_attn_hidden, L0_attn_w_kv_a, L0_attn_q_a_gamma, L0_attn_w_qb,
           L0_attn_kv_a_gamma, L0_attn_w_kvb, L0_attn_w_o,
           L0_ffn_gate_w, L0_ffn_w_gate, L0_ffn_w_up, L0_ffn_w_down):
    nc, bb = _get_program()
    wts = {
        "w_kv_a": np.ascontiguousarray(L0_attn_w_kv_a, np.float32),
        "w_qb": np.ascontiguousarray(L0_attn_w_qb, np.float32),
        "w_kvb": np.ascontiguousarray(L0_attn_w_kvb, np.float32),
        "w_o": np.ascontiguousarray(L0_attn_w_o, np.float32),
        "gate_w": np.ascontiguousarray(L0_ffn_gate_w, np.float32),
        "w_gate": np.ascontiguousarray(L0_ffn_w_gate, np.float32),
        "w_up": np.ascontiguousarray(L0_ffn_w_up, np.float32),
        "w_down": np.ascontiguousarray(L0_ffn_w_down, np.float32),
        "gamma_qb": np.ascontiguousarray(np.broadcast_to(
            np.asarray(L0_attn_q_a_gamma, np.float32), (128, QL))),
        "gamma_kvb": np.ascontiguousarray(np.broadcast_to(
            np.asarray(L0_attn_kv_a_gamma, np.float32), (128, KL))),
    }
    x = np.ascontiguousarray(L0_attn_hidden, np.float32)
    in_maps = make_in_maps(x, wts)
    res = run_bass_kernel_spmd(nc, in_maps, core_ids=list(range(NCORE)))
    out = np.concatenate([r["out"].reshape(SEQ, S, HD) for r in res.results], axis=0)
    return out
